# version 05
# baseline (speedup 1.0000x reference)
import contextlib
import numpy as np

import concourse.bass as bass
import concourse.mybir as mybir
from concourse.bass_utils import run_bass_kernel_spmd
from concourse.bass_core_rust import IndirectOffsetOnAxis

F32 = mybir.dt.float32
BF16 = mybir.dt.bfloat16
I32 = mybir.dt.int32
AF = mybir.ActivationFunctionType
ALU = mybir.AluOpType
AX = mybir.AxisListType

D = 1024
DC = D // 128
T = 2048
NCORES = 8


_DMA_SEM = {"y": "yd", "g": "gd", "s": "sd"}


class Prog:
    def __init__(self):
        nc = bass.Bass("TRN2", target_bir_lowering=False)
        self.nc = nc
        self.eng = {"t": nc.tensor, "v": nc.vector, "s": nc.scalar,
                    "g": nc.gpsimd, "y": nc.sync}
        self.sem = {k: nc.alloc_semaphore("S" + k)
                    for k in ["t", "v", "s", "g", "yq", "sq", "gq",
                              "yd", "gd", "sd"]}
        self.cnt = {k: 0 for k in self.sem}
        self.seen = {q: {k: 0 for k in self.sem} for q in self.eng}
        self.dma_issuer = {"yd": "y", "gd": "g", "sd": "s"}
        self.last_cut = {}
        self.writer = {}
        self.readers = {}
        self.floor = {k: 0 for k in self.sem}
        self._uid = 0
        self.consts = {}

    def uname(self, base):
        self._uid += 1
        return f"{base}_{self._uid}"

    def sb(self, base, shape, dtype):
        return self.nc.alloc_sbuf_tensor(self.uname(base), shape, dtype,
                                         side="right")

    @staticmethod
    def _key(b):
        t = getattr(b, "tensor", b)
        return t.name

    def _dep(self, deps, ident):
        if ident is None:
            return
        s, v = ident
        if v > deps.get(s, 0):
            deps[s] = v

    def _gather(self, sem_name, reads, writes, accum):
        deps = {}
        for b in reads:
            self._dep(deps, self.writer.get(self._key(b)))
        for b in writes:
            k = self._key(b)
            w = self.writer.get(k)
            if w and not (accum and w[0] == sem_name):
                self._dep(deps, w)
            for s, v in self.readers.get(k, {}).items():
                self._dep(deps, (s, v))
        for s, v in self.floor.items():
            self._dep(deps, (s, v))
        return deps

    def _make_cut(self, s):
        q = self.dma_issuer[s]
        if self.seen[q].get(s, 0) < self.cnt[s]:
            self.eng[q].wait_ge(self.sem[s], self.cnt[s])
            self.seen[q][s] = self.cnt[s]
        msem = {"y": "yq", "s": "sq", "g": "gq"}[q]
        self.eng[q].sem_inc(self.sem[msem], 1)
        self.cnt[msem] += 1
        self.last_cut[s] = (self.cnt[s], (msem, self.cnt[msem]))

    def _wait(self, q, deps):
        resolved = {}
        for s, v in deps.items():
            if self.is_dma_sem(s):
                cut_cnt, _ = self.last_cut.get(s, (0, None))
                if v > cut_cnt:
                    self._make_cut(s)
                ms, mv = self.last_cut[s][1]
                resolved[ms] = max(resolved.get(ms, 0), mv)
            else:
                resolved[s] = max(resolved.get(s, 0), v)
        seen = self.seen[q]
        for s, v in resolved.items():
            if seen.get(s, 0) < v:
                self.eng[q].wait_ge(self.sem[s], v)
                seen[s] = v

    def is_dma_sem(self, s):
        return s in ("yd", "gd", "sd") or s.startswith("dm_")

    def dma_stream_sem(self, name, q):
        key = "dm_" + name
        if key not in self.sem:
            self.sem[key] = self.nc.alloc_semaphore(key)
            self.cnt[key] = 0
            self.dma_issuer[key] = q
        assert self.dma_issuer[key] == q, "stream must keep one issuing queue"
        return key

    def _record(self, reads, writes, ident):
        for b in writes:
            k = self._key(b)
            self.writer[k] = ident
            self.readers[k] = {}
        for b in reads:
            k = self._key(b)
            if self.writer.get(k) != ident:
                self.readers.setdefault(k, {})[ident[0]] = ident[1]

    def op(self, q, emit, reads=(), writes=(), dma=False, stream=None):
        if dma and stream is not None:
            sem_name = self.dma_stream_sem(stream, q)
        else:
            sem_name = _DMA_SEM[q] if dma else q
        deps = self._gather(sem_name, reads, writes, accum=False)
        self._wait(q, deps)
        ins = emit(self.eng[q])
        amt = 16 if dma else 1
        ins.then_inc(self.sem[sem_name], amt)
        self.cnt[sem_name] += amt
        ident = (sem_name, self.cnt[sem_name])
        self._record(reads, writes, ident)
        return ident

    def dma(self, q, out, in_, stream=None, **kw):
        return self.op(q, lambda e: e.dma_start(out=out, in_=in_, **kw),
                       reads=[in_], writes=[out], dma=True, stream=stream)

    def mm_group(self, out_psum, pairs, extra_reads=(), start=True, stop=True):
        reads = [ap for pr in pairs for ap in pr] + list(extra_reads)
        deps = self._gather("t", reads, [out_psum], accum=not start)
        self._wait("t", deps)
        n = len(pairs)
        ins = None
        for i, (lhsT, rhs) in enumerate(pairs):
            ins = self.nc.tensor.matmul(out_psum, lhsT, rhs,
                                        start=(start and i == 0),
                                        stop=(stop and i == n - 1))
        ins.then_inc(self.sem["t"], 1)
        self.cnt["t"] += 1
        ident = ("t", self.cnt["t"])
        self._record(reads, [out_psum], ident)
        return ident

    def transpose(self, out_psum, in_, ident_tile=None):
        if ident_tile is None:
            p = in_.shape[0]
            ident_tile = self.ident128(in_.tensor.dtype)[0:p, 0:p]
        return self.op("t", lambda e: e.transpose(out_psum, in_, ident_tile),
                       reads=[in_, ident_tile], writes=[out_psum])

    def cp(self, q, out, in_):
        return self.op(q, lambda e: e.tensor_copy(out, in_),
                       reads=[in_], writes=[out])

    def tt(self, q, out, a, b, alu):
        return self.op(q, lambda e: e.tensor_tensor(out, a, b, alu),
                       reads=[a, b], writes=[out])

    def ts(self, q, out, a, s1, s2=None, op0=ALU.mult, op1=ALU.bypass):
        reads = [a] + [x for x in (s1, s2) if hasattr(x, "tensor")]
        if s2 is None:
            return self.op(q, lambda e: e.tensor_scalar(out, a, s1, None, op0),
                           reads=reads, writes=[out])
        return self.op(q, lambda e: e.tensor_scalar(out, a, s1, s2, op0, op1),
                       reads=reads, writes=[out])

    def stt(self, q, out, in0, scalar, in1, op0, op1):
        reads = [in0, in1] + ([scalar] if hasattr(scalar, "tensor") else [])
        return self.op(q, lambda e: e.scalar_tensor_tensor(
            out, in0, scalar, in1, op0, op1), reads=reads, writes=[out])

    def act(self, out, in_, func, bias=0.0, scale=1.0, q="s"):
        if isinstance(bias, float) and bias != 0.0:
            nparts = out.shape[0]
            tile = self.const_sbuf(f"f32c_{bias!r}", np.full((128, 1), bias), F32)
            bias = tile[0:nparts, 0:1]
        reads = [in_] + [x for x in (bias, scale) if hasattr(x, "tensor")]
        return self.op(q, lambda e: e.activation(
            out=out, in_=in_, func=func, bias=bias, scale=scale),
            reads=reads, writes=[out])

    def red(self, q, out, in_, alu, negate=None):
        return self.op(q, lambda e: e.tensor_reduce(
            out, in_, axis=AX.X, op=alu, negate=negate),
            reads=[in_], writes=[out])

    def memset(self, q, ap, val):
        return self.op(q, lambda e: e.memset(ap, val), reads=[], writes=[ap])

    def scope(self):
        return Scope(self)

    def const_dram(self, key, arr):
        ck = "d_" + key
        if ck not in self.consts:
            self.consts[ck] = self.nc.inline_tensor(
                np.ascontiguousarray(arr.astype(np.float32)),
                self.uname("cd_" + key))
        return self.consts[ck]

    def const_scoped(self, sc, key, arr, dtype):
        d = self.const_dram(key, arr)
        h = sc.sb("cl_" + key, list(arr.shape), dtype)
        self.dma("g", h[:, :], d.ap())
        return h

    def const_sbuf(self, key, arr, dtype):
        if key in self.consts:
            return self.consts[key]
        d = self.nc.inline_tensor(np.ascontiguousarray(arr.astype(np.float32)),
                                  self.uname("c_" + key))
        h = self.sb("cs_" + key, list(arr.shape), dtype)
        self.dma("g", h[:, :], d.ap())
        self.consts[key] = h
        return h

    def ones_col(self):
        return self.const_sbuf("ones_col", np.ones((128, 1)), BF16)

    def ones_row(self):
        return self.const_sbuf("ones_row", np.ones((1, 128)), BF16)

    def ones_row_f32(self):
        return self.const_sbuf("ones_row_f32", np.ones((1, 128)), F32)

    def ident128(self, dtype=BF16):
        key = "ident128_" + str(dtype)
        return self.const_sbuf(key, np.eye(128), dtype)


class Scope:

    def __init__(self, P):
        self.P = P
        self.stack = contextlib.ExitStack()
        self.bufs = []

    def sb(self, base, shape, dtype):
        h = self.stack.enter_context(
            self.P.nc.sbuf_tensor(self.P.uname(base), shape, dtype))
        self.bufs.append(h)
        return h

    def ps(self, base, shape, dtype=F32):
        h = self.stack.enter_context(
            self.P.nc.psum_tensor(self.P.uname(base), shape, dtype))
        self.bufs.append(h)
        return h

    def close(self):
        P = self.P
        for h in self.bufs:
            k = h.name
            w = P.writer.pop(k, None)
            if w:
                s, v = w
                P.floor[s] = max(P.floor.get(s, 0), v)
            for s, v in P.readers.pop(k, {}).items():
                P.floor[s] = max(P.floor.get(s, 0), v)
        self.stack.close()

    def __enter__(self):
        return self

    def __exit__(self, exc_type, *a):
        if exc_type is not None:
            return False
        self.close()
        return False



def cast_weight_bf16(P, sc, dram_f32, name):
    if not hasattr(P, "w16cache"):
        P.w16cache = {}
    if name in P.w16cache:
        return P.w16cache[name]
    src = dram_f32 if hasattr(dram_f32, "tensor") else dram_f32.ap()
    R, C = src.shape[-2], src.shape[-1]
    w16 = P.nc.dram_tensor(P.uname(name + "_16"), [R, C], BF16, kind="Internal")
    nbuf = 4
    with P.scope() as sc2:
        tmp = [sc2.sb(name + "_cast", [128, C], BF16) for _ in range(nbuf)]
        for r0 in range(0, R, 128):
            rows = min(128, R - r0)
            i = (r0 // 128) % nbuf
            b = tmp[i]
            P.dma("g", b[0:rows, :], src[r0:r0 + rows, :],
                  stream=f"cast{i}")
            P.dma("y", w16.ap()[r0:r0 + rows, :], b[0:rows, :],
                  stream=f"cast{i}s")
    P.w16cache[name] = w16
    return w16


def load_wT_slabs(P, sc, w16, name, i0=0, ni=None, o0=0, no=None):
    I, O = w16.shape
    ni = ni if ni is not None else I
    no = no if no is not None else O
    slabs = []
    for c in range(ni // 128):
        h = sc.sb(f"{name}_T{c}", [128, no], BF16)
        P.dma("y" if c % 2 == 0 else "s", h[:, :],
              w16.ap()[i0 + c * 128:i0 + (c + 1) * 128, o0:o0 + no])
        slabs.append(h)
    return slabs


def load_w_slabs(P, sc, dram_f32, name, i0=0, ni=None, o0=0, no=None):
    src = dram_f32 if hasattr(dram_f32, "tensor") else dram_f32.ap()
    I, O = src.shape[-2], src.shape[-1]
    ni = ni if ni is not None else I
    no = no if no is not None else O
    slabs = []
    for c in range(ni // 128):
        h = sc.sb(f"{name}_{c}", [128, no], BF16)
        P.dma("g", h[:, :], src[i0 + c * 128:i0 + c * 128 + 128, o0:o0 + no])
        slabs.append(h)
    return slabs


def load_vec_pp(P, sc, dram_vec, name, n=None, dtype=F32):
    src = dram_vec if hasattr(dram_vec, "tensor") else dram_vec.ap()
    N = n if n is not None else src.tensor.shape[0]
    nch = N // 128
    rowmaj = sc.sb(name + "_rm", [nch, 128], dtype)
    P.dma("g", rowmaj[:, :], src.rearrange("(c p) -> c p", p=128))
    out = sc.sb(name + "_pp", [128, nch], dtype)
    with P.scope() as s2:
        pt = s2.ps(name + "_ps", [128, nch], dtype)
        P.transpose(pt[:, :], rowmaj[:, :])
        P.cp("v", out[:, :], pt[:, :])
    return out


def emit_ln_generic(P, src, t0, nT, out_cb, bf16_of=None, eps=1e-5):
    ones = P.ones_col()
    onesr = P.ones_row_f32()
    with P.scope() as s2:
        sq = [s2.sb("ln_sq", [128, 512], BF16) for _ in range(2)]
        xb = None
        if bf16_of is None:
            xb = [s2.sb("ln_xb", [128, 512], BF16) for _ in range(2)]
        else:
            for c in range(DC):
                P.cp("v", bf16_of[c][:, 0:nT], src[c][:, t0:t0 + nT])
        ps_stat_r = [s2.ps("ln_ps", [1, 512], F32) for _ in range(2)]
        ps_stat2_r = [s2.ps("ln_ps2", [1, 512], F32) for _ in range(2)]
        ps_r_r = [s2.ps("ln_bcr", [128, 512], F32) for _ in range(2)]
        ps_m_r = [s2.ps("ln_bcm", [128, 512], F32) for _ in range(2)]
        mean_r = [s2.sb("ln_mean", [1, 512], F32) for _ in range(2)]
        rstd_r = [s2.sb("ln_rstd", [1, 512], F32) for _ in range(2)]
        mr_r = [s2.sb("ln_mr", [1, 512], F32) for _ in range(2)]
        for g in range(nT // 512):
            ps_stat, ps_stat2 = ps_stat_r[g % 2], ps_stat2_r[g % 2]
            ps_r, ps_m = ps_r_r[g % 2], ps_m_r[g % 2]
            mean, rstd, mr = mean_r[g % 2], rstd_r[g % 2], mr_r[g % 2]
            sl = slice(g * 512, g * 512 + 512)
            slx = slice(t0 + g * 512, t0 + g * 512 + 512)
            if bf16_of is not None:
                P.mm_group(ps_stat[:, :],
                           [(ones[:, :], bf16_of[c][:, sl]) for c in range(DC)])
            else:
                for c in range(DC):
                    P.cp("v", xb[c % 2][:, :], src[c][:, slx])
                    P.mm_group(ps_stat[:, :], [(ones[:, :], xb[c % 2][:, :])],
                               start=(c == 0), stop=(c == DC - 1))
            for c in range(DC):
                P.act(sq[c % 2][:, :], src[c][:, slx], AF.Square)
                P.mm_group(ps_stat2[:, :], [(ones[:, :], sq[c % 2][:, :])],
                           start=(c == 0), stop=(c == DC - 1))
            P.ts("v", mean[:, :], ps_stat[:, :], 1.0 / D)
            P.tt("v", rstd[:, :], mean[:, :], mean[:, :], ALU.mult)
            P.stt("v", rstd[:, :], ps_stat2[:, :], 1.0 / D, rstd[:, :],
                  ALU.mult, ALU.subtract)
            P.act(rstd[:, :], rstd[:, :], AF.Ln, bias=eps)
            P.act(rstd[:, :], rstd[:, :], AF.Exp, scale=-0.5)
            P.tt("v", mr[:, :], mean[:, :], rstd[:, :], ALU.mult)
            P.mm_group(ps_r[:, :], [(onesr[:, :], rstd[:, :])])
            P.mm_group(ps_m[:, :], [(onesr[:, :], mr[:, :])])
            for c in range(DC):
                out_cb(c, g, ps_r, ps_m)


def emit_ln(P, sc, X, XN, t0=0, nT=T, eps=1e-5):
    def out_cb(c, g, ps_r, ps_m):
        sl = slice(g * 512, g * 512 + 512)
        slx = slice(t0 + g * 512, t0 + g * 512 + 512)
        P.tt("v", XN[c][:, sl], X[c][:, slx], ps_r[:, :], ALU.mult)
        P.tt("v", XN[c][:, sl], XN[c][:, sl], ps_m[:, :], ALU.subtract)
    emit_ln_generic(P, X, t0, nT, out_cb, bf16_of=XN, eps=eps)


def residual_add(P, m, tg, X, psum_ap, coef, r_gate=None, sc=None):
    sl = slice(tg * 512, tg * 512 + 512)
    if r_gate is None:
        P.stt("v", X[m][:, sl], psum_ap, coef, X[m][:, sl], ALU.mult, ALU.add)
    else:
        t = sc.sb("res_t", [128, 512], F32)
        P.tt("v", t[:, :], psum_ap, r_gate, ALU.mult)
        P.stt("v", X[m][:, sl], t[:, :], coef, X[m][:, sl], ALU.mult, ALU.add)



def _l1_token_shift(P, sub, XN, mu, name, t0, nT):
    xm = [sub.sb(name, [128, nT], BF16) for _ in range(DC)]
    d = sub.sb(name + "_d", [128, nT], BF16)
    for c in range(DC):
        if t0 == 0:
            P.tt("v", d[:, 1:nT], XN[c][:, 1:nT], XN[c][:, 0:nT - 1],
                 ALU.subtract)
            P.stt("v", xm[c][:, 1:nT], d[:, 1:nT], mu[:, c:c + 1],
                  XN[c][:, 0:nT - 1], ALU.mult, ALU.add)
            P.ts("v", xm[c][:, 0:1], XN[c][:, 0:1], mu[:, c:c + 1])
        else:
            P.tt("v", d[:, :], XN[c][:, t0:t0 + nT], XN[c][:, t0 - 1:t0 + nT - 1],
                 ALU.subtract)
            P.stt("v", xm[c][:, :], d[:, :], mu[:, c:c + 1],
                  XN[c][:, t0 - 1:t0 + nT - 1], ALU.mult, ALU.add)
    return xm


def emit_l1_ffn(P, X, W):
    TH = 1024
    with P.scope() as sc:
        XN = [sc.sb("XN", [128, T], BF16) for _ in range(DC)]
        emit_ln(P, sc, X, XN)
        mu_r = load_vec_pp(P, sc, W["L1_ffn_mu_r"], "mu_r")
        mu_k = load_vec_pp(P, sc, W["L1_ffn_mu_k"], "mu_k")
        with P.scope() as sW:
            wr16 = cast_weight_bf16(P, sW, W["L1_ffn_W_r"], "l1wr")
            wk16 = cast_weight_bf16(P, sW, W["L1_ffn_W_k"], "l1wk")
            wv16 = cast_weight_bf16(P, sW, W["L1_ffn_W_v"], "l1wv")
        for th in range(T // TH):
            t0 = th * TH
            with P.scope() as sth:
                r = [sth.sb("r", [128, TH], BF16) for _ in range(DC)]
                ka = [sth.sb("ka", [128, TH], BF16) for _ in range(DC)]
                ps = [sth.ps(f"l1ps{i}", [128, 512], F32) for i in range(4)]
                with P.scope() as s1:
                    xr = _l1_token_shift(P, s1, XN, mu_r, "xr", t0, TH)
                    wrT = load_wT_slabs(P, s1, wr16, "wrT")
                    for m in range(DC):
                        for tg in range(TH // 512):
                            pb = ps[(m * (TH // 512) + tg) % 4]
                            sl = slice(tg * 512, tg * 512 + 512)
                            P.mm_group(pb[:, :],
                                       [(wrT[i][:, m * 128:(m + 1) * 128],
                                         xr[i][:, sl]) for i in range(DC)])
                            P.act(r[m][:, sl], pb[:, :], AF.Sigmoid)
                with P.scope() as s2:
                    xk = _l1_token_shift(P, s2, XN, mu_k, "xk", t0, TH)
                    wkT = load_wT_slabs(P, s2, wk16, "wkT")
                    for m in range(DC):
                        for tg in range(TH // 512):
                            pb = ps[(m * (TH // 512) + tg) % 4]
                            sl = slice(tg * 512, tg * 512 + 512)
                            P.mm_group(pb[:, :],
                                       [(wkT[i][:, m * 128:(m + 1) * 128],
                                         xk[i][:, sl]) for i in range(DC)])
                            P.op("v", lambda e, pb=pb, m=m, sl=sl:
                                 e.tensor_scalar_max(ka[m][:, sl], pb[:, :], 0.0),
                                 reads=[pb[:, :]], writes=[ka[m][:, sl]])
                            P.act(ka[m][:, sl], ka[m][:, sl], AF.Square)
                with P.scope() as s3:
                    wvT = load_wT_slabs(P, s3, wv16, "wvT")
                    for m in range(DC):
                        for tg in range(TH // 512):
                            pb = ps[(m * (TH // 512) + tg) % 4]
                            sl = slice(tg * 512, tg * 512 + 512)
                            P.mm_group(pb[:, :],
                                       [(wvT[i][:, m * 128:(m + 1) * 128],
                                         ka[i][:, sl]) for i in range(DC)])
                            residual_add(P, m, t0 // 512 + tg, X, pb[:, :], 0.5,
                                         r_gate=r[m][:, sl], sc=s3)



def emit_l3_ffn(P, X, W):
    TQ = 512
    HC = 32
    with P.scope() as sc:
        w1_16 = cast_weight_bf16(P, sc, W["L3_ffn_w1_w"], "l3w1")
        w2_16 = cast_weight_bf16(P, sc, W["L3_ffn_w2_w"], "l3w2")
        h1_16 = cast_weight_bf16(P, sc, W["L3_ffn_h1_w"], "l3h1")
        h2_16 = cast_weight_bf16(P, sc, W["L3_ffn_h2_w"], "l3h2")
        w3_16 = cast_weight_bf16(P, sc, W["L3_ffn_w3_w"], "l3w3")
        s1 = load_vec_pp(P, sc, W["L3_ffn_w1_s"], "s1")
        s2 = load_vec_pp(P, sc, W["L3_ffn_w2_s"], "s2")
        sh1 = load_vec_pp(P, sc, W["L3_ffn_h1_s"], "sh1")
        sh2 = load_vec_pp(P, sc, W["L3_ffn_h2_s"], "sh2")
        s3 = load_vec_pp(P, sc, W["L3_ffn_w3_s"], "s3")
        s3h = sc.sb("s3h", [128, 8], F32)
        P.ts("v", s3h[:, :], s3[:, :], 0.5)

        def dense4k(sq, dst, src, w16h, spp, func):
            with P.scope() as sw:
                psd = [sw.ps("l3d", [128, TQ], F32) for _ in range(4)]
                wT = [[sw.sb("hT", [128, 512], BF16) for _ in range(HC)]
                      for _ in range(2)]

                def load(ob):
                    ws = wT[ob % 2]
                    for i in range(HC):
                        q = "y" if i % 2 == 0 else "s"
                        P.dma(q, ws[i][:, :],
                              w16h.ap()[i * 128:(i + 1) * 128,
                                        ob * 512:(ob + 1) * 512],
                              stream=f"l3d{ob % 2}{q}")
                load(0)
                for ob in range(8):
                    if ob + 1 < 8:
                        load(ob + 1)
                    ws = wT[ob % 2]
                    for mm in range(4):
                        m = ob * 4 + mm
                        pa = psd[mm % 4]
                        P.mm_group(pa[:, :],
                                   [(ws[i][:, mm * 128:(mm + 1) * 128],
                                     src[i][:, :]) for i in range(HC)])
                        P.act(dst[m][:, :], pa[:, :], func, scale=spp[:, m:m + 1])

        for q in range(T // TQ):
            t0 = q * TQ
            with P.scope() as sq:
                h = [sq.sb("h", [128, TQ], BF16) for _ in range(HC)]
                h1 = [sq.sb("h1", [128, TQ], BF16) for _ in range(HC)]
                with P.scope() as sx:
                    xnq = [sx.sb("xnq", [128, TQ], BF16) for _ in range(DC)]
                    emit_ln(P, sx, X, xnq, t0, TQ)
                    aa = [sx.sb("l3aa", [128, TQ], BF16) for _ in range(2)]
                    psp = [sx.ps("l3p", [128, TQ], F32) for _ in range(4)]
                    with P.scope() as sw:
                        wTb = [[sw.sb("w12T", [128, 1024], BF16)
                                for _ in range(2 * DC)] for _ in range(2)]

                        def load12(ob):
                            ws = wTb[ob % 2]
                            for i in range(DC):
                                P.dma("y", ws[i][:, :],
                                      w1_16.ap()[i * 128:(i + 1) * 128,
                                                 ob * 1024:(ob + 1) * 1024],
                                      stream=f"l3a{ob % 2}y")
                                P.dma("s", ws[DC + i][:, :],
                                      w2_16.ap()[i * 128:(i + 1) * 128,
                                                 ob * 1024:(ob + 1) * 1024],
                                      stream=f"l3a{ob % 2}s")
                        load12(0)
                        for ob in range(4):
                            if ob + 1 < 4:
                                load12(ob + 1)
                            ws = wTb[ob % 2]
                            for mm in range(8):
                                m = ob * 8 + mm
                                pa = psp[2 * (mm % 2)]
                                pb = psp[2 * (mm % 2) + 1]
                                a = aa[mm % 2]
                                P.mm_group(pa[:, :],
                                           [(ws[i][:, mm * 128:(mm + 1) * 128],
                                             xnq[i][:, :]) for i in range(DC)])
                                P.act(a[:, :], pa[:, :], AF.Silu,
                                      scale=s1[:, m:m + 1])
                                P.mm_group(pb[:, :],
                                           [(ws[DC + i][:, mm * 128:(mm + 1) * 128],
                                             xnq[i][:, :]) for i in range(DC)])
                                P.stt("v", h[m][:, :], pb[:, :], s2[:, m:m + 1],
                                      a[:, :], ALU.mult, ALU.mult)
                dense4k(sq, h1, h, h1_16, sh1, AF.Gelu)
                dense4k(sq, h, h1, h2_16, sh2, AF.Gelu)
                with P.scope() as so:
                    ps8 = [so.ps("l3o", [128, TQ], F32) for _ in range(8)]
                    w3T = [[so.sb("w3T", [128, 1024], BF16) for _ in range(8)]
                           for _ in range(2)]

                    def load3(kg):
                        ws = w3T[kg % 2]
                        for i in range(8):
                            ic = kg * 8 + i
                            q = "y" if i % 2 == 0 else "s"
                            P.dma(q, ws[i][:, :],
                                  w3_16.ap()[ic * 128:(ic + 1) * 128, 0:1024],
                                  stream=f"l3c{kg % 2}{q}")
                    load3(0)
                    for kg in range(4):
                        if kg + 1 < 4:
                            load3(kg + 1)
                        ws = w3T[kg % 2]
                        for m in range(8):
                            P.mm_group(ps8[m][:, :],
                                       [(ws[i][:, m * 128:(m + 1) * 128],
                                         h[kg * 8 + i][:, :])
                                        for i in range(8)],
                                       start=(kg == 0), stop=(kg == 3))
                    for m in range(DC):
                        P.stt("v", X[m][:, t0:t0 + TQ], ps8[m][:, :],
                              s3h[:, m:m + 1], X[m][:, t0:t0 + TQ],
                              ALU.mult, ALU.add)



def load_vec_small(P, sc, dram_vec, name, n, dtype=BF16):
    src = dram_vec if hasattr(dram_vec, "tensor") else dram_vec.ap()
    h = sc.sb(name + "_row", [1, n], dtype)
    P.dma("g", h[:, :], src[None, :] if len(src.shape) == 1 else src)
    return h


def emit_l2_ffn(P, X, W):
    TH = 1024
    E = 8
    with P.scope() as sc:
        XN = [sc.sb("XN", [128, T], BF16) for _ in range(DC)]
        emit_ln(P, sc, X, XN)
        b_enc = load_vec_pp(P, sc, W["L2_ffn_b_enc"], "benc")
        b_c2s = load_vec_pp(P, sc, W["L2_ffn_b_c2s"], "bc2s")
        b_dec = load_vec_pp(P, sc, W["L2_ffn_b_dec"], "bdec")
        b1 = load_vec_pp(P, sc, W["L2_ffn_b1"].rearrange("e f -> (e f)"),
                         "b1", n=8192)
        b2rm = sc.sb("b2rm", [8, 64], F32)
        P.dma("g", b2rm[:, :], W["L2_ffn_b2"])
        b2 = sc.sb("b2t", [64, 8], F32)
        with P.scope() as st0:
            pb2 = st0.ps("b2ps", [64, 8], F32)
            P.transpose(pb2[:, :], b2rm[:, :])
            P.cp("v", b2[:, :], pb2[:, :])
        ln_g = load_vec_pp(P, sc, W["L2_ffn_ln_g"], "lng")
        ln_b = load_vec_pp(P, sc, W["L2_ffn_ln_b"], "lnb")
        gh = sc.sb("l2gh", [128, 8], F32)
        bh = sc.sb("l2bh", [128, 8], F32)
        P.ts("v", gh[:, :], ln_g[:, :], 0.5)
        P.ts("v", bh[:, :], ln_b[:, :], 0.5)
        b_s2c = load_vec_small(P, sc, W["L2_ffn_b_s2c"], "bs2c", 64, F32)
        bs2cp = sc.sb("bs2cp", [64, 1], F32)
        with P.scope() as st:
            pt = st.ps("bs2cT", [64, 1], F32)
            P.transpose(pt[:, :], b_s2c[:, :])
            P.cp("v", bs2cp[:, :], pt[:, :])
        wr_t = sc.sb("l2wr", [64, 8], BF16)
        P.dma("g", wr_t[:, :], W["L2_ffn_Wr"])
        brr = sc.sb("l2br", [1, 8], BF16)
        P.dma("g", brr[:, :], W["L2_ffn_br"][None, :])
        esel = P.const_sbuf("esel", np.kron(np.eye(8), np.ones((1, 64))), BF16)
        onesr = P.ones_row()
        for th in range(T // TH):
            t0 = th * TH
            with P.scope() as sh:
                cont = [sh.sb("cont", [64, TH], BF16)]
                gate_fm = sh.sb("gatef", [8, TH], BF16)
                eo = [sh.sb("eo", [64, TH], F32)]
                with P.scope() as se:
                    spikes = [se.sb("spk", [128, TH], BF16) for _ in range(16)]
                    pse = [se.ps("l2pe", [128, 512], F32) for _ in range(4)]
                    with P.scope() as sw:
                        wenc = load_w_slabs(P, sw, W["L2_ffn_W_enc"], "wenc")
                        for oc in range(16):
                            for tg in range(TH // 512):
                                pb = pse[(oc * 2 + tg) % 4]
                                sl = slice(tg * 512, tg * 512 + 512)
                                P.mm_group(pb[:, :],
                                           [(wenc[i][:, oc * 128:(oc + 1) * 128],
                                             XN[i][:, t0 + tg * 512:t0 + tg * 512 + 512])
                                            for i in range(DC)])
                                P.act(spikes[oc][:, sl], pb[:, :], AF.Sigmoid,
                                      bias=b_enc[:, oc:oc + 1])
                    with P.scope() as sw:
                        ws2c = load_w_slabs(P, sw, W["L2_ffn_W_s2c"], "ws2c")
                        psc = sw.ps("l2pc", [64, 512], F32)
                        for tg in range(TH // 512):
                            sl = slice(tg * 512, tg * 512 + 512)
                            P.mm_group(psc[:, :],
                                       [(ws2c[i][:, :], spikes[i][:, sl])
                                        for i in range(16)])
                            P.op("s", lambda e, psc=psc, sl=sl: e.activation(
                                out=cont[0][:, sl], in_=psc[:, :],
                                func=AF.Identity, bias=bs2cp[0:64, 0:1]),
                                reads=[psc[:, :], bs2cp[0:64, 0:1]],
                                writes=[cont[0][:, sl]])
                with P.scope() as sr:
                    psr = sr.ps("l2pr", [128, 8], F32)
                    pst = sr.ps("l2pt", [8, 128], BF16)
                    probs = sr.sb("probs", [128, 8], F32)
                    tmp = sr.sb("rtmp", [128, 8], F32)
                    red = sr.sb("rred", [128, 1], F32)
                    m1 = sr.sb("rm1", [128, 1], F32)
                    m2 = sr.sb("rm2", [128, 1], F32)
                    mask1 = sr.sb("rmask1", [128, 8], F32)
                    gtm = sr.sb("gtm", [128, 8], BF16)
                    for tt in range(TH // 128):
                        P.mm_group(psr[:, :],
                                   [(cont[0][:, tt * 128:(tt + 1) * 128], wr_t[:, :]),
                                    (onesr[:, :], brr[:, :])])
                        P.red("v", red[:, :], psr[:, :], ALU.max)
                        P.ts("v", probs[:, :], psr[:, :], red[:, 0:1], None,
                             ALU.subtract)
                        P.act(probs[:, :], probs[:, :], AF.Exp)
                        P.red("v", red[:, :], probs[:, :], ALU.add)
                        P.op("v", lambda e: e.reciprocal(red[:, :], red[:, :]),
                             reads=[red[:, :]], writes=[red[:, :]])
                        P.ts("v", probs[:, :], probs[:, :], red[:, 0:1], None,
                             ALU.mult)
                        P.red("v", m1[:, :], probs[:, :], ALU.max)
                        P.ts("v", mask1[:, :], probs[:, :], m1[:, 0:1], None,
                             ALU.is_equal)
                        P.ts("v", tmp[:, :], mask1[:, :], -1.0, 1.0,
                             ALU.mult, ALU.add)
                        P.tt("v", tmp[:, :], probs[:, :], tmp[:, :], ALU.mult)
                        P.red("v", m2[:, :], tmp[:, :], ALU.max)
                        P.ts("v", probs[:, :], tmp[:, :], m2[:, 0:1], None,
                             ALU.is_equal)
                        P.tt("v", red[:, :], m1[:, :], m2[:, :], ALU.add)
                        P.op("v", lambda e: e.reciprocal(red[:, :], red[:, :]),
                             reads=[red[:, :]], writes=[red[:, :]])
                        P.tt("v", m1[:, :], m1[:, :], red[:, :], ALU.mult)
                        P.tt("v", m2[:, :], m2[:, :], red[:, :], ALU.mult)
                        P.ts("v", mask1[:, :], mask1[:, :], m1[:, 0:1], None,
                             ALU.mult)
                        P.ts("v", probs[:, :], probs[:, :], m2[:, 0:1], None,
                             ALU.mult)
                        P.tt("v", gtm[:, :], mask1[:, :], probs[:, :], ALU.add)
                        P.transpose(pst[:, :], gtm[:, :])
                        P.cp("v", gate_fm[:, tt * 128:(tt + 1) * 128], pst[:, :])
                P.memset("v", eo[0][:, :], 0.0)
                for e in range(E):
                    with P.scope() as sex:
                        w1e_ = sex.sb("l2w1", [64, 1024], BF16)
                        w2e_ = [sex.sb("l2w2", [128, 64], BF16)
                                for _ in range(8)]
                        P.dma("g", w1e_[:, :], W["L2_ffn_W1"][e, :, :])
                        for i in range(8):
                            P.dma("g", w2e_[i][:, :],
                                  W["L2_ffn_W2"][e, i * 128:(i + 1) * 128, :])
                        he = [sex.sb("he", [128, TH], BF16) for _ in range(8)]
                        ps1 = [sex.ps("l2p1", [128, 512], F32) for _ in range(4)]
                        ps2 = sex.ps("l2p2", [64, 512], F32)
                        psg = sex.ps("l2pg", [64, 512], F32)
                        gsb = sex.sb("l2gsb", [64, 512], F32)
                        t2 = sex.sb("l2t2", [64, 512], F32)
                        for oc in range(8):
                            for tg in range(TH // 512):
                                pb = ps1[(oc * 2 + tg) % 4]
                                sl = slice(tg * 512, tg * 512 + 512)
                                P.mm_group(pb[:, :],
                                           [(w1e_[:, oc * 128:(oc + 1) * 128],
                                             cont[0][:, sl])])
                                P.act(he[oc][:, sl], pb[:, :], AF.Relu,
                                      bias=b1[:, e * 8 + oc:e * 8 + oc + 1])
                        for tg in range(TH // 512):
                            sl = slice(tg * 512, tg * 512 + 512)
                            P.mm_group(ps2[:, :],
                                       [(w2e_[i][:, :], he[i][:, sl])
                                        for i in range(8)])
                            P.mm_group(psg[:, :],
                                       [(esel[:, e * 64:(e + 1) * 64],
                                         gate_fm[:, sl])])
                            P.cp("v", gsb[:, :], psg[:, :])
                            P.stt("v", t2[:, :], ps2[:, :],
                                  b2[:, e:e + 1],
                                  gsb[:, :], ALU.add, ALU.mult)
                            P.tt("v", eo[0][:, sl], eo[0][:, sl], t2[:, :],
                                 ALU.add)
                eo16 = sh.sb("eo16", [64, TH], BF16)
                P.cp("v", eo16[:, :], eo[0][:, :])
                dec = [sh.sb("dec", [128, TH], F32) for _ in range(DC)]
                with P.scope() as sd:
                    sp2 = [sd.sb("sp2", [128, TH], BF16) for _ in range(16)]
                    wc2s = sd.sb("wc2s", [64, 2048], BF16)
                    P.dma("g", wc2s[:, :], W["L2_ffn_W_c2s"])
                    ps3 = [sd.ps("l2p3", [128, 512], F32) for _ in range(4)]
                    for oc in range(16):
                        for tg in range(TH // 512):
                            pb = ps3[(oc * 2 + tg) % 4]
                            sl = slice(tg * 512, tg * 512 + 512)
                            P.mm_group(pb[:, :],
                                       [(wc2s[:, oc * 128:(oc + 1) * 128],
                                         eo16[:, sl])])
                            P.act(sp2[oc][:, sl], pb[:, :], AF.Sigmoid,
                                  bias=b_c2s[:, oc:oc + 1])
                    for oh in range(2):
                        with P.scope() as sw:
                            wdec = load_w_slabs(P, sw, W["L2_ffn_W_dec"],
                                                "wdec", o0=oh * 512, no=512)
                            for mm in range(4):
                                m = oh * 4 + mm
                                for tg in range(TH // 512):
                                    pb = ps3[(mm * 2 + tg) % 4]
                                    sl = slice(tg * 512, tg * 512 + 512)
                                    P.mm_group(pb[:, :],
                                               [(wdec[i][:, mm * 128:(mm + 1) * 128],
                                                 sp2[i][:, sl]) for i in range(16)])
                                    P.act(dec[m][:, sl], pb[:, :], AF.Identity,
                                          bias=b_dec[:, m:m + 1])
                with P.scope() as sf:
                    t1 = sf.sb("l2t1", [128, 512], F32)

                    def out_cb(c, g, ps_r, ps_m):
                        sl = slice(g * 512, g * 512 + 512)
                        slx = slice(t0 + g * 512, t0 + g * 512 + 512)
                        P.tt("v", t1[:, :], dec[c][:, sl], ps_r[:, :],
                             ALU.mult)
                        P.tt("v", t1[:, :], t1[:, :], ps_m[:, :],
                             ALU.subtract)
                        P.ts("v", t1[:, :], t1[:, :], gh[:, c:c + 1],
                             bh[:, c:c + 1], ALU.mult, ALU.add)
                        P.tt("v", X[c][:, slx], X[c][:, slx], t1[:, :],
                             ALU.add)
                    emit_ln_generic(P, dec, 0, TH, out_cb)



def _l0_maskbig():
    m = np.zeros((128, 896), np.float32)
    for jl in range(128):
        m[jl, :384 + jl] = -1e30
    return m


def emit_l0_attn(P, X, W):
    with P.scope() as sc:
        onesr = P.ones_row()
        Q = [sc.sb("l0q", [128, T], BF16) for _ in range(DC)]
        Kf = [sc.sb("l0k", [128, T], BF16) for _ in range(DC)]
        V = [sc.sb("l0v", [128, 1024], BF16) for _ in range(T // 128)]
        sxn = P.scope()
        XN = [sxn.sb("XN", [128, T], BF16) for _ in range(DC)]
        emit_ln(P, sxn, X, XN)
        for (dst, wname, bname) in [(Q, "L0_attn_Wq", "L0_attn_bq"),
                                    (Kf, "L0_attn_Wk", "L0_attn_bk")]:
            for oh in range(2):
                with P.scope() as sw:
                    bias = load_vec_pp(P, sw, W[bname], bname)
                    wsl = load_w_slabs(P, sw, W[wname], wname,
                                       o0=oh * 512, no=512)
                    psp = [sw.ps("l0pp", [128, 512], F32) for _ in range(4)]
                    for mm in range(4):
                        m = oh * 4 + mm
                        for tg in range(T // 512):
                            pb = psp[(mm * 4 + tg) % 4]
                            sl = slice(tg * 512, tg * 512 + 512)
                            P.mm_group(pb[:, :],
                                       [(wsl[i][:, mm * 128:(mm + 1) * 128],
                                         XN[i][:, sl]) for i in range(DC)])
                            P.act(dst[m][:, sl], pb[:, :], AF.Identity,
                                  bias=bias[:, m:m + 1])
        for og in range(2):
            with P.scope() as sw:
                osl = slice(og * 512, og * 512 + 512)
                bvr = sw.sb("bvrow", [1, 1024], BF16)
                P.dma("g", bvr[:, :], W["L0_attn_bv"][None, :])
                wsl = load_w_slabs(P, sw, W["L0_attn_Wv"], "L0_attn_Wv",
                                   o0=og * 512, no=512)
                psp = [sw.ps("l0pv", [128, 512], F32) for _ in range(4)]
                for tt in range(T // 128):
                    pb = psp[tt % 4]
                    P.mm_group(pb[:, :],
                               [(XN[i][:, tt * 128:(tt + 1) * 128],
                                 wsl[i][:, :]) for i in range(DC)]
                               + [(onesr[:, :], bvr[:, osl])])
                    P.cp("v", V[tt][:, osl], pb[:, :])
        for c in range(DC):
            P.tt("v", X[c][:, :], X[c][:, :], XN[c][:, :], ALU.add)
        sxn.close()
        NT = T // 128
        cm = sc.sb("l0cm", [128, NT], F32)
        rcs = sc.sb("l0rcs", [128, NT], F32)
        mbig = P.const_scoped(sc, "l0maskbig", _l0_maskbig(), BF16)
        masks = [mbig[:, 384 - 128 * k:896 - 128 * k] for k in range(4)]
        with P.scope() as s1:
            pss = [s1.ps("l0ps", [128, 512], F32) for _ in range(4)]
            etmp = s1.sb("l0et", [128, 512], BF16)
            red = s1.sb("l0red", [128, 1], F32)
            acc = s1.sb("l0acc", [128, 1], F32)
            for jt in range(NT):
                ig0 = jt // 4
                for ig in range(ig0, 4):
                    P.mm_group(pss[ig][:, :],
                               [(Kf[d][:, jt * 128:(jt + 1) * 128],
                                 Q[d][:, ig * 512:(ig + 1) * 512])
                                for d in range(DC)])
                    if ig == ig0:
                        P.tt("v", pss[ig][:, :], pss[ig][:, :],
                             masks[jt % 4], ALU.add)
                for n, ig in enumerate(range(ig0, 4)):
                    if n == 0:
                        P.red("v", acc[:, :], pss[ig][:, :], ALU.max)
                    else:
                        P.red("v", red[:, :], pss[ig][:, :], ALU.max)
                        P.tt("v", acc[:, :], acc[:, :], red[:, :], ALU.max)
                P.ts("v", cm[:, jt:jt + 1], acc[:, :], -1.0)
                first = True
                for ig in range(ig0, 4):
                    P.act(etmp[:, :], pss[ig][:, :], AF.Exp,
                          bias=cm[:, jt:jt + 1])
                    if first:
                        P.red("v", acc[:, :], etmp[:, :], ALU.add)
                        first = False
                    else:
                        P.red("v", red[:, :], etmp[:, :], ALU.add)
                        P.tt("v", acc[:, :], acc[:, :], red[:, :], ALU.add)
                P.op("v", lambda e: e.reciprocal(rcs[:, jt:jt + 1], acc[:, :]),
                     reads=[acc[:, :]], writes=[rcs[:, jt:jt + 1]])
        for jt in range(NT):
            P.ts("v", V[jt][:, :], V[jt][:, :], rcs[:, jt:jt + 1],
                 1.0 / 32.0, ALU.mult, ALU.mult)
        for ig in range(4):
            jt_max = 4 * ig + 4
            isl = slice(ig * 512, ig * 512 + 512)
            with P.scope() as s2:
                E = [s2.sb("l0E", [128, 512], BF16) for _ in range(jt_max)]
                with P.scope() as s2e:
                    pse = [s2e.ps("l0pe", [128, 512], F32) for _ in range(4)]
                    for jt in range(jt_max):
                        pb = pse[jt % 4]
                        P.mm_group(pb[:, :],
                                   [(Kf[d][:, jt * 128:(jt + 1) * 128],
                                     Q[d][:, isl]) for d in range(DC)])
                        if jt // 4 == ig:
                            P.tt("v", pb[:, :], pb[:, :],
                                 masks[jt % 4], ALU.add)
                        P.act(E[jt][:, :], pb[:, :], AF.Exp,
                              bias=cm[:, jt:jt + 1])
                with P.scope() as s2r:
                    psr = [s2r.ps("l0pr", [128, 512], F32) for _ in range(8)]
                    for vc in range(8):
                        P.mm_group(psr[vc][:, :],
                                   [(V[jt][:, vc * 128:(vc + 1) * 128],
                                     E[jt][:, :]) for jt in range(jt_max)])
                    for vc in range(8):
                        P.tt("v", X[vc][:, isl], X[vc][:, isl],
                             psr[vc][:, :], ALU.add)



def _l1_maskT(part_lo, row0=0):
    m = np.zeros((128, 128), np.float32)
    for r in range(128 - row0):
        k = part_lo + r
        for ql in range(max(0, k - 63), min(128, k - 47)):
            m[row0 + r, ql] = 1.0
    return m


def emit_l1_attn(P, X, W):
    TH = 1024
    NH = 16
    with P.scope() as sc:
        with P.scope() as sW:
            wq16 = cast_weight_bf16(P, sW, W["L1_attn_wq"], "l1wq")
            wkv16 = cast_weight_bf16(P, sW, W["L1_attn_wkv"], "l1wkv")
            wo16 = cast_weight_bf16(P, sW, W["L1_attn_wo"], "l1wo")
        gam = load_vec_pp(P, sc, W["L1_attn_gamma"], "l1g")
        bet = load_vec_pp(P, sc, W["L1_attn_beta"], "l1b")
        mtA = P.const_scoped(sc, "l1mtA", _l1_maskT(0, row0=64), BF16)
        mtB = P.const_scoped(sc, "l1mtB", _l1_maskT(64), BF16)
        xnh = [sc.sb("xnh", [128, 512], BF16) for _ in range(DC)]
        emit_ln(P, sc, X, xnh, T // 2 - 512, 512)
        for th in range(T // TH):
            t0 = th * TH
            with P.scope() as sh:
                xn = [sh.sb("xn", [128, TH], BF16) for _ in range(DC)]
                emit_ln(P, sh, X, xn, t0, TH)
                msg = [sh.sb("l1msg", [128, TH], BF16) for _ in range(DC)]
                with P.scope() as sa:
                    q = [sa.sb("l1q", [128, TH], BF16) for _ in range(DC)]
                    kfm = [sa.sb("l1kf", [128, TH + 128], BF16)
                           for _ in range(DC)]
                    kvtm = [sa.sb("l1kv", [128, 2048], BF16) for _ in range(9)]
                    for oh in range(2):
                        with P.scope() as sw:
                            wqT = load_wT_slabs(P, sw, wq16, "wqT",
                                                o0=oh * 512, no=512)
                            psp = [sw.ps("l1pq", [128, 512], F32)
                                   for _ in range(4)]
                            for mm in range(4):
                                m = oh * 4 + mm
                                for tg in range(TH // 512):
                                    pb = psp[tg % 4]
                                    sl = slice(tg * 512, tg * 512 + 512)
                                    P.mm_group(pb[:, :],
                                               [(wqT[i][:, mm * 128:(mm + 1) * 128],
                                                 xn[i][:, sl]) for i in range(DC)])
                                    P.cp("v", q[m][:, sl], pb[:, :])
                    for og in range(4):
                        with P.scope() as sw:
                            osl = slice(og * 512, og * 512 + 512)
                            kvT = load_wT_slabs(P, sw, wkv16, "kvT",
                                                o0=og * 512, no=512)
                            psp = [sw.ps("l1pkv", [128, 512], F32)
                                   for _ in range(4)]
                            for tt in range(9):
                                if tt == 0 and th == 0:
                                    if og == 0:
                                        P.memset("v", kvtm[0][:, :], 0.0)
                                    continue
                                if tt == 0:
                                    rhs = [xnh[i][:, 384:512] for i in range(DC)]
                                else:
                                    rhs = [xn[i][:, (tt - 1) * 128:tt * 128]
                                           for i in range(DC)]
                                pb = psp[tt % 4]
                                P.mm_group(pb[:, :],
                                           [(rhs[i], kvT[i][:, :])
                                            for i in range(DC)])
                                P.cp("v", kvtm[tt][:, osl], pb[:, :])
                    with P.scope() as st:
                        pst = [st.ps("l1pt", [64, 128], BF16) for _ in range(2)]
                        for tt in range(9):
                            for h in range(NH):
                                pb = pst[(tt * NH + h) % 2]
                                P.transpose(pb[:, :],
                                            kvtm[tt][:, h * 128:h * 128 + 64])
                                P.cp("v",
                                     kfm[h // 2][64 * (h % 2):64 * (h % 2) + 64,
                                                 tt * 128:(tt + 1) * 128],
                                     pb[:, :])
                    with P.scope() as sb:
                        psA = [sb.ps("l1sA", [64, 128], F32) for _ in range(2)]
                        psB = [sb.ps("l1sB", [128, 128], F32) for _ in range(2)]
                        psM = [sb.ps("l1sM", [64, 128], F32) for _ in range(2)]
                        ptA = [sb.sb("l1ptA", [128, 128], BF16) for _ in range(2)]
                        ptB = [sb.sb("l1ptB", [128, 128], BF16) for _ in range(2)]
                        for qt in range(TH // 128):
                            for h in range(NH):
                                r = (qt * NH + h) % 2
                                hc, hp = h // 2, 64 * (h % 2)
                                qh = q[hc][hp:hp + 64,
                                           qt * 128:(qt + 1) * 128]
                                skipA = (th == 0 and qt == 0)
                                kbase = qt * 128 + 128
                                if not skipA:
                                    P.mm_group(psA[r][:, :],
                                               [(kfm[hc][hp:hp + 64,
                                                         kbase - 64:kbase], qh)])
                                    P.act(ptA[r][64:128, :], psA[r][:, :],
                                          AF.Sigmoid, scale=0.125)
                                    P.tt("v", ptA[r][64:128, :],
                                         ptA[r][64:128, :],
                                         mtA[64:128, :], ALU.mult)
                                P.mm_group(psB[r][:, :],
                                           [(kfm[hc][hp:hp + 64,
                                                     kbase:kbase + 128], qh)])
                                P.act(ptB[r][:, :], psB[r][:, :],
                                      AF.Sigmoid, scale=0.125)
                                P.tt("v", ptB[r][:, :], ptB[r][:, :],
                                     mtB[:, :], ALU.mult)
                                pairs = []
                                if not skipA:
                                    pairs.append(
                                        (kvtm[qt][64:128,
                                                  h * 128 + 64:h * 128 + 128],
                                         ptA[r][64:128, :]))
                                pairs.append(
                                    (kvtm[qt + 1][:, h * 128 + 64:h * 128 + 128],
                                     ptB[r][:, :]))
                                P.mm_group(psM[r][:, :], pairs)
                                P.cp("v",
                                     msg[hc][hp:hp + 64,
                                             qt * 128:(qt + 1) * 128],
                                     psM[r][:, :])
                y = [sh.sb("l1y", [128, TH], F32) for _ in range(DC)]
                for oh in range(2):
                    with P.scope() as sw:
                        woT = load_wT_slabs(P, sw, wo16, "woT",
                                            o0=oh * 512, no=512)
                        psp = [sw.ps("l1py", [128, 512], F32) for _ in range(4)]
                        for mm in range(4):
                            m = oh * 4 + mm
                            for tg in range(TH // 512):
                                pb = psp[tg % 4]
                                sl = slice(tg * 512, tg * 512 + 512)
                                P.mm_group(pb[:, :],
                                           [(woT[i][:, mm * 128:(mm + 1) * 128],
                                             msg[i][:, sl]) for i in range(DC)])
                                P.tt("v", y[m][:, sl], pb[:, :], xn[m][:, sl],
                                     ALU.add)
                with P.scope() as sf:
                    t1 = sf.sb("l1t1", [128, 512], F32)

                    def out_cb(c, g, ps_r, ps_m):
                        sl = slice(g * 512, g * 512 + 512)
                        slx = slice(t0 + g * 512, t0 + g * 512 + 512)
                        P.tt("v", t1[:, :], y[c][:, sl], ps_r[:, :], ALU.mult)
                        P.tt("v", t1[:, :], t1[:, :], ps_m[:, :], ALU.subtract)
                        P.ts("v", t1[:, :], t1[:, :], gam[:, c:c + 1],
                             bet[:, c:c + 1], ALU.mult, ALU.add)
                        P.tt("v", X[c][:, slx], X[c][:, slx], t1[:, :], ALU.add)
                    emit_ln_generic(P, y, 0, TH, out_cb)



def _l3_rope_tables():
    inv = 1.0 / (10000.0 ** (np.arange(0, 64, 2, dtype=np.float64) / 64.0))
    t = np.arange(1024, dtype=np.float64) % 64
    ang = inv[:, None] * t[None, :]
    cos = np.cos(ang)
    sin = np.sin(ang)
    C = np.concatenate([cos, cos], 0)
    S = np.concatenate([sin, sin], 0)
    return np.concatenate([C, C], 0), np.concatenate([S, S], 0)


def _l3_r2t():
    r = np.zeros((128, 128), np.float32)
    for blk in range(2):
        o = blk * 64
        for m in range(32):
            r[o + m + 32, o + m] = 1.0
        for m in range(32, 64):
            r[o + m - 32, o + m] = -1.0
    return r


def _l3_bdiag():
    m = np.zeros((128, 128), np.float32)
    m[:64, :64] = 1.0
    m[64:, 64:] = 1.0
    return m


def emit_l3_attn(P, X, W):
    from concourse.bass import broadcast_tensor_aps
    TH = 1024
    NH = 16
    with P.scope() as sc:
        with P.scope() as sW:
            wa16 = cast_weight_bf16(P, sW, W["L3_attn_w_attn"], "l3wa")
            wp16 = cast_weight_bf16(P, sW, W["L3_attn_w_proj"], "l3wp")
        Cnp, Snp = _l3_rope_tables()
        Ct = P.const_scoped(sc, "l3C", Cnp, BF16)
        St = P.const_scoped(sc, "l3S", Snp, BF16)
        R2T = P.const_scoped(sc, "l3R2T", _l3_r2t(), BF16)
        BD = P.const_scoped(sc, "l3bd", _l3_bdiag(), BF16)
        for th in range(T // TH):
            t0 = th * TH
            with P.scope() as sh:
                q = [sh.sb("l3q", [128, TH], BF16) for _ in range(DC)]
                k = [sh.sb("l3k", [128, TH], BF16) for _ in range(DC)]
                vtm = [sh.sb("l3v", [128, 1024], BF16) for _ in range(8)]
                msg = [sh.sb("l3m", [128, TH], BF16) for _ in range(DC)]
                with P.scope() as sx:
                    xn = [sx.sb("xn", [128, TH], BF16) for _ in range(DC)]
                    emit_ln(P, sx, X, xn, t0, TH)
                    for dst, obase in ((q, 0), (k, 1024)):
                        for ob in range(2):
                            with P.scope() as sw:
                                wT = load_wT_slabs(P, sw, wa16, "waT",
                                                   o0=obase + ob * 512, no=512)
                                psp = [sw.ps("l3pp", [128, 512], F32)
                                       for _ in range(4)]
                                for mm in range(4):
                                    m = ob * 4 + mm
                                    for tg in range(TH // 512):
                                        pb = psp[tg % 4]
                                        sl = slice(tg * 512, tg * 512 + 512)
                                        P.mm_group(pb[:, :],
                                                   [(wT[i][:, mm * 128:(mm + 1) * 128],
                                                     xn[i][:, sl]) for i in range(DC)])
                                        P.cp("v", dst[m][:, sl], pb[:, :])
                    for ob in range(2):
                        with P.scope() as sw:
                            wT = load_wT_slabs(P, sw, wa16, "waTv",
                                               o0=2048 + ob * 512, no=512)
                            psp = [sw.ps("l3pv", [128, 512], F32)
                                   for _ in range(4)]
                            for tt in range(TH // 128):
                                pb = psp[tt % 4]
                                P.mm_group(pb[:, :],
                                           [(xn[i][:, tt * 128:(tt + 1) * 128],
                                             wT[i][:, :]) for i in range(DC)])
                                P.cp("v", vtm[tt][:, ob * 512:(ob + 1) * 512],
                                     pb[:, :])
                with P.scope() as sr:
                    psr = [sr.ps("l3rr", [128, 512], F32) for _ in range(2)]
                    t1 = [sr.sb("l3t1", [128, 512], BF16) for _ in range(2)]
                    for tiles in (q, k):
                        for c in range(DC):
                            for g in range(TH // 512):
                                r = g % 2
                                sl = slice(g * 512, g * 512 + 512)
                                P.mm_group(psr[r][:, :],
                                           [(R2T[:, :], tiles[c][:, sl])])
                                P.tt("v", t1[r][:, :], psr[r][:, :],
                                     St[:, sl], ALU.mult)
                                P.tt("v", tiles[c][:, sl], tiles[c][:, sl],
                                     Ct[:, sl], ALU.mult)
                                P.tt("v", tiles[c][:, sl], tiles[c][:, sl],
                                     t1[r][:, :], ALU.add)
                with P.scope() as sb:
                    pss = [sb.ps("l3ps", [128, 512], F32) for _ in range(4)]
                    pst = [sb.ps("l3pt", [128, 128], BF16) for _ in range(2)]
                    psm = [sb.ps("l3pm", [64, 128], F32) for _ in range(2)]
                    ee = sb.sb("l3e", [128, 2048], BF16)
                    PT = sb.sb("l3P", [128, 2048], BF16)
                    mx = sb.sb("l3mx", [128, 16], F32)
                    sm = sb.sb("l3sm", [128, 16], F32)
                    for cp in range(TH // 128):
                        csl = slice(cp * 128, (cp + 1) * 128)
                        for h in range(NH):
                            hc, hp = h // 2, 64 * (h % 2)
                            P.mm_group(pss[h // 4][:, (h % 4) * 128:(h % 4 + 1) * 128],
                                       [(q[hc][hp:hp + 64, csl],
                                         k[hc][hp:hp + 64, csl])])
                        for b in range(4):
                            bsl = slice(b * 4, b * 4 + 4)
                            P.op("v", lambda e, b=b, bsl=bsl: e.tensor_reduce(
                                mx[:, bsl][:, :, None],
                                pss[b][:, :].rearrange("p (n w) -> p n w", w=128),
                                axis=AX.X, op=ALU.max, opt_input=False,
                                opt_output=False),
                                reads=[pss[b][:, :]], writes=[mx[:, bsl]])
                            ein = ee[:, b * 512:(b + 1) * 512].rearrange(
                                "p (n w) -> p n w", w=128)
                            pin = pss[b][:, :].rearrange("p (n w) -> p n w", w=128)
                            _, mb = broadcast_tensor_aps(pin, mx[:, bsl][:, :, None])
                            P.op("v", lambda e, ein=ein, pin=pin, mb=mb:
                                 e.tensor_tensor(ein, pin, mb, ALU.subtract),
                                 reads=[pss[b][:, :], mx[:, bsl]],
                                 writes=[ee[:, b * 512:(b + 1) * 512]])
                        P.act(ee[:, :], ee[:, :], AF.Exp, scale=0.125)
                        eall = ee[:, :].rearrange("p (n w) -> p n w", w=128)
                        _, bdb = broadcast_tensor_aps(eall, BD[:, None, :])
                        P.op("v", lambda e, eall=eall, bdb=bdb: e.tensor_tensor(
                            eall, eall, bdb, ALU.mult),
                            reads=[ee[:, :], BD[:, :]], writes=[ee[:, :]])
                        P.op("v", lambda e: e.tensor_reduce(
                            sm[:, :, None], ee[:, :].rearrange(
                                "p (n w) -> p n w", w=128),
                            axis=AX.X, op=ALU.add, opt_input=False,
                            opt_output=False),
                            reads=[ee[:, :]], writes=[sm[:, :]])
                        P.op("v", lambda e: e.reciprocal(sm[:, :], sm[:, :]),
                             reads=[sm[:, :]], writes=[sm[:, :]])
                        _, smb = broadcast_tensor_aps(eall, sm[:, :, None])
                        P.op("v", lambda e, eall=eall, smb=smb: e.tensor_tensor(
                            eall, eall, smb, ALU.mult),
                            reads=[ee[:, :], sm[:, :]], writes=[ee[:, :]])
                        for h in range(NH):
                            r = h % 2
                            hc, hp = h // 2, 64 * (h % 2)
                            P.transpose(pst[r][:, :],
                                        ee[:, h * 128:(h + 1) * 128])
                            P.cp("v", PT[:, h * 128:(h + 1) * 128],
                                 pst[r][:, :])
                            P.mm_group(psm[r][:, :],
                                       [(vtm[cp][:, h * 64:(h + 1) * 64],
                                         PT[:, h * 128:(h + 1) * 128])])
                            P.cp("v", msg[hc][hp:hp + 64, csl], psm[r][:, :])
                for ob in range(2):
                    with P.scope() as sw:
                        wT = load_wT_slabs(P, sw, wp16, "wpT",
                                           o0=ob * 512, no=512)
                        psp = [sw.ps("l3po", [128, 512], F32) for _ in range(2)]
                        for mm in range(4):
                            m = ob * 4 + mm
                            for tg in range(TH // 512):
                                pb = psp[tg % 4]
                                sl = slice(tg * 512, tg * 512 + 512)
                                slx = slice(t0 + tg * 512, t0 + tg * 512 + 512)
                                P.mm_group(pb[:, :],
                                           [(wT[i][:, mm * 128:(mm + 1) * 128],
                                             msg[i][:, sl]) for i in range(DC)])
                                P.tt("v", X[m][:, slx], X[m][:, slx], pb[:, :],
                                     ALU.add)



def _l2_rope_tables():
    inv = 1.0 / (10000.0 ** (np.arange(0, 32, 2, dtype=np.float64) / 32.0))
    t = np.arange(T, dtype=np.float64)
    ang = inv[:, None] * t[None, :]
    C32 = np.repeat(np.cos(ang), 2, axis=0)
    S32 = np.repeat(np.sin(ang), 2, axis=0)
    return np.tile(C32, (4, 1)), np.tile(S32, (4, 1))


def _l2_r4t():
    r = np.zeros((128, 128), np.float32)
    for blk in range(4):
        o = blk * 32
        for i in range(16):
            r[o + 2 * i + 1, o + 2 * i] = -1.0
            r[o + 2 * i, o + 2 * i + 1] = 1.0
    return r


def _l2_causal_big():
    m = np.zeros((128, 896), np.float32)
    for kr in range(128):
        m[kr, 384 + kr:] = 1.0
    return m


def _l2_rope_inplace(P, sr, tiles, Ct, St, R4T, ncols, nrows=128):
    psr = [sr.ps("l2rr", [nrows, 512], F32) for _ in range(2)]
    t1 = [sr.sb("l2rt", [nrows, 512], BF16) for _ in range(2)]
    for ti, tile in enumerate(tiles):
        for g in range(ncols // 512):
            r = g % 2
            sl = slice(g * 512, g * 512 + 512)
            P.mm_group(psr[r][:, :], [(R4T[0:nrows, 0:nrows], tile[:, sl])])
            P.tt("v", t1[r][:, :], psr[r][:, :], St[0:nrows, sl], ALU.mult)
            P.tt("v", tile[:, sl], tile[:, sl], Ct[0:nrows, sl], ALU.mult)
            P.tt("v", tile[:, sl], tile[:, sl], t1[r][:, :], ALU.add)


def load_wT_pe(P, sc, w16, name, n_out, n_in=1024):
    return load_wT_slabs(P, sc, w16, name, ni=n_in, no=n_out)


def emit_l2_attn(P, X, W):
    NH = 16
    CLIP_HI = float(np.exp(80.0 / np.sqrt(96.0)))
    CLIP_LO = float(np.exp(-80.0 / np.sqrt(96.0)))
    INVS = float(1.0 / np.sqrt(96.0))
    with P.scope() as sc:
        with P.scope() as sW:
            w16 = {n: cast_weight_bf16(P, sW, W["L2_attn_" + n], "l2a" + n)
                   for n in ["W_DQ", "W_UQ", "W_QR", "W_DKV", "W_UK",
                             "W_UV", "W_KR", "W_O"]}
        Cnp, Snp = _l2_rope_tables()
        CtD = P.const_dram("l2C", Cnp)
        StD = P.const_dram("l2S", Snp)
        R4T = P.const_scoped(sc, "l2R4T", _l2_r4t(), BF16)
        BIG = P.const_scoped(sc, "l2big", _l2_causal_big(), BF16)
        cQ = [sc.sb("l2cq", [128, T], BF16) for _ in range(4)]
        kC = [sc.sb("l2kc", [128, T], BF16) for _ in range(DC)]
        kRr2 = sc.sb("l2kr2", [96, T], BF16)
        vtm = [sc.sb("l2v", [128, 1040], BF16) for _ in range(T // 128)]

        def proj_fm(dst, wname, src, n_in, n_out, t0=0, nT=T, src_t0=0):
            for ob in range(max(1, n_out // 512)):
                no = min(512, n_out)
                with P.scope() as sw:
                    wT = load_wT_slabs(P, sw, w16[wname], wname,
                                       ni=n_in, o0=ob * 512, no=no)
                    psp = [sw.ps("l2pp", [128, 512], F32)
                           for _ in range(4)]
                    for mm in range(max(1, no // 128)):
                        m = (ob * 512) // 128 + mm
                        for tg in range(nT // 512):
                            pb = psp[tg % 4]
                            sl = slice(t0 + tg * 512, t0 + tg * 512 + 512)
                            ssl = slice(src_t0 + tg * 512,
                                        src_t0 + tg * 512 + 512)
                            P.mm_group(pb[:, :],
                                       [(wT[i][:, mm * 128:(mm + 1) * 128],
                                         src[i][:, ssl])
                                        for i in range(n_in // 128)])
                            P.cp("v", dst[m][:, sl], pb[:, :])
        with P.scope() as s2:
            cKV = [s2.sb("l2ckv", [128, T], BF16) for _ in range(4)]
            with P.scope() as skr:
                kR = skr.sb("l2krt", [32, T], BF16)
                for xh in range(2):
                    x0 = xh * 1024
                    with P.scope() as sx:
                        xn = [sx.sb("xn", [128, 1024], BF16)
                              for _ in range(DC)]
                        emit_ln(P, sx, X, xn, x0, 1024)
                        proj_fm(cQ, "W_DQ", xn, 1024, 512, t0=x0, nT=1024)
                        proj_fm(cKV, "W_DKV", xn, 1024, 512, t0=x0, nT=1024)
                        with P.scope() as sw:
                            wT = load_wT_slabs(P, sw, w16["W_KR"], "W_KR",
                                               no=32)
                            psp = sw.ps("l2pkr", [32, 512], F32)
                            for tg in range(2):
                                sl = slice(x0 + tg * 512, x0 + tg * 512 + 512)
                                slr = slice(tg * 512, tg * 512 + 512)
                                P.mm_group(psp[:, :],
                                           [(wT[i][:, :], xn[i][:, slr])
                                            for i in range(DC)])
                                P.cp("v", kR[:, sl], psp[:, :])
                with P.scope() as sr:
                    Ct = P.const_scoped(sr, "l2C", Cnp, BF16)
                    St = P.const_scoped(sr, "l2S", Snp, BF16)
                    _l2_rope_inplace(P, sr, [kR], Ct, St, R4T, T, nrows=32)
                for b in range(3):
                    P.cp("v", kRr2[32 * b:32 * b + 32, :], kR[:, :])
            proj_fm(kC, "W_UK", cKV, 512, 1024)
            for ob in range(2):
                with P.scope() as sw:
                    wT = load_wT_slabs(P, sw, w16["W_UV"], "W_UV",
                                       ni=512, o0=ob * 512, no=512)
                    psp = [sw.ps("l2pv", [128, 512], F32) for _ in range(4)]
                    for tt in range(T // 128):
                        pb = psp[tt % 4]
                        P.mm_group(pb[:, :],
                                   [(cKV[i][:, tt * 128:(tt + 1) * 128],
                                     wT[i][:, :]) for i in range(4)])
                        dst = vtm[tt][:, :].rearrange(
                            "p (h c) -> p h c", c=65)[:, ob * 8:(ob + 1) * 8, 0:64]
                        src = pb[:, :].rearrange("p (h c) -> p h c", c=64)
                        P.op("v", lambda e, dst=dst, src=src:
                             e.tensor_copy(dst, src),
                             reads=[pb[:, :]], writes=[vtm[tt][:, :]])
            for tt in range(T // 128):
                ones_sl = vtm[tt][:, :].rearrange(
                    "p (h c) -> p h c", c=65)[:, :, 64:65]
                P.op("v", lambda e, a=ones_sl: e.memset(a, 1.0),
                     reads=[], writes=[vtm[tt][:, :]])
        for qh_i in range(2):
            q0 = qh_i * 1024
            with P.scope() as sq:
                qC = [sq.sb("l2qc", [128, 1024], BF16) for _ in range(DC)]
                qR = [sq.sb("l2qr", [96, 1024], BF16) for _ in range(6)]
                P.memset("v", qR[5][32:64, :], 0.0)
                P.memset("v", qR[5][64:96, :], 0.0)
                atn = [sq.sb("l2at", [128, 1024], BF16) for _ in range(DC)]
                with P.scope() as sp:
                    for (dst, wname, no_all) in ((qC, "W_UQ", 1024),
                                                 (qR, "W_QR", 512)):
                        for ob in range(no_all // 512):
                            with P.scope() as sw:
                                wT = load_wT_slabs(P, sw, w16[wname], wname,
                                                   ni=512, o0=ob * 512, no=512)
                                psp = [sw.ps("l2pq", [128, 512], F32)
                                       for _ in range(4)]
                                for mm in range(4):
                                    m = ob * 4 + mm
                                    for tg in range(2):
                                        pb = psp[tg % 4]
                                        sl = slice(tg * 512, tg * 512 + 512)
                                        slx = slice(q0 + tg * 512,
                                                    q0 + tg * 512 + 512)
                                        P.mm_group(pb[:, :],
                                                   [(wT[i][:, mm * 128:(mm + 1) * 128],
                                                     cQ[i][:, slx])
                                                    for i in range(4)])
                                        if dst is qC:
                                            P.cp("v", dst[m][:, sl], pb[:, :])
                                        else:
                                            for j in range(4):
                                                hh = 4 * m + j
                                                P.cp("v",
                                                     dst[hh // 3][32 * (hh % 3):
                                                                  32 * (hh % 3) + 32,
                                                                  sl],
                                                     pb[32 * j:32 * j + 32, :])
                    with P.scope() as sr:
                        Ct = P.const_scoped(sr, "l2C", Cnp, BF16)
                        St = P.const_scoped(sr, "l2S", Snp, BF16)
                        _l2_rope_inplace(P, sr, qR, Ct[:, q0:q0 + 1024],
                                         St[:, q0:q0 + 1024], R4T, 1024,
                                         nrows=96)
                with P.scope() as sa:
                    E = [sa.sb("l2E", [128, 512], BF16)
                         for _ in range(8)]
                    psE = [sa.ps("l2pe", [128, 512], F32) for _ in range(4)]
                    psAT = sa.ps("l2pat", [65, 512], F32)
                    psBC = sa.ps("l2pbc", [64, 512], F32)
                    bcsb = sa.sb("l2bcsb", [64, 512], BF16)
                    r1 = sa.sb("l2r1", [1, 512], BF16)
                    onesr = P.ones_row()
                    for h in range(NH):
                        hc, hp = h // 2, 64 * (h % 2)
                        qb_base = 32 * (h % 3)
                        qr_tile = qR[h // 3]
                        for qb in range(2):
                            qsl = slice(qb * 512, qb * 512 + 512)
                            kt_max = (q0 + qb * 512 + 512) // 128
                            ngrp = (kt_max + 7) // 8
                            for kg in range(ngrp):
                                kts = list(range(kg * 8,
                                                 min(kt_max, kg * 8 + 8)))
                                for kt in kts:
                                    ksl = slice(kt * 128, kt * 128 + 128)
                                    pe = psE[kt % 4]
                                    P.mm_group(pe[:, :],
                                               [(kC[hc][hp:hp + 64, ksl],
                                                 qC[hc][hp:hp + 64, qsl]),
                                                (kRr2[qb_base:qb_base + 32, ksl],
                                                 qr_tile[qb_base:qb_base + 32,
                                                         qsl])])
                                    P.act(E[kt % 8][:, :], pe[:, :], AF.Exp,
                                          scale=INVS)
                                    P.ts("v", E[kt % 8][:, :], E[kt % 8][:, :],
                                         CLIP_HI, CLIP_LO, ALU.min, ALU.max)
                                    jd = kt - (q0 + qb * 512) // 128
                                    if 0 <= jd <= 3:
                                        P.tt("v", E[kt % 8][:, :],
                                             E[kt % 8][:, :],
                                             BIG[:, 384 - 128 * jd:896 - 128 * jd],
                                             ALU.mult)
                                P.mm_group(psAT[:, :],
                                           [(vtm[kt][:, 65 * h:65 * h + 65],
                                             E[kt % 8][:, :]) for kt in kts],
                                           start=(kg == 0),
                                           stop=(kg == ngrp - 1))
                            P.act(r1[:, :], psAT[64:65, :], AF.Ln)
                            P.act(r1[:, :], r1[:, :], AF.Exp, scale=-1.0)
                            P.mm_group(psBC[:, :],
                                       [(onesr[0:1, 0:64], r1[:, :])])
                            P.cp("v", bcsb[:, :], psBC[:, :])
                            P.tt("v", atn[hc][hp:hp + 64, qsl],
                                 psAT[0:64, :], bcsb[:, :], ALU.mult)
                for ob in range(2):
                    with P.scope() as sw:
                        wT = load_wT_slabs(P, sw, w16["W_O"], "W_O",
                                           o0=ob * 512, no=512)
                        psp = [sw.ps("l2po", [128, 512], F32) for _ in range(4)]
                        for mm in range(4):
                            m = ob * 4 + mm
                            for tg in range(2):
                                pb = psp[tg % 4]
                                sl = slice(tg * 512, tg * 512 + 512)
                                slx = slice(q0 + tg * 512, q0 + tg * 512 + 512)
                                P.mm_group(pb[:, :],
                                           [(wT[i][:, mm * 128:(mm + 1) * 128],
                                             atn[i][:, sl]) for i in range(DC)])
                                P.tt("v", X[m][:, slx], X[m][:, slx],
                                     pb[:, :], ALU.add)



CAP = 640
SENT = float(1 << 20)


def _l0f_repkT(k):
    m = np.zeros((128, 128), np.float32)
    for t in range(64):
        m[t, 2 * t + k] = 1.0
        m[64 + t, 2 * t + k] = 1.0
    return m


def _l0f_LT():
    m = np.zeros((128, 128), np.float32)
    for s in range(128):
        m[:s + 1, s] = 1.0
    return m


def emit_l0_ffn(P, X, W):
    E, F, FC = 8, 3584, 28
    cid = P.uname("l0f")
    xn_tm = P.nc.dram_tensor(cid + "_xntm", [T, 1024], BF16, kind="Internal")
    src_tab = P.nc.dram_tensor(cid + "_src", [E * CAP, 1], I32, kind="Internal")
    eout = P.nc.dram_tensor(cid + "_eout", [E * CAP, 1024], BF16,
                            kind="Internal")
    bc_tab = P.eng["g"].to_reg(E * CAP - 1)
    bc_tok = P.eng["g"].to_reg(T - 1)
    with P.scope() as sc:
        w1_16 = [cast_weight_bf16(P, sc, W["L0_ffn_w1"][e_, :, :],
                                  f"l0w1e{e_}") for e_ in range(E)]
        w2_16 = [cast_weight_bf16(P, sc, W["L0_ffn_w2"][e_, :, :],
                                  f"l0w2e{e_}") for e_ in range(E)]
        w3_16 = [cast_weight_bf16(P, sc, W["L0_ffn_w3"][e_, :, :],
                                  f"l0w3e{e_}") for e_ in range(E)]
        iota8 = P.const_scoped(sc, "iota8",
                               np.tile(np.arange(8.0), (128, 1)), F32)
        tok64 = P.const_scoped(sc, "tok64",
                               np.repeat(np.arange(64.0), 2)[:, None], F32)
        repT = [P.const_scoped(sc, f"repTf{k}", _l0f_repkT(k), F32)
                for k in range(2)]
        deinT = [P.const_scoped(sc, f"deinTf{k}",
                                _l0f_repkT(k)[:64].T.copy(), F32)
                 for k in range(2)]
        LT = P.const_scoped(sc, "cumLTf", _l0f_LT(), F32)
        onesr_f = P.ones_row_f32()
        onesc_f = P.const_sbuf("ones_col_f32", np.ones((128, 1)), F32)
        gidx = [sc.sb("l0fgi", [128, 16], F32) for _ in range(2)]
        m1 = sc.sb("l0fm1", [128, 16], F32)
        m2 = sc.sb("l0fm2", [128, 16], F32)
        with P.scope() as si:
            sent = si.sb("l0fsent", [E * CAP // 128, 128], I32)
            P.memset("v", sent[:, :], 1 << 20)
            P.dma("y", src_tab.ap().rearrange("(p c) x -> p (c x)",
                                              p=E * CAP // 128),
                  sent[:, :])
        with P.scope() as sr:
            xn = [sr.sb("xn", [128, T], BF16) for _ in range(DC)]
            emit_ln(P, sr, X, xn)
            with P.scope() as st:
                ptt = [st.ps("l0ftp", [128, 128], BF16) for _ in range(2)]
                tmt = [st.sb("l0ftm", [128, 1024], BF16) for _ in range(2)]
                for tt in range(T // 128):
                    tm = tmt[tt % 2]
                    for c in range(DC):
                        pb = ptt[(tt * DC + c) % 2]
                        P.transpose(pb[:, :],
                                    xn[c][:, tt * 128:(tt + 1) * 128])
                        P.cp("v", tm[:, c * 128:(c + 1) * 128], pb[:, :])
                    P.dma("y", xn_tm.ap()[tt * 128:(tt + 1) * 128, :],
                          tm[:, :])
            with P.scope() as sg:
                gwT = load_wT_slabs(P, sg, cast_weight_bf16(
                    P, sg, W["L0_ffn_gate_w"], "l0gw"), "l0gwT", no=8)
                psr = sg.ps("l0fpr", [128, 8], F32)
                probs = sg.sb("l0fpb", [128, 8], F32)
                red = sg.sb("l0fred", [128, 1], F32)
                i1 = sg.sb("l0fi1", [128, 16], F32)
                i2 = sg.sb("l0fi2", [128, 16], F32)
                msk = sg.sb("l0fmsk", [128, 8], F32)
                tmp = sg.sb("l0ftmp", [128, 8], F32)
                oh = [[sg.sb("l0foh", [128, 8], F32)
                       for _ in range(2)] for _ in range(T // 128)]

                def top_extract(mv, iv, ohv, tt):
                    P.red("v", mv[:, tt:tt + 1], probs[:, :], ALU.max)
                    P.ts("v", msk[:, :], probs[:, :], mv[:, tt:tt + 1], None,
                         ALU.is_equal)
                    P.cp("v", ohv[:, :], msk[:, :])
                    P.tt("v", tmp[:, :], iota8[:, :], msk[:, :], ALU.mult)
                    P.ts("v", msk[:, :], msk[:, :], -999.0, 999.0,
                         ALU.mult, ALU.add)
                    P.tt("v", tmp[:, :], tmp[:, :], msk[:, :], ALU.add)
                    P.red("v", iv[:, tt:tt + 1], tmp[:, :], ALU.min)

                for tt in range(T // 128):
                    P.mm_group(psr[:, :],
                               [(xn[i][:, tt * 128:(tt + 1) * 128],
                                 gwT[i][:, :]) for i in range(DC)])
                    P.red("v", red[:, :], psr[:, :], ALU.max)
                    P.ts("v", probs[:, :], psr[:, :], red[:, 0:1], None,
                         ALU.subtract)
                    P.act(probs[:, :], probs[:, :], AF.Exp)
                    P.red("v", red[:, :], probs[:, :], ALU.add)
                    P.op("v", lambda e: e.reciprocal(red[:, :], red[:, :]),
                         reads=[red[:, :]], writes=[red[:, :]])
                    P.ts("v", probs[:, :], probs[:, :], red[:, 0:1], None,
                         ALU.mult)
                    top_extract(m1, i1, oh[tt][0], tt)
                    P.ts("v", tmp[:, :], probs[:, :], m1[:, tt:tt + 1], None,
                         ALU.is_equal)
                    P.ts("v", tmp[:, :], tmp[:, :], -1.0, 1.0,
                         ALU.mult, ALU.add)
                    P.tt("v", probs[:, :], probs[:, :], tmp[:, :], ALU.mult)
                    top_extract(m2, i2, oh[tt][1], tt)
                tot = sg.sb("l0ftot", [1, 8], F32)
                totp = sg.ps("l0ftotp", [1, 8], F32)
                psc_r = [sg.ps("l0fpc", [128, 8], F32) for _ in range(2)]
                at_r = [sg.sb("l0fat", [128, 8], F32) for _ in range(2)]
                cum_r = [sg.sb("l0fcum", [128, 8], F32) for _ in range(2)]
                posS_r = [sg.sb("l0fpos", [128, 1], F32) for _ in range(2)]
                eS_r = [sg.sb("l0fes", [128, 1], F32) for _ in range(2)]
                offs_r = [sg.sb("l0foff", [128, 1], F32) for _ in range(2)]
                offsi_r = [sg.sb("l0foffi", [128, 1], I32) for _ in range(2)]
                tokS_r = [sg.sb("l0ftok", [128, 1], F32) for _ in range(2)]
                tokSi_r = [sg.sb("l0ftoki", [128, 1], I32) for _ in range(2)]
                drp_r = [sg.sb("l0fdrp", [128, 1], F32) for _ in range(2)]
                pde_r = [sg.ps("l0fpd", [64, 1], F32) for _ in range(2)]
                P.memset("v", tot[:, :], 0.0)
                for ch in range(2 * T // 128):
                    tt, half = ch // 2, 64 * (ch % 2)
                    psc, at_sb, cum = psc_r[ch % 2], at_r[ch % 2], cum_r[ch % 2]
                    posS, eS, offs = posS_r[ch % 2], eS_r[ch % 2], offs_r[ch % 2]
                    offs_i, tokS = offsi_r[ch % 2], tokS_r[ch % 2]
                    tokS_i, drp = tokSi_r[ch % 2], drp_r[ch % 2]
                    pde = pde_r[ch % 2]
                    P.mm_group(psc[:, :],
                               [(repT[0][half:half + 64, :],
                                 oh[tt][0][half:half + 64, :]),
                                (repT[1][half:half + 64, :],
                                 oh[tt][1][half:half + 64, :])])
                    P.cp("v", at_sb[:, :], psc[:, :])
                    P.mm_group(psc[:, :],
                               [(LT[:, :], at_sb[:, :]),
                                (onesr_f[:, :], tot[:, :])])
                    P.cp("v", cum[:, :], psc[:, :])
                    P.mm_group(totp[:, :],
                               [(onesc_f[:, :], at_sb[:, :]),
                                (onesr_f[0:1, 0:1], tot[:, :])])
                    P.cp("v", tot[:, :], totp[:, :])
                    P.tt("v", cum[:, :], cum[:, :], at_sb[:, :], ALU.mult)
                    P.red("v", posS[:, :], cum[:, :], ALU.add)
                    P.mm_group(psc[:, 0:1],
                               [(repT[0][half:half + 64, :],
                                 i1[half:half + 64, tt:tt + 1]),
                                (repT[1][half:half + 64, :],
                                 i2[half:half + 64, tt:tt + 1])])
                    P.cp("v", eS[:, :], psc[:, 0:1])
                    P.ts("v", drp[:, :], posS[:, :], float(CAP), SENT,
                         ALU.is_gt, ALU.mult)
                    P.stt("v", offs[:, :], eS[:, :], float(CAP), posS[:, :],
                          ALU.mult, ALU.add)
                    P.ts("v", offs[:, :], offs[:, :], -1.0, None, ALU.add)
                    P.tt("v", offs[:, :], offs[:, :], drp[:, :], ALU.add)
                    P.cp("v", offs_i[:, :], offs[:, :])
                    P.ts("v", tokS[:, :], tok64[:, :], 1.0, float(ch * 64),
                         ALU.mult, ALU.add)
                    P.cp("v", tokS_i[:, :], tokS[:, :])
                    P.op("g", lambda e: e.indirect_dma_start(
                        src_tab.ap(),
                        IndirectOffsetOnAxis(offs_i[:, :], 0),
                        tokS_i[:, :], None,
                        bounds_check=bc_tab, oob_is_err=False),
                        reads=[offs_i[:, :], tokS_i[:, :]],
                        writes=[src_tab.ap()], dma=True)
                    for k in range(2):
                        P.mm_group(pde[:, :], [(deinT[k][:, :], offs[:, :])])
                        P.cp("v", gidx[k][half:half + 64, tt:tt + 1],
                             pde[:, :])
        for e in range(E):
            with P.scope() as se:
                xe = [se.sb("l0fxe", [128, CAP], BF16) for _ in range(DC)]
                with P.scope() as sgt:
                    idxt = sgt.sb("l0fidx", [128, 1], I32)
                    xet = [sgt.sb("l0fxet", [128, 1024], BF16)
                           for _ in range(2)]
                    ptt = [sgt.ps("l0fxp", [128, 128], BF16)
                           for _ in range(2)]
                    for blk in range(CAP // 128):
                        xt = xet[blk % 2]
                        P.dma("y", idxt[:, :],
                              src_tab.ap()[e * CAP + blk * 128:
                                           e * CAP + blk * 128 + 128, :])
                        P.op("g", lambda e_, xt=xt: e_.indirect_dma_start(
                            xt[:, :], None, xn_tm.ap(),
                            IndirectOffsetOnAxis(idxt[:, :], 0),
                            bounds_check=bc_tok, oob_is_err=False),
                            reads=[idxt[:, :], xn_tm.ap()],
                            writes=[xt[:, :]], dma=True)
                        for c in range(DC):
                            pb = ptt[c % 2]
                            P.transpose(pb[:, :],
                                        xt[:, c * 128:(c + 1) * 128])
                            P.cp("v", xe[c][:, blk * 128:(blk + 1) * 128],
                                 pb[:, :])
                g = [se.sb("l0fg", [128, CAP], BF16) for _ in range(FC)]
                with P.scope() as sw:
                    w13 = [[sw.sb("w13s", [128, 896], BF16)
                            for _ in range(2 * DC)] for _ in range(2)]
                    psp = [sw.ps("l0fp1", [128, 512], F32) for _ in range(6)]
                    aa = [sw.sb("l0faa", [128, 512], BF16) for _ in range(3)]

                    def load13(ob):
                        ws = w13[ob % 2]
                        for i in range(DC):
                            P.dma("y", ws[i][:, :],
                                  w1_16[e].ap()[i * 128:(i + 1) * 128,
                                                ob * 896:(ob + 1) * 896],
                                  stream=f"l0a{ob % 2}y")
                            P.dma("s", ws[DC + i][:, :],
                                  w3_16[e].ap()[i * 128:(i + 1) * 128,
                                                ob * 896:(ob + 1) * 896],
                                  stream=f"l0a{ob % 2}s")
                    load13(0)
                    for ob in range(4):
                        if ob + 1 < 4:
                            load13(ob + 1)
                        ws = w13[ob % 2]
                        for m7 in range(7):
                            m = ob * 7 + m7
                            for gi, (g0, gw) in enumerate(((0, 512),
                                                          (512, 128))):
                                r = (m7 * 2 + gi) % 3
                                pa, pb = psp[2 * r], psp[2 * r + 1]
                                a = aa[r]
                                P.mm_group(pa[:, 0:gw],
                                           [(ws[i][:, m7 * 128:(m7 + 1) * 128],
                                             xe[i][:, g0:g0 + gw])
                                            for i in range(DC)])
                                P.act(a[:, 0:gw], pa[:, 0:gw], AF.Silu)
                                P.mm_group(pb[:, 0:gw],
                                           [(ws[DC + i][:, m7 * 128:(m7 + 1) * 128],
                                             xe[i][:, g0:g0 + gw])
                                            for i in range(DC)])
                                P.tt("v", g[m][:, g0:g0 + gw], pb[:, 0:gw],
                                     a[:, 0:gw], ALU.mult)
                oe = [se.sb("l0foe", [128, CAP], BF16) for _ in range(DC)]
                with P.scope() as sw:
                    w2s = [[sw.sb("w2s", [128, 512], BF16)
                            for _ in range(FC)] for _ in range(2)]
                    psp = [sw.ps("l0fp2", [128, 512], F32) for _ in range(4)]

                    def load2(og):
                        ws = w2s[og % 2]
                        for i in range(FC):
                            q = "y" if i % 2 == 0 else "s"
                            P.dma(q, ws[i][:, :],
                                  w2_16[e].ap()[i * 128:(i + 1) * 128,
                                                og * 512:(og + 1) * 512],
                                  stream=f"l0b{og % 2}{q}")
                    load2(0)
                    for og in range(2):
                        if og + 1 < 2:
                            load2(og + 1)
                        ws = w2s[og % 2]
                        for m4 in range(4):
                            m = og * 4 + m4
                            for gi, (g0, gw) in enumerate(((0, 512),
                                                          (512, 128))):
                                pa = psp[(m4 * 2 + gi) % 4]
                                P.mm_group(pa[:, 0:gw],
                                           [(ws[i][:, m4 * 128:(m4 + 1) * 128],
                                             g[i][:, g0:g0 + gw])
                                            for i in range(FC)])
                                P.cp("v", oe[m][:, g0:g0 + gw], pa[:, 0:gw])
                with P.scope() as sot:
                    ptt = [sot.ps("l0fop", [128, 128], BF16)
                           for _ in range(2)]
                    tmo = [sot.sb("l0fot", [128, 1024], BF16)
                           for _ in range(2)]
                    for blk in range(CAP // 128):
                        tm = tmo[blk % 2]
                        for c in range(DC):
                            pb = ptt[c % 2]
                            P.transpose(pb[:, :],
                                        oe[c][:, blk * 128:(blk + 1) * 128])
                            P.cp("v", tm[:, c * 128:(c + 1) * 128], pb[:, :])
                        P.dma("y", eout.ap()[e * CAP + blk * 128:
                                             e * CAP + blk * 128 + 128, :],
                              tm[:, :])
        with P.scope() as scb:
            idxt = [scb.sb("l0fci", [128, 1], I32) for _ in range(2)]
            gt = [scb.sb("l0fgt", [128, 1024], BF16) for _ in range(2)]
            acc = scb.sb("l0facc", [128, 1024], F32)
            ptt = [scb.ps("l0fcp", [128, 128], F32) for _ in range(2)]
            for tt in range(T // 128):
                for k in range(2):
                    P.cp("v", idxt[k][:, :], gidx[k][:, tt:tt + 1])
                    P.op("g", lambda e_, k=k: e_.indirect_dma_start(
                        gt[k][:, :], None, eout.ap(),
                        IndirectOffsetOnAxis(idxt[k][:, :], 0),
                        bounds_check=bc_tab, oob_is_err=False),
                        reads=[idxt[k][:, :], eout.ap()],
                        writes=[gt[k][:, :]], dma=True)
                P.ts("v", acc[:, :], gt[0][:, :], m1[:, tt:tt + 1], None,
                     ALU.mult)
                P.op("v", lambda e_, tt=tt: e_.scalar_tensor_tensor(
                    acc[:, :], gt[1][:, :], m2[:, tt:tt + 1], acc[:, :],
                    ALU.mult, ALU.add),
                    reads=[gt[1][:, :], m2[:, tt:tt + 1], acc[:, :]],
                    writes=[acc[:, :]])
                for c in range(DC):
                    pb = ptt[c % 2]
                    P.transpose(pb[:, :], acc[:, c * 128:(c + 1) * 128])
                    P.op("v", lambda e_, pb=pb, c=c, tt=tt:
                         e_.scalar_tensor_tensor(
                             X[c][:, tt * 128:(tt + 1) * 128], pb[:, :], 0.5,
                             X[c][:, tt * 128:(tt + 1) * 128],
                             ALU.mult, ALU.add),
                         reads=[pb[:, :], X[c][:, tt * 128:(tt + 1) * 128]],
                         writes=[X[c][:, tt * 128:(tt + 1) * 128]])


TRANSPOSED = {
    "L1_ffn_W_r", "L1_ffn_W_k", "L1_ffn_W_v",
    "L1_attn_wq", "L1_attn_wkv", "L1_attn_wo",
    "L2_attn_W_DQ", "L2_attn_W_UQ", "L2_attn_W_QR", "L2_attn_W_DKV",
    "L2_attn_W_UK", "L2_attn_W_UV", "L2_attn_W_KR", "L2_attn_W_O",
    "L3_attn_w_attn", "L3_attn_w_proj",
    "L3_ffn_w1_w", "L3_ffn_w2_w", "L3_ffn_h1_w", "L3_ffn_h2_w",
    "L3_ffn_w3_w",
    "L0_ffn_gate_w",
}


def stage_input(wn, arr):
    a = np.asarray(arr, dtype=np.float32)
    if wn in TRANSPOSED:
        a = a.T
    return np.ascontiguousarray(a)


def staged_shape(wn):
    shp = WEIGHT_SHAPES[wn]
    if wn in TRANSPOSED:
        return list(reversed(shp))
    return list(shp)


WEIGHT_NAMES = {
    "l1_ffn": ["L1_ffn_W_r", "L1_ffn_W_k", "L1_ffn_W_v",
               "L1_ffn_mu_r", "L1_ffn_mu_k"],
    "l3_ffn": ["L3_ffn_w1_w", "L3_ffn_w1_s", "L3_ffn_w2_w", "L3_ffn_w2_s",
               "L3_ffn_h1_w", "L3_ffn_h1_s", "L3_ffn_h2_w", "L3_ffn_h2_s",
               "L3_ffn_w3_w", "L3_ffn_w3_s"],
    "l0_ffn": ["L0_ffn_gate_w", "L0_ffn_w1", "L0_ffn_w2", "L0_ffn_w3"],
    "l2_attn": ["L2_attn_W_DQ", "L2_attn_W_UQ", "L2_attn_W_QR",
                "L2_attn_W_DKV", "L2_attn_W_UK", "L2_attn_W_UV",
                "L2_attn_W_KR", "L2_attn_W_O"],
    "l3_attn": ["L3_attn_w_attn", "L3_attn_w_proj"],
    "l1_attn": ["L1_attn_wq", "L1_attn_wkv", "L1_attn_wo",
                "L1_attn_gamma", "L1_attn_beta"],
    "l0_attn": ["L0_attn_Wq", "L0_attn_bq", "L0_attn_Wk", "L0_attn_bk",
                "L0_attn_Wv", "L0_attn_bv"],
    "l2_ffn": ["L2_ffn_W_enc", "L2_ffn_b_enc", "L2_ffn_W_s2c", "L2_ffn_b_s2c",
               "L2_ffn_Wr", "L2_ffn_br", "L2_ffn_W1", "L2_ffn_b1",
               "L2_ffn_W2", "L2_ffn_b2", "L2_ffn_W_c2s", "L2_ffn_b_c2s",
               "L2_ffn_W_dec", "L2_ffn_b_dec", "L2_ffn_ln_g", "L2_ffn_ln_b"],
}

WEIGHT_SHAPES = {
    "L1_ffn_W_r": [1024, 1024], "L1_ffn_W_k": [1024, 1024],
    "L1_ffn_W_v": [1024, 1024],
    "L1_ffn_mu_r": [1024], "L1_ffn_mu_k": [1024],
    "L3_ffn_w1_w": [4096, 1024], "L3_ffn_w1_s": [4096],
    "L3_ffn_w2_w": [4096, 1024], "L3_ffn_w2_s": [4096],
    "L3_ffn_h1_w": [4096, 4096], "L3_ffn_h1_s": [4096],
    "L3_ffn_h2_w": [4096, 4096], "L3_ffn_h2_s": [4096],
    "L3_ffn_w3_w": [1024, 4096], "L3_ffn_w3_s": [1024],
    "L0_ffn_gate_w": [8, 1024],
    "L0_ffn_w1": [8, 1024, 3584], "L0_ffn_w2": [8, 3584, 1024],
    "L0_ffn_w3": [8, 1024, 3584],
    "L2_attn_W_DQ": [512, 1024], "L2_attn_W_UQ": [1024, 512],
    "L2_attn_W_QR": [512, 512], "L2_attn_W_DKV": [512, 1024],
    "L2_attn_W_UK": [1024, 512], "L2_attn_W_UV": [1024, 512],
    "L2_attn_W_KR": [32, 1024], "L2_attn_W_O": [1024, 1024],
    "L3_attn_w_attn": [3072, 1024], "L3_attn_w_proj": [1024, 1024],
    "L1_attn_wq": [1024, 1024], "L1_attn_wkv": [2048, 1024],
    "L1_attn_wo": [1024, 1024],
    "L1_attn_gamma": [1024], "L1_attn_beta": [1024],
    "L0_attn_Wq": [1024, 1024], "L0_attn_bq": [1024],
    "L0_attn_Wk": [1024, 1024], "L0_attn_bk": [1024],
    "L0_attn_Wv": [1024, 1024], "L0_attn_bv": [1024],
    "L2_ffn_W_enc": [1024, 2048], "L2_ffn_b_enc": [2048],
    "L2_ffn_W_s2c": [2048, 64], "L2_ffn_b_s2c": [64],
    "L2_ffn_Wr": [64, 8], "L2_ffn_br": [8],
    "L2_ffn_W1": [8, 64, 1024], "L2_ffn_b1": [8, 1024],
    "L2_ffn_W2": [8, 1024, 64], "L2_ffn_b2": [8, 64],
    "L2_ffn_W_c2s": [64, 2048], "L2_ffn_b_c2s": [2048],
    "L2_ffn_W_dec": [2048, 1024], "L2_ffn_b_dec": [1024],
    "L2_ffn_ln_g": [1024], "L2_ffn_ln_b": [1024],
}

EMITTERS = {
    "l1_ffn": emit_l1_ffn,
    "l3_ffn": emit_l3_ffn,
    "l2_ffn": emit_l2_ffn,
    "l0_attn": emit_l0_attn,
    "l1_attn": emit_l1_attn,
    "l3_attn": emit_l3_attn,
    "l2_attn": emit_l2_attn,
    "l0_ffn": emit_l0_ffn,
}


CAST_PREP = {
    "l0_ffn": ([(f"l0w1e{e}", ("L0_ffn_w1", e)) for e in range(8)]
               + [(f"l0w3e{e}", ("L0_ffn_w3", e)) for e in range(8)]
               + [(f"l0w2e{e}", ("L0_ffn_w2", e)) for e in range(8)]
               + [("l0gw", ("L0_ffn_gate_w", None))]),
    "l1_ffn": [("l1wr", ("L1_ffn_W_r", None)), ("l1wk", ("L1_ffn_W_k", None)),
               ("l1wv", ("L1_ffn_W_v", None))],
    "l1_attn": [("l1wq", ("L1_attn_wq", None)),
                ("l1wkv", ("L1_attn_wkv", None)),
                ("l1wo", ("L1_attn_wo", None))],
    "l2_attn": [("l2aW_" + n, ("L2_attn_W_" + n, None))
                for n in ["DQ", "UQ", "QR", "DKV", "UK", "UV", "KR", "O"]],
    "l3_attn": [("l3wa", ("L3_attn_w_attn", None)),
                ("l3wp", ("L3_attn_w_proj", None))],
    "l3_ffn": [("l3w1", ("L3_ffn_w1_w", None)), ("l3w2", ("L3_ffn_w2_w", None)),
               ("l3h1", ("L3_ffn_h1_w", None)), ("l3h2", ("L3_ffn_h2_w", None)),
               ("l3w3", ("L3_ffn_w3_w", None))],
    "l0_attn": [], "l2_ffn": [],
}


def emit_weight_prep(P, W, steps):
    seen = set()
    for st in steps:
        if st in seen:
            continue
        seen.add(st)
        for name, (key, e) in CAST_PREP[st]:
            ap = W[key] if e is None else W[key][e, :, :]
            cast_weight_bf16(P, None, ap, name)


def build_program(steps):
    P = Prog()
    nc = P.nc
    xin = nc.dram_tensor("xT_in", [D, T], F32, kind="ExternalInput").ap()
    xout = nc.dram_tensor("xT_out", [D, T], F32, kind="ExternalOutput").ap()
    W = {}
    for st in steps:
        for wn in WEIGHT_NAMES[st]:
            if wn not in W:
                W[wn] = nc.dram_tensor(wn, staged_shape(wn), F32,
                                       kind="ExternalInput").ap()
    X = [P.sb("X", [128, T], F32) for _ in range(DC)]
    for c in range(DC):
        P.dma("y", X[c][:, :], xin[c * 128:(c + 1) * 128, :])
    emit_weight_prep(P, W, steps)
    for st in steps:
        EMITTERS[st](P, X, W)
    for c in range(DC):
        P.dma("y", xout[c * 128:(c + 1) * 128, :], X[c][:, :])
    P.eng["y"].wait_ge(P.sem["yd"], P.cnt["yd"])
    return P


ALL_STEPS = ["l0_ffn", "l0_attn", "l0_ffn",
             "l1_ffn", "l1_attn", "l1_ffn",
             "l2_ffn", "l2_attn", "l2_ffn",
             "l3_ffn", "l3_attn", "l3_ffn"]


def kernel(**inputs):
    x = np.ascontiguousarray(inputs["L0_attn_minibatch"])
    xT = [np.ascontiguousarray(x[b].T) for b in range(NCORES)]
    for group in [ALL_STEPS]:
        P = build_program(group)
        in_maps = []
        for b in range(NCORES):
            m = {"xT_in": xT[b]}
            for st in group:
                for wn in WEIGHT_NAMES[st]:
                    m[wn] = stage_input(wn, inputs[wn])
            in_maps.append(m)
        res = run_bass_kernel_spmd(P.nc, in_maps, core_ids=list(range(NCORES)))
        xT = [res.results[b]["xT_out"] for b in range(NCORES)]
    out = np.stack([xT[b].T for b in range(NCORES)], axis=0)
    return np.ascontiguousarray(out.astype(np.float32))


# version 06 (final)
# speedup vs baseline: 1.1729x; 1.1729x over previous
import contextlib
import numpy as np

import concourse.bass as bass
import concourse.mybir as mybir
from concourse.bass_utils import run_bass_kernel_spmd
from concourse.bass_core_rust import IndirectOffsetOnAxis

F32 = mybir.dt.float32
BF16 = mybir.dt.bfloat16
I32 = mybir.dt.int32
AF = mybir.ActivationFunctionType
ALU = mybir.AluOpType
AX = mybir.AxisListType

D = 1024
DC = D // 128
T = 2048
NCORES = 8


_DMA_SEM = {"y": "yd", "g": "gd", "s": "sd"}


class Prog:
    def __init__(self):
        nc = bass.Bass("TRN2", target_bir_lowering=False)
        self.nc = nc
        self.eng = {"t": nc.tensor, "v": nc.vector, "s": nc.scalar,
                    "g": nc.gpsimd, "y": nc.sync}
        self.sem = {k: nc.alloc_semaphore("S" + k)
                    for k in ["t", "v", "s", "g", "yq", "sq", "gq",
                              "yd", "gd", "sd"]}
        self.cnt = {k: 0 for k in self.sem}
        self.seen = {q: {k: 0 for k in self.sem} for q in self.eng}
        self.dma_issuer = {"yd": "y", "gd": "g", "sd": "s"}
        self.last_cut = {}
        self.writer = {}
        self.readers = {}
        self.floor = {}
        self.ranges = {}
        self.freed = []
        self.initdeps = {}
        self._uid = 0
        self.consts = {}

    def uname(self, base):
        self._uid += 1
        return f"{base}_{self._uid}"

    def sb(self, base, shape, dtype):
        return self.nc.alloc_sbuf_tensor(self.uname(base), shape, dtype,
                                         side="right")

    def sbuf_mark(self):
        return self.nc.sbuf_base

    def psum_mark(self):
        return self.nc.psum_base

    @staticmethod
    def _key(b):
        t = getattr(b, "tensor", b)
        return t.name

    def _dep(self, deps, ident):
        if ident is None:
            return
        s, v = ident
        if v > deps.get(s, 0):
            deps[s] = v

    def _gather(self, sem_name, reads, writes, accum):
        deps = {}
        for b in reads:
            self._dep(deps, self.writer.get(self._key(b)))
        for b in writes:
            k = self._key(b)
            w = self.writer.get(k)
            if w and not (accum and w[0] == sem_name):
                self._dep(deps, w)
            for s, v in self.readers.get(k, {}).items():
                self._dep(deps, (s, v))
            for s, v in self.initdeps.get(k, {}).items():
                self._dep(deps, (s, v))
        return deps

    def register_alloc(self, name, space, lo, hi):
        self.ranges[name] = (space, lo, hi)
        deps = {}
        for (sp, flo, fhi, fdeps) in self.freed:
            if sp == space and flo < hi and lo < fhi:
                for s, v in fdeps.items():
                    if v > deps.get(s, 0):
                        deps[s] = v
        if deps:
            self.initdeps[name] = deps

    def _make_cut(self, s):
        q = self.dma_issuer[s]
        if self.seen[q].get(s, 0) < self.cnt[s]:
            self.eng[q].wait_ge(self.sem[s], self.cnt[s])
            self.seen[q][s] = self.cnt[s]
        msem = {"y": "yq", "s": "sq", "g": "gq"}[q]
        self.eng[q].sem_inc(self.sem[msem], 1)
        self.cnt[msem] += 1
        self.last_cut[s] = (self.cnt[s], (msem, self.cnt[msem]))

    def _wait(self, q, deps):
        resolved = {}
        for s, v in deps.items():
            if self.is_dma_sem(s):
                cut_cnt, _ = self.last_cut.get(s, (0, None))
                if v > cut_cnt:
                    self._make_cut(s)
                ms, mv = self.last_cut[s][1]
                resolved[ms] = max(resolved.get(ms, 0), mv)
            else:
                resolved[s] = max(resolved.get(s, 0), v)
        seen = self.seen[q]
        for s, v in resolved.items():
            if seen.get(s, 0) < v:
                self.eng[q].wait_ge(self.sem[s], v)
                seen[s] = v

    def is_dma_sem(self, s):
        return s in ("yd", "gd", "sd") or s.startswith("dm_")

    def dma_stream_sem(self, name, q):
        key = "dm_" + name
        if key not in self.sem:
            self.sem[key] = self.nc.alloc_semaphore(key)
            self.cnt[key] = 0
            self.dma_issuer[key] = q
        assert self.dma_issuer[key] == q, "stream must keep one issuing queue"
        return key

    def _record(self, reads, writes, ident):
        for b in writes:
            k = self._key(b)
            self.writer[k] = ident
            self.readers[k] = {}
        for b in reads:
            k = self._key(b)
            if self.writer.get(k) != ident:
                self.readers.setdefault(k, {})[ident[0]] = ident[1]

    def op(self, q, emit, reads=(), writes=(), dma=False, stream=None):
        if dma and stream is not None:
            sem_name = self.dma_stream_sem(stream, q)
        else:
            sem_name = _DMA_SEM[q] if dma else q
        deps = self._gather(sem_name, reads, writes, accum=False)
        self._wait(q, deps)
        ins = emit(self.eng[q])
        amt = 16 if dma else 1
        ins.then_inc(self.sem[sem_name], amt)
        self.cnt[sem_name] += amt
        ident = (sem_name, self.cnt[sem_name])
        self._record(reads, writes, ident)
        return ident

    def dma(self, q, out, in_, stream=None, **kw):
        return self.op(q, lambda e: e.dma_start(out=out, in_=in_, **kw),
                       reads=[in_], writes=[out], dma=True, stream=stream)

    def mm_group(self, out_psum, pairs, extra_reads=(), start=True, stop=True):
        reads = [ap for pr in pairs for ap in pr] + list(extra_reads)
        deps = self._gather("t", reads, [out_psum], accum=not start)
        self._wait("t", deps)
        n = len(pairs)
        ins = None
        for i, (lhsT, rhs) in enumerate(pairs):
            ins = self.nc.tensor.matmul(out_psum, lhsT, rhs,
                                        start=(start and i == 0),
                                        stop=(stop and i == n - 1))
        ins.then_inc(self.sem["t"], 1)
        self.cnt["t"] += 1
        ident = ("t", self.cnt["t"])
        self._record(reads, [out_psum], ident)
        return ident

    def transpose(self, out_psum, in_, ident_tile=None):
        if ident_tile is None:
            p = in_.shape[0]
            ident_tile = self.ident128(in_.tensor.dtype)[0:p, 0:p]
        return self.op("t", lambda e: e.transpose(out_psum, in_, ident_tile),
                       reads=[in_, ident_tile], writes=[out_psum])

    def cp(self, q, out, in_):
        return self.op(q, lambda e: e.tensor_copy(out, in_),
                       reads=[in_], writes=[out])

    def tt(self, q, out, a, b, alu):
        return self.op(q, lambda e: e.tensor_tensor(out, a, b, alu),
                       reads=[a, b], writes=[out])

    def ts(self, q, out, a, s1, s2=None, op0=ALU.mult, op1=ALU.bypass):
        reads = [a] + [x for x in (s1, s2) if hasattr(x, "tensor")]
        if s2 is None:
            return self.op(q, lambda e: e.tensor_scalar(out, a, s1, None, op0),
                           reads=reads, writes=[out])
        return self.op(q, lambda e: e.tensor_scalar(out, a, s1, s2, op0, op1),
                       reads=reads, writes=[out])

    def stt(self, q, out, in0, scalar, in1, op0, op1):
        reads = [in0, in1] + ([scalar] if hasattr(scalar, "tensor") else [])
        return self.op(q, lambda e: e.scalar_tensor_tensor(
            out, in0, scalar, in1, op0, op1), reads=reads, writes=[out])

    def act(self, out, in_, func, bias=0.0, scale=1.0, q="s"):
        if isinstance(bias, float) and bias != 0.0:
            nparts = out.shape[0]
            tile = self.const_sbuf(f"f32c_{bias!r}", np.full((128, 1), bias), F32)
            bias = tile[0:nparts, 0:1]
        reads = [in_] + [x for x in (bias, scale) if hasattr(x, "tensor")]
        return self.op(q, lambda e: e.activation(
            out=out, in_=in_, func=func, bias=bias, scale=scale),
            reads=reads, writes=[out])

    def red(self, q, out, in_, alu, negate=None):
        return self.op(q, lambda e: e.tensor_reduce(
            out, in_, axis=AX.X, op=alu, negate=negate),
            reads=[in_], writes=[out])

    def memset(self, q, ap, val):
        return self.op(q, lambda e: e.memset(ap, val), reads=[], writes=[ap])

    def scope(self):
        return Scope(self)

    def const_dram(self, key, arr):
        ck = "d_" + key
        if ck not in self.consts:
            self.consts[ck] = self.nc.inline_tensor(
                np.ascontiguousarray(arr.astype(np.float32)),
                self.uname("cd_" + key))
        return self.consts[ck]

    def const_scoped(self, sc, key, arr, dtype):
        d = self.const_dram(key, arr)
        h = sc.sb("cl_" + key, list(arr.shape), dtype)
        self.dma("g", h[:, :], d.ap())
        return h

    def const_sbuf(self, key, arr, dtype):
        if key in self.consts:
            return self.consts[key]
        d = self.nc.inline_tensor(np.ascontiguousarray(arr.astype(np.float32)),
                                  self.uname("c_" + key))
        h = self.sb("cs_" + key, list(arr.shape), dtype)
        self.dma("g", h[:, :], d.ap())
        self.consts[key] = h
        return h

    def ones_col(self):
        return self.const_sbuf("ones_col", np.ones((128, 1)), BF16)

    def ones_row(self):
        return self.const_sbuf("ones_row", np.ones((1, 128)), BF16)

    def ones_row_f32(self):
        return self.const_sbuf("ones_row_f32", np.ones((1, 128)), F32)

    def ident128(self, dtype=BF16):
        key = "ident128_" + str(dtype)
        return self.const_sbuf(key, np.eye(128), dtype)


class Scope:

    def __init__(self, P):
        self.P = P
        self.stack = contextlib.ExitStack()
        self.bufs = []

    def sb(self, base, shape, dtype):
        P = self.P
        lo = P.sbuf_mark()
        h = self.stack.enter_context(
            P.nc.sbuf_tensor(P.uname(base), shape, dtype))
        P.register_alloc(h.name, "sbuf", lo, P.sbuf_mark())
        self.bufs.append(h)
        return h

    def ps(self, base, shape, dtype=F32):
        P = self.P
        lo = P.psum_mark()
        h = self.stack.enter_context(
            P.nc.psum_tensor(P.uname(base), shape, dtype))
        P.register_alloc(h.name, "psum", lo, P.psum_mark())
        self.bufs.append(h)
        return h

    def close(self):
        P = self.P
        for h in self.bufs:
            k = h.name
            deps = {}
            w = P.writer.pop(k, None)
            if w:
                deps[w[0]] = w[1]
            for s, v in P.readers.pop(k, {}).items():
                if v > deps.get(s, 0):
                    deps[s] = v
            for s, v in P.initdeps.pop(k, {}).items():
                if v > deps.get(s, 0):
                    deps[s] = v
            sp, lo, hi = P.ranges.pop(k)
            if deps:
                P.freed.append((sp, lo, hi, deps))
        self.stack.close()

    def __enter__(self):
        return self

    def __exit__(self, exc_type, *a):
        if exc_type is not None:
            return False
        self.close()
        return False



def cast_weight_bf16(P, sc, dram_f32, name):
    if not hasattr(P, "w16cache"):
        P.w16cache = {}
    if name in P.w16cache:
        return P.w16cache[name]
    src = dram_f32 if hasattr(dram_f32, "tensor") else dram_f32.ap()
    R, C = src.shape[-2], src.shape[-1]
    w16 = P.nc.dram_tensor(P.uname(name + "_16"), [R, C], BF16, kind="Internal")
    nbuf = 4
    with P.scope() as sc2:
        tmp = [sc2.sb(name + "_cast", [128, C], BF16) for _ in range(nbuf)]
        for r0 in range(0, R, 128):
            rows = min(128, R - r0)
            i = (r0 // 128) % nbuf
            b = tmp[i]
            P.dma("g", b[0:rows, :], src[r0:r0 + rows, :],
                  stream=f"cast{i}")
            P.dma("y", w16.ap()[r0:r0 + rows, :], b[0:rows, :],
                  stream=f"cast{i}s")
    P.w16cache[name] = w16
    return w16


def load_wT_slabs(P, sc, w16, name, i0=0, ni=None, o0=0, no=None):
    I, O = w16.shape
    ni = ni if ni is not None else I
    no = no if no is not None else O
    slabs = []
    for c in range(ni // 128):
        h = sc.sb(f"{name}_T{c}", [128, no], BF16)
        P.dma("y" if c % 2 == 0 else "s", h[:, :],
              w16.ap()[i0 + c * 128:i0 + (c + 1) * 128, o0:o0 + no])
        slabs.append(h)
    return slabs


def load_w_slabs(P, sc, dram_f32, name, i0=0, ni=None, o0=0, no=None):
    src = dram_f32 if hasattr(dram_f32, "tensor") else dram_f32.ap()
    I, O = src.shape[-2], src.shape[-1]
    ni = ni if ni is not None else I
    no = no if no is not None else O
    slabs = []
    for c in range(ni // 128):
        h = sc.sb(f"{name}_{c}", [128, no], BF16)
        P.dma("g", h[:, :], src[i0 + c * 128:i0 + c * 128 + 128, o0:o0 + no])
        slabs.append(h)
    return slabs


def load_vec_pp(P, sc, dram_vec, name, n=None, dtype=F32):
    src = dram_vec if hasattr(dram_vec, "tensor") else dram_vec.ap()
    N = n if n is not None else src.tensor.shape[0]
    nch = N // 128
    rowmaj = sc.sb(name + "_rm", [nch, 128], dtype)
    P.dma("g", rowmaj[:, :], src.rearrange("(c p) -> c p", p=128))
    out = sc.sb(name + "_pp", [128, nch], dtype)
    with P.scope() as s2:
        pt = s2.ps(name + "_ps", [128, nch], dtype)
        P.transpose(pt[:, :], rowmaj[:, :])
        P.cp("v", out[:, :], pt[:, :])
    return out


def emit_ln_generic(P, src, t0, nT, out_cb, bf16_of=None, eps=1e-5):
    ones = P.ones_col()
    onesr = P.ones_row_f32()
    with P.scope() as s2:
        sq = [s2.sb("ln_sq", [128, 512], BF16) for _ in range(2)]
        xb = None
        if bf16_of is None:
            xb = [s2.sb("ln_xb", [128, 512], BF16) for _ in range(2)]
        else:
            for c in range(DC):
                P.cp("v", bf16_of[c][:, 0:nT], src[c][:, t0:t0 + nT])
        ps_stat_r = [s2.ps("ln_ps", [1, 512], F32) for _ in range(2)]
        ps_stat2_r = [s2.ps("ln_ps2", [1, 512], F32) for _ in range(2)]
        ps_r_r = [s2.ps("ln_bcr", [128, 512], F32) for _ in range(2)]
        ps_m_r = [s2.ps("ln_bcm", [128, 512], F32) for _ in range(2)]
        mean_r = [s2.sb("ln_mean", [1, 512], F32) for _ in range(2)]
        rstd_r = [s2.sb("ln_rstd", [1, 512], F32) for _ in range(2)]
        mr_r = [s2.sb("ln_mr", [1, 512], F32) for _ in range(2)]
        for g in range(nT // 512):
            ps_stat, ps_stat2 = ps_stat_r[g % 2], ps_stat2_r[g % 2]
            ps_r, ps_m = ps_r_r[g % 2], ps_m_r[g % 2]
            mean, rstd, mr = mean_r[g % 2], rstd_r[g % 2], mr_r[g % 2]
            sl = slice(g * 512, g * 512 + 512)
            slx = slice(t0 + g * 512, t0 + g * 512 + 512)
            if bf16_of is not None:
                P.mm_group(ps_stat[:, :],
                           [(ones[:, :], bf16_of[c][:, sl]) for c in range(DC)])
            else:
                for c in range(DC):
                    P.cp("v", xb[c % 2][:, :], src[c][:, slx])
                    P.mm_group(ps_stat[:, :], [(ones[:, :], xb[c % 2][:, :])],
                               start=(c == 0), stop=(c == DC - 1))
            for c in range(DC):
                P.act(sq[c % 2][:, :], src[c][:, slx], AF.Square)
                P.mm_group(ps_stat2[:, :], [(ones[:, :], sq[c % 2][:, :])],
                           start=(c == 0), stop=(c == DC - 1))
            P.ts("v", mean[:, :], ps_stat[:, :], 1.0 / D)
            P.tt("v", rstd[:, :], mean[:, :], mean[:, :], ALU.mult)
            P.stt("v", rstd[:, :], ps_stat2[:, :], 1.0 / D, rstd[:, :],
                  ALU.mult, ALU.subtract)
            P.act(rstd[:, :], rstd[:, :], AF.Ln, bias=eps)
            P.act(rstd[:, :], rstd[:, :], AF.Exp, scale=-0.5)
            P.tt("v", mr[:, :], mean[:, :], rstd[:, :], ALU.mult)
            P.mm_group(ps_r[:, :], [(onesr[:, :], rstd[:, :])])
            P.mm_group(ps_m[:, :], [(onesr[:, :], mr[:, :])])
            for c in range(DC):
                out_cb(c, g, ps_r, ps_m)


def emit_ln(P, sc, X, XN, t0=0, nT=T, eps=1e-5):
    def out_cb(c, g, ps_r, ps_m):
        sl = slice(g * 512, g * 512 + 512)
        slx = slice(t0 + g * 512, t0 + g * 512 + 512)
        P.tt("v", XN[c][:, sl], X[c][:, slx], ps_r[:, :], ALU.mult)
        P.tt("v", XN[c][:, sl], XN[c][:, sl], ps_m[:, :], ALU.subtract)
    emit_ln_generic(P, X, t0, nT, out_cb, bf16_of=XN, eps=eps)


def residual_add(P, m, tg, X, psum_ap, coef, r_gate=None, sc=None):
    sl = slice(tg * 512, tg * 512 + 512)
    if r_gate is None:
        P.stt("v", X[m][:, sl], psum_ap, coef, X[m][:, sl], ALU.mult, ALU.add)
    else:
        t = sc.sb("res_t", [128, 512], F32)
        P.tt("v", t[:, :], psum_ap, r_gate, ALU.mult)
        P.stt("v", X[m][:, sl], t[:, :], coef, X[m][:, sl], ALU.mult, ALU.add)



def _l1_token_shift(P, sub, XN, mu, name, t0, nT):
    xm = [sub.sb(name, [128, nT], BF16) for _ in range(DC)]
    d = sub.sb(name + "_d", [128, nT], BF16)
    for c in range(DC):
        if t0 == 0:
            P.tt("v", d[:, 1:nT], XN[c][:, 1:nT], XN[c][:, 0:nT - 1],
                 ALU.subtract)
            P.stt("v", xm[c][:, 1:nT], d[:, 1:nT], mu[:, c:c + 1],
                  XN[c][:, 0:nT - 1], ALU.mult, ALU.add)
            P.ts("v", xm[c][:, 0:1], XN[c][:, 0:1], mu[:, c:c + 1])
        else:
            P.tt("v", d[:, :], XN[c][:, t0:t0 + nT], XN[c][:, t0 - 1:t0 + nT - 1],
                 ALU.subtract)
            P.stt("v", xm[c][:, :], d[:, :], mu[:, c:c + 1],
                  XN[c][:, t0 - 1:t0 + nT - 1], ALU.mult, ALU.add)
    return xm


def emit_l1_ffn(P, X, W):
    TH = 1024
    with P.scope() as sc:
        XN = [sc.sb("XN", [128, T], BF16) for _ in range(DC)]
        emit_ln(P, sc, X, XN)
        mu_r = load_vec_pp(P, sc, W["L1_ffn_mu_r"], "mu_r")
        mu_k = load_vec_pp(P, sc, W["L1_ffn_mu_k"], "mu_k")
        with P.scope() as sW:
            wr16 = cast_weight_bf16(P, sW, W["L1_ffn_W_r"], "l1wr")
            wk16 = cast_weight_bf16(P, sW, W["L1_ffn_W_k"], "l1wk")
            wv16 = cast_weight_bf16(P, sW, W["L1_ffn_W_v"], "l1wv")
        for th in range(T // TH):
            t0 = th * TH
            with P.scope() as sth:
                r = [sth.sb("r", [128, TH], BF16) for _ in range(DC)]
                ka = [sth.sb("ka", [128, TH], BF16) for _ in range(DC)]
                ps = [sth.ps(f"l1ps{i}", [128, 512], F32) for i in range(4)]
                with P.scope() as s1:
                    xr = _l1_token_shift(P, s1, XN, mu_r, "xr", t0, TH)
                    wrT = load_wT_slabs(P, s1, wr16, "wrT")
                    for m in range(DC):
                        for tg in range(TH // 512):
                            pb = ps[(m * (TH // 512) + tg) % 4]
                            sl = slice(tg * 512, tg * 512 + 512)
                            P.mm_group(pb[:, :],
                                       [(wrT[i][:, m * 128:(m + 1) * 128],
                                         xr[i][:, sl]) for i in range(DC)])
                            P.act(r[m][:, sl], pb[:, :], AF.Sigmoid)
                with P.scope() as s2:
                    xk = _l1_token_shift(P, s2, XN, mu_k, "xk", t0, TH)
                    wkT = load_wT_slabs(P, s2, wk16, "wkT")
                    for m in range(DC):
                        for tg in range(TH // 512):
                            pb = ps[(m * (TH // 512) + tg) % 4]
                            sl = slice(tg * 512, tg * 512 + 512)
                            P.mm_group(pb[:, :],
                                       [(wkT[i][:, m * 128:(m + 1) * 128],
                                         xk[i][:, sl]) for i in range(DC)])
                            P.op("v", lambda e, pb=pb, m=m, sl=sl:
                                 e.tensor_scalar_max(ka[m][:, sl], pb[:, :], 0.0),
                                 reads=[pb[:, :]], writes=[ka[m][:, sl]])
                            P.act(ka[m][:, sl], ka[m][:, sl], AF.Square)
                with P.scope() as s3:
                    wvT = load_wT_slabs(P, s3, wv16, "wvT")
                    for m in range(DC):
                        for tg in range(TH // 512):
                            pb = ps[(m * (TH // 512) + tg) % 4]
                            sl = slice(tg * 512, tg * 512 + 512)
                            P.mm_group(pb[:, :],
                                       [(wvT[i][:, m * 128:(m + 1) * 128],
                                         ka[i][:, sl]) for i in range(DC)])
                            residual_add(P, m, t0 // 512 + tg, X, pb[:, :], 0.5,
                                         r_gate=r[m][:, sl], sc=s3)



def emit_l3_ffn(P, X, W):
    TQ = 512
    HC = 32
    with P.scope() as sc:
        w1_16 = cast_weight_bf16(P, sc, W["L3_ffn_w1_w"], "l3w1")
        w2_16 = cast_weight_bf16(P, sc, W["L3_ffn_w2_w"], "l3w2")
        h1_16 = cast_weight_bf16(P, sc, W["L3_ffn_h1_w"], "l3h1")
        h2_16 = cast_weight_bf16(P, sc, W["L3_ffn_h2_w"], "l3h2")
        w3_16 = cast_weight_bf16(P, sc, W["L3_ffn_w3_w"], "l3w3")
        s1 = load_vec_pp(P, sc, W["L3_ffn_w1_s"], "s1")
        s2 = load_vec_pp(P, sc, W["L3_ffn_w2_s"], "s2")
        sh1 = load_vec_pp(P, sc, W["L3_ffn_h1_s"], "sh1")
        sh2 = load_vec_pp(P, sc, W["L3_ffn_h2_s"], "sh2")
        s3 = load_vec_pp(P, sc, W["L3_ffn_w3_s"], "s3")
        s3h = sc.sb("s3h", [128, 8], F32)
        P.ts("v", s3h[:, :], s3[:, :], 0.5)

        def dense4k(sq, dst, src, w16h, spp, func):
            with P.scope() as sw:
                psd = [sw.ps("l3d", [128, TQ], F32) for _ in range(4)]
                wT = [[sw.sb("hT", [128, 512], BF16) for _ in range(HC)]
                      for _ in range(2)]

                def load(ob):
                    ws = wT[ob % 2]
                    for i in range(HC):
                        q = "y" if i % 2 == 0 else "s"
                        P.dma(q, ws[i][:, :],
                              w16h.ap()[i * 128:(i + 1) * 128,
                                        ob * 512:(ob + 1) * 512],
                              stream=f"l3d{ob % 2}{q}")
                load(0)
                for ob in range(8):
                    if ob + 1 < 8:
                        load(ob + 1)
                    ws = wT[ob % 2]
                    for mm in range(4):
                        m = ob * 4 + mm
                        pa = psd[mm % 4]
                        P.mm_group(pa[:, :],
                                   [(ws[i][:, mm * 128:(mm + 1) * 128],
                                     src[i][:, :]) for i in range(HC)])
                        P.act(dst[m][:, :], pa[:, :], func, scale=spp[:, m:m + 1])

        for q in range(T // TQ):
            t0 = q * TQ
            with P.scope() as sq:
                h = [sq.sb("h", [128, TQ], BF16) for _ in range(HC)]
                h1 = [sq.sb("h1", [128, TQ], BF16) for _ in range(HC)]
                with P.scope() as sx:
                    xnq = [sx.sb("xnq", [128, TQ], BF16) for _ in range(DC)]
                    emit_ln(P, sx, X, xnq, t0, TQ)
                    aa = [sx.sb("l3aa", [128, TQ], BF16) for _ in range(2)]
                    psp = [sx.ps("l3p", [128, TQ], F32) for _ in range(4)]
                    with P.scope() as sw:
                        wTb = [[sw.sb("w12T", [128, 1024], BF16)
                                for _ in range(2 * DC)] for _ in range(2)]

                        def load12(ob):
                            ws = wTb[ob % 2]
                            for i in range(DC):
                                P.dma("y", ws[i][:, :],
                                      w1_16.ap()[i * 128:(i + 1) * 128,
                                                 ob * 1024:(ob + 1) * 1024],
                                      stream=f"l3a{ob % 2}y")
                                P.dma("s", ws[DC + i][:, :],
                                      w2_16.ap()[i * 128:(i + 1) * 128,
                                                 ob * 1024:(ob + 1) * 1024],
                                      stream=f"l3a{ob % 2}s")
                        load12(0)
                        for ob in range(4):
                            if ob + 1 < 4:
                                load12(ob + 1)
                            ws = wTb[ob % 2]
                            for mm in range(8):
                                m = ob * 8 + mm
                                pa = psp[2 * (mm % 2)]
                                pb = psp[2 * (mm % 2) + 1]
                                a = aa[mm % 2]
                                P.mm_group(pa[:, :],
                                           [(ws[i][:, mm * 128:(mm + 1) * 128],
                                             xnq[i][:, :]) for i in range(DC)])
                                P.act(a[:, :], pa[:, :], AF.Silu,
                                      scale=s1[:, m:m + 1])
                                P.mm_group(pb[:, :],
                                           [(ws[DC + i][:, mm * 128:(mm + 1) * 128],
                                             xnq[i][:, :]) for i in range(DC)])
                                P.stt("v", h[m][:, :], pb[:, :], s2[:, m:m + 1],
                                      a[:, :], ALU.mult, ALU.mult)
                dense4k(sq, h1, h, h1_16, sh1, AF.Gelu)
                dense4k(sq, h, h1, h2_16, sh2, AF.Gelu)
                with P.scope() as so:
                    ps8 = [so.ps("l3o", [128, TQ], F32) for _ in range(8)]
                    w3T = [[so.sb("w3T", [128, 1024], BF16) for _ in range(8)]
                           for _ in range(2)]

                    def load3(kg):
                        ws = w3T[kg % 2]
                        for i in range(8):
                            ic = kg * 8 + i
                            q = "y" if i % 2 == 0 else "s"
                            P.dma(q, ws[i][:, :],
                                  w3_16.ap()[ic * 128:(ic + 1) * 128, 0:1024],
                                  stream=f"l3c{kg % 2}{q}")
                    load3(0)
                    for kg in range(4):
                        if kg + 1 < 4:
                            load3(kg + 1)
                        ws = w3T[kg % 2]
                        for m in range(8):
                            P.mm_group(ps8[m][:, :],
                                       [(ws[i][:, m * 128:(m + 1) * 128],
                                         h[kg * 8 + i][:, :])
                                        for i in range(8)],
                                       start=(kg == 0), stop=(kg == 3))
                    for m in range(DC):
                        P.stt("v", X[m][:, t0:t0 + TQ], ps8[m][:, :],
                              s3h[:, m:m + 1], X[m][:, t0:t0 + TQ],
                              ALU.mult, ALU.add)



def load_vec_small(P, sc, dram_vec, name, n, dtype=BF16):
    src = dram_vec if hasattr(dram_vec, "tensor") else dram_vec.ap()
    h = sc.sb(name + "_row", [1, n], dtype)
    P.dma("g", h[:, :], src[None, :] if len(src.shape) == 1 else src)
    return h


def emit_l2_ffn(P, X, W):
    TH = 1024
    E = 8
    with P.scope() as sc:
        XN = [sc.sb("XN", [128, T], BF16) for _ in range(DC)]
        emit_ln(P, sc, X, XN)
        b_enc = load_vec_pp(P, sc, W["L2_ffn_b_enc"], "benc")
        b_c2s = load_vec_pp(P, sc, W["L2_ffn_b_c2s"], "bc2s")
        b_dec = load_vec_pp(P, sc, W["L2_ffn_b_dec"], "bdec")
        b1 = load_vec_pp(P, sc, W["L2_ffn_b1"].rearrange("e f -> (e f)"),
                         "b1", n=8192)
        b2rm = sc.sb("b2rm", [8, 64], F32)
        P.dma("g", b2rm[:, :], W["L2_ffn_b2"])
        b2 = sc.sb("b2t", [64, 8], F32)
        with P.scope() as st0:
            pb2 = st0.ps("b2ps", [64, 8], F32)
            P.transpose(pb2[:, :], b2rm[:, :])
            P.cp("v", b2[:, :], pb2[:, :])
        ln_g = load_vec_pp(P, sc, W["L2_ffn_ln_g"], "lng")
        ln_b = load_vec_pp(P, sc, W["L2_ffn_ln_b"], "lnb")
        gh = sc.sb("l2gh", [128, 8], F32)
        bh = sc.sb("l2bh", [128, 8], F32)
        P.ts("v", gh[:, :], ln_g[:, :], 0.5)
        P.ts("v", bh[:, :], ln_b[:, :], 0.5)
        b_s2c = load_vec_small(P, sc, W["L2_ffn_b_s2c"], "bs2c", 64, F32)
        bs2cp = sc.sb("bs2cp", [64, 1], F32)
        with P.scope() as st:
            pt = st.ps("bs2cT", [64, 1], F32)
            P.transpose(pt[:, :], b_s2c[:, :])
            P.cp("v", bs2cp[:, :], pt[:, :])
        wr_t = sc.sb("l2wr", [64, 8], BF16)
        P.dma("g", wr_t[:, :], W["L2_ffn_Wr"])
        brr = sc.sb("l2br", [1, 8], BF16)
        P.dma("g", brr[:, :], W["L2_ffn_br"][None, :])
        esel = P.const_sbuf("esel", np.kron(np.eye(8), np.ones((1, 64))), BF16)
        onesr = P.ones_row()
        for th in range(T // TH):
            t0 = th * TH
            with P.scope() as sh:
                cont = [sh.sb("cont", [64, TH], BF16)]
                gate_fm = sh.sb("gatef", [8, TH], BF16)
                eo = [sh.sb("eo", [64, TH], F32)]
                with P.scope() as se:
                    spikes = [se.sb("spk", [128, TH], BF16) for _ in range(16)]
                    pse = [se.ps("l2pe", [128, 512], F32) for _ in range(4)]
                    with P.scope() as sw:
                        wenc = load_w_slabs(P, sw, W["L2_ffn_W_enc"], "wenc")
                        for oc in range(16):
                            for tg in range(TH // 512):
                                pb = pse[(oc * 2 + tg) % 4]
                                sl = slice(tg * 512, tg * 512 + 512)
                                P.mm_group(pb[:, :],
                                           [(wenc[i][:, oc * 128:(oc + 1) * 128],
                                             XN[i][:, t0 + tg * 512:t0 + tg * 512 + 512])
                                            for i in range(DC)])
                                P.act(spikes[oc][:, sl], pb[:, :], AF.Sigmoid,
                                      bias=b_enc[:, oc:oc + 1])
                    with P.scope() as sw:
                        ws2c = load_w_slabs(P, sw, W["L2_ffn_W_s2c"], "ws2c")
                        psc = sw.ps("l2pc", [64, 512], F32)
                        for tg in range(TH // 512):
                            sl = slice(tg * 512, tg * 512 + 512)
                            P.mm_group(psc[:, :],
                                       [(ws2c[i][:, :], spikes[i][:, sl])
                                        for i in range(16)])
                            P.op("s", lambda e, psc=psc, sl=sl: e.activation(
                                out=cont[0][:, sl], in_=psc[:, :],
                                func=AF.Identity, bias=bs2cp[0:64, 0:1]),
                                reads=[psc[:, :], bs2cp[0:64, 0:1]],
                                writes=[cont[0][:, sl]])
                with P.scope() as sr:
                    psr = sr.ps("l2pr", [128, 8], F32)
                    pst = sr.ps("l2pt", [8, 128], BF16)
                    probs = sr.sb("probs", [128, 8], F32)
                    tmp = sr.sb("rtmp", [128, 8], F32)
                    red = sr.sb("rred", [128, 1], F32)
                    m1 = sr.sb("rm1", [128, 1], F32)
                    m2 = sr.sb("rm2", [128, 1], F32)
                    mask1 = sr.sb("rmask1", [128, 8], F32)
                    gtm = sr.sb("gtm", [128, 8], BF16)
                    for tt in range(TH // 128):
                        P.mm_group(psr[:, :],
                                   [(cont[0][:, tt * 128:(tt + 1) * 128], wr_t[:, :]),
                                    (onesr[:, :], brr[:, :])])
                        P.red("v", red[:, :], psr[:, :], ALU.max)
                        P.ts("v", probs[:, :], psr[:, :], red[:, 0:1], None,
                             ALU.subtract)
                        P.act(probs[:, :], probs[:, :], AF.Exp)
                        P.red("v", red[:, :], probs[:, :], ALU.add)
                        P.op("v", lambda e: e.reciprocal(red[:, :], red[:, :]),
                             reads=[red[:, :]], writes=[red[:, :]])
                        P.ts("v", probs[:, :], probs[:, :], red[:, 0:1], None,
                             ALU.mult)
                        P.red("v", m1[:, :], probs[:, :], ALU.max)
                        P.ts("v", mask1[:, :], probs[:, :], m1[:, 0:1], None,
                             ALU.is_equal)
                        P.ts("v", tmp[:, :], mask1[:, :], -1.0, 1.0,
                             ALU.mult, ALU.add)
                        P.tt("v", tmp[:, :], probs[:, :], tmp[:, :], ALU.mult)
                        P.red("v", m2[:, :], tmp[:, :], ALU.max)
                        P.ts("v", probs[:, :], tmp[:, :], m2[:, 0:1], None,
                             ALU.is_equal)
                        P.tt("v", red[:, :], m1[:, :], m2[:, :], ALU.add)
                        P.op("v", lambda e: e.reciprocal(red[:, :], red[:, :]),
                             reads=[red[:, :]], writes=[red[:, :]])
                        P.tt("v", m1[:, :], m1[:, :], red[:, :], ALU.mult)
                        P.tt("v", m2[:, :], m2[:, :], red[:, :], ALU.mult)
                        P.ts("v", mask1[:, :], mask1[:, :], m1[:, 0:1], None,
                             ALU.mult)
                        P.ts("v", probs[:, :], probs[:, :], m2[:, 0:1], None,
                             ALU.mult)
                        P.tt("v", gtm[:, :], mask1[:, :], probs[:, :], ALU.add)
                        P.transpose(pst[:, :], gtm[:, :])
                        P.cp("v", gate_fm[:, tt * 128:(tt + 1) * 128], pst[:, :])
                P.memset("v", eo[0][:, :], 0.0)
                for e in range(E):
                    with P.scope() as sex:
                        w1e_ = sex.sb("l2w1", [64, 1024], BF16)
                        w2e_ = [sex.sb("l2w2", [128, 64], BF16)
                                for _ in range(8)]
                        P.dma("g", w1e_[:, :], W["L2_ffn_W1"][e, :, :])
                        for i in range(8):
                            P.dma("g", w2e_[i][:, :],
                                  W["L2_ffn_W2"][e, i * 128:(i + 1) * 128, :])
                        he = [sex.sb("he", [128, TH], BF16) for _ in range(8)]
                        ps1 = [sex.ps("l2p1", [128, 512], F32) for _ in range(4)]
                        ps2 = sex.ps("l2p2", [64, 512], F32)
                        psg = sex.ps("l2pg", [64, 512], F32)
                        gsb = sex.sb("l2gsb", [64, 512], F32)
                        t2 = sex.sb("l2t2", [64, 512], F32)
                        for oc in range(8):
                            for tg in range(TH // 512):
                                pb = ps1[(oc * 2 + tg) % 4]
                                sl = slice(tg * 512, tg * 512 + 512)
                                P.mm_group(pb[:, :],
                                           [(w1e_[:, oc * 128:(oc + 1) * 128],
                                             cont[0][:, sl])])
                                P.act(he[oc][:, sl], pb[:, :], AF.Relu,
                                      bias=b1[:, e * 8 + oc:e * 8 + oc + 1])
                        for tg in range(TH // 512):
                            sl = slice(tg * 512, tg * 512 + 512)
                            P.mm_group(ps2[:, :],
                                       [(w2e_[i][:, :], he[i][:, sl])
                                        for i in range(8)])
                            P.mm_group(psg[:, :],
                                       [(esel[:, e * 64:(e + 1) * 64],
                                         gate_fm[:, sl])])
                            P.cp("v", gsb[:, :], psg[:, :])
                            P.stt("v", t2[:, :], ps2[:, :],
                                  b2[:, e:e + 1],
                                  gsb[:, :], ALU.add, ALU.mult)
                            P.tt("v", eo[0][:, sl], eo[0][:, sl], t2[:, :],
                                 ALU.add)
                eo16 = sh.sb("eo16", [64, TH], BF16)
                P.cp("v", eo16[:, :], eo[0][:, :])
                dec = [sh.sb("dec", [128, TH], F32) for _ in range(DC)]
                with P.scope() as sd:
                    sp2 = [sd.sb("sp2", [128, TH], BF16) for _ in range(16)]
                    wc2s = sd.sb("wc2s", [64, 2048], BF16)
                    P.dma("g", wc2s[:, :], W["L2_ffn_W_c2s"])
                    ps3 = [sd.ps("l2p3", [128, 512], F32) for _ in range(4)]
                    for oc in range(16):
                        for tg in range(TH // 512):
                            pb = ps3[(oc * 2 + tg) % 4]
                            sl = slice(tg * 512, tg * 512 + 512)
                            P.mm_group(pb[:, :],
                                       [(wc2s[:, oc * 128:(oc + 1) * 128],
                                         eo16[:, sl])])
                            P.act(sp2[oc][:, sl], pb[:, :], AF.Sigmoid,
                                  bias=b_c2s[:, oc:oc + 1])
                    for oh in range(2):
                        with P.scope() as sw:
                            wdec = load_w_slabs(P, sw, W["L2_ffn_W_dec"],
                                                "wdec", o0=oh * 512, no=512)
                            for mm in range(4):
                                m = oh * 4 + mm
                                for tg in range(TH // 512):
                                    pb = ps3[(mm * 2 + tg) % 4]
                                    sl = slice(tg * 512, tg * 512 + 512)
                                    P.mm_group(pb[:, :],
                                               [(wdec[i][:, mm * 128:(mm + 1) * 128],
                                                 sp2[i][:, sl]) for i in range(16)])
                                    P.act(dec[m][:, sl], pb[:, :], AF.Identity,
                                          bias=b_dec[:, m:m + 1])
                with P.scope() as sf:
                    t1 = sf.sb("l2t1", [128, 512], F32)

                    def out_cb(c, g, ps_r, ps_m):
                        sl = slice(g * 512, g * 512 + 512)
                        slx = slice(t0 + g * 512, t0 + g * 512 + 512)
                        P.tt("v", t1[:, :], dec[c][:, sl], ps_r[:, :],
                             ALU.mult)
                        P.tt("v", t1[:, :], t1[:, :], ps_m[:, :],
                             ALU.subtract)
                        P.ts("v", t1[:, :], t1[:, :], gh[:, c:c + 1],
                             bh[:, c:c + 1], ALU.mult, ALU.add)
                        P.tt("v", X[c][:, slx], X[c][:, slx], t1[:, :],
                             ALU.add)
                    emit_ln_generic(P, dec, 0, TH, out_cb)



def _l0_maskbig():
    m = np.zeros((128, 896), np.float32)
    for jl in range(128):
        m[jl, :384 + jl] = -1e30
    return m


def emit_l0_attn(P, X, W):
    with P.scope() as sc:
        onesr = P.ones_row()
        Q = [sc.sb("l0q", [128, T], BF16) for _ in range(DC)]
        Kf = [sc.sb("l0k", [128, T], BF16) for _ in range(DC)]
        V = [sc.sb("l0v", [128, 1024], BF16) for _ in range(T // 128)]
        sxn = P.scope()
        XN = [sxn.sb("XN", [128, T], BF16) for _ in range(DC)]
        emit_ln(P, sxn, X, XN)
        for (dst, wname, bname) in [(Q, "L0_attn_Wq", "L0_attn_bq"),
                                    (Kf, "L0_attn_Wk", "L0_attn_bk")]:
            for oh in range(2):
                with P.scope() as sw:
                    bias = load_vec_pp(P, sw, W[bname], bname)
                    wsl = load_w_slabs(P, sw, W[wname], wname,
                                       o0=oh * 512, no=512)
                    psp = [sw.ps("l0pp", [128, 512], F32) for _ in range(4)]
                    for mm in range(4):
                        m = oh * 4 + mm
                        for tg in range(T // 512):
                            pb = psp[(mm * 4 + tg) % 4]
                            sl = slice(tg * 512, tg * 512 + 512)
                            P.mm_group(pb[:, :],
                                       [(wsl[i][:, mm * 128:(mm + 1) * 128],
                                         XN[i][:, sl]) for i in range(DC)])
                            P.act(dst[m][:, sl], pb[:, :], AF.Identity,
                                  bias=bias[:, m:m + 1])
        for og in range(2):
            with P.scope() as sw:
                osl = slice(og * 512, og * 512 + 512)
                bvr = sw.sb("bvrow", [1, 1024], BF16)
                P.dma("g", bvr[:, :], W["L0_attn_bv"][None, :])
                wsl = load_w_slabs(P, sw, W["L0_attn_Wv"], "L0_attn_Wv",
                                   o0=og * 512, no=512)
                psp = [sw.ps("l0pv", [128, 512], F32) for _ in range(4)]
                for tt in range(T // 128):
                    pb = psp[tt % 4]
                    P.mm_group(pb[:, :],
                               [(XN[i][:, tt * 128:(tt + 1) * 128],
                                 wsl[i][:, :]) for i in range(DC)]
                               + [(onesr[:, :], bvr[:, osl])])
                    P.cp("v", V[tt][:, osl], pb[:, :])
        for c in range(DC):
            P.tt("v", X[c][:, :], X[c][:, :], XN[c][:, :], ALU.add)
        sxn.close()
        NT = T // 128
        cm = sc.sb("l0cm", [128, NT], F32)
        rcs = sc.sb("l0rcs", [128, NT], F32)
        mbig = P.const_scoped(sc, "l0maskbig", _l0_maskbig(), BF16)
        masks = [mbig[:, 384 - 128 * k:896 - 128 * k] for k in range(4)]
        with P.scope() as s1:
            pss = [s1.ps("l0ps", [128, 512], F32) for _ in range(4)]
            etmp = s1.sb("l0et", [128, 512], BF16)
            red = s1.sb("l0red", [128, 1], F32)
            acc = s1.sb("l0acc", [128, 1], F32)
            for jt in range(NT):
                ig0 = jt // 4
                for ig in range(ig0, 4):
                    P.mm_group(pss[ig][:, :],
                               [(Kf[d][:, jt * 128:(jt + 1) * 128],
                                 Q[d][:, ig * 512:(ig + 1) * 512])
                                for d in range(DC)])
                    if ig == ig0:
                        P.tt("v", pss[ig][:, :], pss[ig][:, :],
                             masks[jt % 4], ALU.add)
                for n, ig in enumerate(range(ig0, 4)):
                    if n == 0:
                        P.red("v", acc[:, :], pss[ig][:, :], ALU.max)
                    else:
                        P.red("v", red[:, :], pss[ig][:, :], ALU.max)
                        P.tt("v", acc[:, :], acc[:, :], red[:, :], ALU.max)
                P.ts("v", cm[:, jt:jt + 1], acc[:, :], -1.0)
                first = True
                for ig in range(ig0, 4):
                    P.act(etmp[:, :], pss[ig][:, :], AF.Exp,
                          bias=cm[:, jt:jt + 1])
                    if first:
                        P.red("v", acc[:, :], etmp[:, :], ALU.add)
                        first = False
                    else:
                        P.red("v", red[:, :], etmp[:, :], ALU.add)
                        P.tt("v", acc[:, :], acc[:, :], red[:, :], ALU.add)
                P.op("v", lambda e: e.reciprocal(rcs[:, jt:jt + 1], acc[:, :]),
                     reads=[acc[:, :]], writes=[rcs[:, jt:jt + 1]])
        for jt in range(NT):
            P.ts("v", V[jt][:, :], V[jt][:, :], rcs[:, jt:jt + 1],
                 1.0 / 32.0, ALU.mult, ALU.mult)
        for ig in range(4):
            jt_max = 4 * ig + 4
            isl = slice(ig * 512, ig * 512 + 512)
            with P.scope() as s2:
                E = [s2.sb("l0E", [128, 512], BF16) for _ in range(jt_max)]
                with P.scope() as s2e:
                    pse = [s2e.ps("l0pe", [128, 512], F32) for _ in range(4)]
                    for jt in range(jt_max):
                        pb = pse[jt % 4]
                        P.mm_group(pb[:, :],
                                   [(Kf[d][:, jt * 128:(jt + 1) * 128],
                                     Q[d][:, isl]) for d in range(DC)])
                        if jt // 4 == ig:
                            P.tt("v", pb[:, :], pb[:, :],
                                 masks[jt % 4], ALU.add)
                        P.act(E[jt][:, :], pb[:, :], AF.Exp,
                              bias=cm[:, jt:jt + 1])
                with P.scope() as s2r:
                    psr = [s2r.ps("l0pr", [128, 512], F32) for _ in range(8)]
                    for vc in range(8):
                        P.mm_group(psr[vc][:, :],
                                   [(V[jt][:, vc * 128:(vc + 1) * 128],
                                     E[jt][:, :]) for jt in range(jt_max)])
                    for vc in range(8):
                        P.tt("v", X[vc][:, isl], X[vc][:, isl],
                             psr[vc][:, :], ALU.add)



def _l1_maskT(part_lo, row0=0):
    m = np.zeros((128, 128), np.float32)
    for r in range(128 - row0):
        k = part_lo + r
        for ql in range(max(0, k - 63), min(128, k - 47)):
            m[row0 + r, ql] = 1.0
    return m


def emit_l1_attn(P, X, W):
    TH = 1024
    NH = 16
    with P.scope() as sc:
        with P.scope() as sW:
            wq16 = cast_weight_bf16(P, sW, W["L1_attn_wq"], "l1wq")
            wkv16 = cast_weight_bf16(P, sW, W["L1_attn_wkv"], "l1wkv")
            wo16 = cast_weight_bf16(P, sW, W["L1_attn_wo"], "l1wo")
        gam = load_vec_pp(P, sc, W["L1_attn_gamma"], "l1g")
        bet = load_vec_pp(P, sc, W["L1_attn_beta"], "l1b")
        mtA = P.const_scoped(sc, "l1mtA", _l1_maskT(0, row0=64), BF16)
        mtB = P.const_scoped(sc, "l1mtB", _l1_maskT(64), BF16)
        xnh = [sc.sb("xnh", [128, 512], BF16) for _ in range(DC)]
        emit_ln(P, sc, X, xnh, T // 2 - 512, 512)
        for th in range(T // TH):
            t0 = th * TH
            with P.scope() as sh:
                xn = [sh.sb("xn", [128, TH], BF16) for _ in range(DC)]
                emit_ln(P, sh, X, xn, t0, TH)
                msg = [sh.sb("l1msg", [128, TH], BF16) for _ in range(DC)]
                with P.scope() as sa:
                    q = [sa.sb("l1q", [128, TH], BF16) for _ in range(DC)]
                    kfm = [sa.sb("l1kf", [128, TH + 128], BF16)
                           for _ in range(DC)]
                    kvtm = [sa.sb("l1kv", [128, 2048], BF16) for _ in range(9)]
                    for oh in range(2):
                        with P.scope() as sw:
                            wqT = load_wT_slabs(P, sw, wq16, "wqT",
                                                o0=oh * 512, no=512)
                            psp = [sw.ps("l1pq", [128, 512], F32)
                                   for _ in range(4)]
                            for mm in range(4):
                                m = oh * 4 + mm
                                for tg in range(TH // 512):
                                    pb = psp[tg % 4]
                                    sl = slice(tg * 512, tg * 512 + 512)
                                    P.mm_group(pb[:, :],
                                               [(wqT[i][:, mm * 128:(mm + 1) * 128],
                                                 xn[i][:, sl]) for i in range(DC)])
                                    P.cp("v", q[m][:, sl], pb[:, :])
                    for og in range(4):
                        with P.scope() as sw:
                            osl = slice(og * 512, og * 512 + 512)
                            kvT = load_wT_slabs(P, sw, wkv16, "kvT",
                                                o0=og * 512, no=512)
                            psp = [sw.ps("l1pkv", [128, 512], F32)
                                   for _ in range(4)]
                            for tt in range(9):
                                if tt == 0 and th == 0:
                                    if og == 0:
                                        P.memset("v", kvtm[0][:, :], 0.0)
                                    continue
                                if tt == 0:
                                    rhs = [xnh[i][:, 384:512] for i in range(DC)]
                                else:
                                    rhs = [xn[i][:, (tt - 1) * 128:tt * 128]
                                           for i in range(DC)]
                                pb = psp[tt % 4]
                                P.mm_group(pb[:, :],
                                           [(rhs[i], kvT[i][:, :])
                                            for i in range(DC)])
                                P.cp("v", kvtm[tt][:, osl], pb[:, :])
                    with P.scope() as st:
                        pst = [st.ps("l1pt", [64, 128], BF16) for _ in range(2)]
                        for tt in range(9):
                            for h in range(NH):
                                pb = pst[(tt * NH + h) % 2]
                                P.transpose(pb[:, :],
                                            kvtm[tt][:, h * 128:h * 128 + 64])
                                P.cp("v",
                                     kfm[h // 2][64 * (h % 2):64 * (h % 2) + 64,
                                                 tt * 128:(tt + 1) * 128],
                                     pb[:, :])
                    with P.scope() as sb:
                        psA = [sb.ps("l1sA", [64, 128], F32) for _ in range(2)]
                        psB = [sb.ps("l1sB", [128, 128], F32) for _ in range(2)]
                        psM = [sb.ps("l1sM", [64, 128], F32) for _ in range(2)]
                        ptA = [sb.sb("l1ptA", [128, 128], BF16) for _ in range(2)]
                        ptB = [sb.sb("l1ptB", [128, 128], BF16) for _ in range(2)]
                        for qt in range(TH // 128):
                            for h in range(NH):
                                r = (qt * NH + h) % 2
                                hc, hp = h // 2, 64 * (h % 2)
                                qh = q[hc][hp:hp + 64,
                                           qt * 128:(qt + 1) * 128]
                                skipA = (th == 0 and qt == 0)
                                kbase = qt * 128 + 128
                                if not skipA:
                                    P.mm_group(psA[r][:, :],
                                               [(kfm[hc][hp:hp + 64,
                                                         kbase - 64:kbase], qh)])
                                    P.act(ptA[r][64:128, :], psA[r][:, :],
                                          AF.Sigmoid, scale=0.125)
                                    P.tt("v", ptA[r][64:128, :],
                                         ptA[r][64:128, :],
                                         mtA[64:128, :], ALU.mult)
                                P.mm_group(psB[r][:, :],
                                           [(kfm[hc][hp:hp + 64,
                                                     kbase:kbase + 128], qh)])
                                P.act(ptB[r][:, :], psB[r][:, :],
                                      AF.Sigmoid, scale=0.125)
                                P.tt("v", ptB[r][:, :], ptB[r][:, :],
                                     mtB[:, :], ALU.mult)
                                pairs = []
                                if not skipA:
                                    pairs.append(
                                        (kvtm[qt][64:128,
                                                  h * 128 + 64:h * 128 + 128],
                                         ptA[r][64:128, :]))
                                pairs.append(
                                    (kvtm[qt + 1][:, h * 128 + 64:h * 128 + 128],
                                     ptB[r][:, :]))
                                P.mm_group(psM[r][:, :], pairs)
                                P.cp("v",
                                     msg[hc][hp:hp + 64,
                                             qt * 128:(qt + 1) * 128],
                                     psM[r][:, :])
                y = [sh.sb("l1y", [128, TH], F32) for _ in range(DC)]
                for oh in range(2):
                    with P.scope() as sw:
                        woT = load_wT_slabs(P, sw, wo16, "woT",
                                            o0=oh * 512, no=512)
                        psp = [sw.ps("l1py", [128, 512], F32) for _ in range(4)]
                        for mm in range(4):
                            m = oh * 4 + mm
                            for tg in range(TH // 512):
                                pb = psp[tg % 4]
                                sl = slice(tg * 512, tg * 512 + 512)
                                P.mm_group(pb[:, :],
                                           [(woT[i][:, mm * 128:(mm + 1) * 128],
                                             msg[i][:, sl]) for i in range(DC)])
                                P.tt("v", y[m][:, sl], pb[:, :], xn[m][:, sl],
                                     ALU.add)
                with P.scope() as sf:
                    t1 = sf.sb("l1t1", [128, 512], F32)

                    def out_cb(c, g, ps_r, ps_m):
                        sl = slice(g * 512, g * 512 + 512)
                        slx = slice(t0 + g * 512, t0 + g * 512 + 512)
                        P.tt("v", t1[:, :], y[c][:, sl], ps_r[:, :], ALU.mult)
                        P.tt("v", t1[:, :], t1[:, :], ps_m[:, :], ALU.subtract)
                        P.ts("v", t1[:, :], t1[:, :], gam[:, c:c + 1],
                             bet[:, c:c + 1], ALU.mult, ALU.add)
                        P.tt("v", X[c][:, slx], X[c][:, slx], t1[:, :], ALU.add)
                    emit_ln_generic(P, y, 0, TH, out_cb)



def _l3_rope_tables():
    inv = 1.0 / (10000.0 ** (np.arange(0, 64, 2, dtype=np.float64) / 64.0))
    t = np.arange(1024, dtype=np.float64) % 64
    ang = inv[:, None] * t[None, :]
    cos = np.cos(ang)
    sin = np.sin(ang)
    C = np.concatenate([cos, cos], 0)
    S = np.concatenate([sin, sin], 0)
    return np.concatenate([C, C], 0), np.concatenate([S, S], 0)


def _l3_r2t():
    r = np.zeros((128, 128), np.float32)
    for blk in range(2):
        o = blk * 64
        for m in range(32):
            r[o + m + 32, o + m] = 1.0
        for m in range(32, 64):
            r[o + m - 32, o + m] = -1.0
    return r


def _l3_bdiag():
    m = np.zeros((128, 128), np.float32)
    m[:64, :64] = 1.0
    m[64:, 64:] = 1.0
    return m


def emit_l3_attn(P, X, W):
    from concourse.bass import broadcast_tensor_aps
    TH = 1024
    NH = 16
    with P.scope() as sc:
        with P.scope() as sW:
            wa16 = cast_weight_bf16(P, sW, W["L3_attn_w_attn"], "l3wa")
            wp16 = cast_weight_bf16(P, sW, W["L3_attn_w_proj"], "l3wp")
        Cnp, Snp = _l3_rope_tables()
        Ct = P.const_scoped(sc, "l3C", Cnp, BF16)
        St = P.const_scoped(sc, "l3S", Snp, BF16)
        R2T = P.const_scoped(sc, "l3R2T", _l3_r2t(), BF16)
        BD = P.const_scoped(sc, "l3bd", _l3_bdiag(), BF16)
        for th in range(T // TH):
            t0 = th * TH
            with P.scope() as sh:
                q = [sh.sb("l3q", [128, TH], BF16) for _ in range(DC)]
                k = [sh.sb("l3k", [128, TH], BF16) for _ in range(DC)]
                vtm = [sh.sb("l3v", [128, 1024], BF16) for _ in range(8)]
                msg = [sh.sb("l3m", [128, TH], BF16) for _ in range(DC)]
                with P.scope() as sx:
                    xn = [sx.sb("xn", [128, TH], BF16) for _ in range(DC)]
                    emit_ln(P, sx, X, xn, t0, TH)
                    for dst, obase in ((q, 0), (k, 1024)):
                        for ob in range(2):
                            with P.scope() as sw:
                                wT = load_wT_slabs(P, sw, wa16, "waT",
                                                   o0=obase + ob * 512, no=512)
                                psp = [sw.ps("l3pp", [128, 512], F32)
                                       for _ in range(4)]
                                for mm in range(4):
                                    m = ob * 4 + mm
                                    for tg in range(TH // 512):
                                        pb = psp[tg % 4]
                                        sl = slice(tg * 512, tg * 512 + 512)
                                        P.mm_group(pb[:, :],
                                                   [(wT[i][:, mm * 128:(mm + 1) * 128],
                                                     xn[i][:, sl]) for i in range(DC)])
                                        P.cp("v", dst[m][:, sl], pb[:, :])
                    for ob in range(2):
                        with P.scope() as sw:
                            wT = load_wT_slabs(P, sw, wa16, "waTv",
                                               o0=2048 + ob * 512, no=512)
                            psp = [sw.ps("l3pv", [128, 512], F32)
                                   for _ in range(4)]
                            for tt in range(TH // 128):
                                pb = psp[tt % 4]
                                P.mm_group(pb[:, :],
                                           [(xn[i][:, tt * 128:(tt + 1) * 128],
                                             wT[i][:, :]) for i in range(DC)])
                                P.cp("v", vtm[tt][:, ob * 512:(ob + 1) * 512],
                                     pb[:, :])
                with P.scope() as sr:
                    psr = [sr.ps("l3rr", [128, 512], F32) for _ in range(2)]
                    t1 = [sr.sb("l3t1", [128, 512], BF16) for _ in range(2)]
                    for tiles in (q, k):
                        for c in range(DC):
                            for g in range(TH // 512):
                                r = g % 2
                                sl = slice(g * 512, g * 512 + 512)
                                P.mm_group(psr[r][:, :],
                                           [(R2T[:, :], tiles[c][:, sl])])
                                P.tt("v", t1[r][:, :], psr[r][:, :],
                                     St[:, sl], ALU.mult)
                                P.tt("v", tiles[c][:, sl], tiles[c][:, sl],
                                     Ct[:, sl], ALU.mult)
                                P.tt("v", tiles[c][:, sl], tiles[c][:, sl],
                                     t1[r][:, :], ALU.add)
                with P.scope() as sb:
                    pss = [sb.ps("l3ps", [128, 512], F32) for _ in range(4)]
                    pst = [sb.ps("l3pt", [128, 128], BF16) for _ in range(2)]
                    psm = [sb.ps("l3pm", [64, 128], F32) for _ in range(2)]
                    ee = sb.sb("l3e", [128, 2048], BF16)
                    PT = sb.sb("l3P", [128, 2048], BF16)
                    mx = sb.sb("l3mx", [128, 16], F32)
                    sm = sb.sb("l3sm", [128, 16], F32)
                    for cp in range(TH // 128):
                        csl = slice(cp * 128, (cp + 1) * 128)
                        for h in range(NH):
                            hc, hp = h // 2, 64 * (h % 2)
                            P.mm_group(pss[h // 4][:, (h % 4) * 128:(h % 4 + 1) * 128],
                                       [(q[hc][hp:hp + 64, csl],
                                         k[hc][hp:hp + 64, csl])])
                        for b in range(4):
                            bsl = slice(b * 4, b * 4 + 4)
                            P.op("v", lambda e, b=b, bsl=bsl: e.tensor_reduce(
                                mx[:, bsl][:, :, None],
                                pss[b][:, :].rearrange("p (n w) -> p n w", w=128),
                                axis=AX.X, op=ALU.max, opt_input=False,
                                opt_output=False),
                                reads=[pss[b][:, :]], writes=[mx[:, bsl]])
                            ein = ee[:, b * 512:(b + 1) * 512].rearrange(
                                "p (n w) -> p n w", w=128)
                            pin = pss[b][:, :].rearrange("p (n w) -> p n w", w=128)
                            _, mb = broadcast_tensor_aps(pin, mx[:, bsl][:, :, None])
                            P.op("v", lambda e, ein=ein, pin=pin, mb=mb:
                                 e.tensor_tensor(ein, pin, mb, ALU.subtract),
                                 reads=[pss[b][:, :], mx[:, bsl]],
                                 writes=[ee[:, b * 512:(b + 1) * 512]])
                        P.act(ee[:, :], ee[:, :], AF.Exp, scale=0.125)
                        eall = ee[:, :].rearrange("p (n w) -> p n w", w=128)
                        _, bdb = broadcast_tensor_aps(eall, BD[:, None, :])
                        P.op("v", lambda e, eall=eall, bdb=bdb: e.tensor_tensor(
                            eall, eall, bdb, ALU.mult),
                            reads=[ee[:, :], BD[:, :]], writes=[ee[:, :]])
                        P.op("v", lambda e: e.tensor_reduce(
                            sm[:, :, None], ee[:, :].rearrange(
                                "p (n w) -> p n w", w=128),
                            axis=AX.X, op=ALU.add, opt_input=False,
                            opt_output=False),
                            reads=[ee[:, :]], writes=[sm[:, :]])
                        P.op("v", lambda e: e.reciprocal(sm[:, :], sm[:, :]),
                             reads=[sm[:, :]], writes=[sm[:, :]])
                        _, smb = broadcast_tensor_aps(eall, sm[:, :, None])
                        P.op("v", lambda e, eall=eall, smb=smb: e.tensor_tensor(
                            eall, eall, smb, ALU.mult),
                            reads=[ee[:, :], sm[:, :]], writes=[ee[:, :]])
                        for h in range(NH):
                            r = h % 2
                            hc, hp = h // 2, 64 * (h % 2)
                            P.transpose(pst[r][:, :],
                                        ee[:, h * 128:(h + 1) * 128])
                            P.cp("v", PT[:, h * 128:(h + 1) * 128],
                                 pst[r][:, :])
                            P.mm_group(psm[r][:, :],
                                       [(vtm[cp][:, h * 64:(h + 1) * 64],
                                         PT[:, h * 128:(h + 1) * 128])])
                            P.cp("v", msg[hc][hp:hp + 64, csl], psm[r][:, :])
                for ob in range(2):
                    with P.scope() as sw:
                        wT = load_wT_slabs(P, sw, wp16, "wpT",
                                           o0=ob * 512, no=512)
                        psp = [sw.ps("l3po", [128, 512], F32) for _ in range(2)]
                        for mm in range(4):
                            m = ob * 4 + mm
                            for tg in range(TH // 512):
                                pb = psp[tg % 4]
                                sl = slice(tg * 512, tg * 512 + 512)
                                slx = slice(t0 + tg * 512, t0 + tg * 512 + 512)
                                P.mm_group(pb[:, :],
                                           [(wT[i][:, mm * 128:(mm + 1) * 128],
                                             msg[i][:, sl]) for i in range(DC)])
                                P.tt("v", X[m][:, slx], X[m][:, slx], pb[:, :],
                                     ALU.add)



def _l2_rope_tables():
    inv = 1.0 / (10000.0 ** (np.arange(0, 32, 2, dtype=np.float64) / 32.0))
    t = np.arange(T, dtype=np.float64)
    ang = inv[:, None] * t[None, :]
    C32 = np.repeat(np.cos(ang), 2, axis=0)
    S32 = np.repeat(np.sin(ang), 2, axis=0)
    return np.tile(C32, (4, 1)), np.tile(S32, (4, 1))


def _l2_r4t():
    r = np.zeros((128, 128), np.float32)
    for blk in range(4):
        o = blk * 32
        for i in range(16):
            r[o + 2 * i + 1, o + 2 * i] = -1.0
            r[o + 2 * i, o + 2 * i + 1] = 1.0
    return r


def _l2_causal_big():
    m = np.zeros((128, 896), np.float32)
    for kr in range(128):
        m[kr, 384 + kr:] = 1.0
    return m


def _l2_rope_inplace(P, sr, tiles, Ct, St, R4T, ncols, nrows=128):
    psr = [sr.ps("l2rr", [nrows, 512], F32) for _ in range(2)]
    t1 = [sr.sb("l2rt", [nrows, 512], BF16) for _ in range(2)]
    for ti, tile in enumerate(tiles):
        for g in range(ncols // 512):
            r = g % 2
            sl = slice(g * 512, g * 512 + 512)
            P.mm_group(psr[r][:, :], [(R4T[0:nrows, 0:nrows], tile[:, sl])])
            P.tt("v", t1[r][:, :], psr[r][:, :], St[0:nrows, sl], ALU.mult)
            P.tt("v", tile[:, sl], tile[:, sl], Ct[0:nrows, sl], ALU.mult)
            P.tt("v", tile[:, sl], tile[:, sl], t1[r][:, :], ALU.add)


def load_wT_pe(P, sc, w16, name, n_out, n_in=1024):
    return load_wT_slabs(P, sc, w16, name, ni=n_in, no=n_out)


def emit_l2_attn(P, X, W):
    NH = 16
    CLIP_HI = float(np.exp(80.0 / np.sqrt(96.0)))
    CLIP_LO = float(np.exp(-80.0 / np.sqrt(96.0)))
    INVS = float(1.0 / np.sqrt(96.0))
    with P.scope() as sc:
        with P.scope() as sW:
            w16 = {n: cast_weight_bf16(P, sW, W["L2_attn_" + n], "l2a" + n)
                   for n in ["W_DQ", "W_UQ", "W_QR", "W_DKV", "W_UK",
                             "W_UV", "W_KR", "W_O"]}
        Cnp, Snp = _l2_rope_tables()
        CtD = P.const_dram("l2C", Cnp)
        StD = P.const_dram("l2S", Snp)
        R4T = P.const_scoped(sc, "l2R4T", _l2_r4t(), BF16)
        BIG = P.const_scoped(sc, "l2big", _l2_causal_big(), BF16)
        cQ = [sc.sb("l2cq", [128, T], BF16) for _ in range(4)]
        kC = [sc.sb("l2kc", [128, T], BF16) for _ in range(DC)]
        kRr2 = sc.sb("l2kr2", [96, T], BF16)
        vtm = [sc.sb("l2v", [128, 1040], BF16) for _ in range(T // 128)]

        def proj_fm(dst, wname, src, n_in, n_out, t0=0, nT=T, src_t0=0):
            for ob in range(max(1, n_out // 512)):
                no = min(512, n_out)
                with P.scope() as sw:
                    wT = load_wT_slabs(P, sw, w16[wname], wname,
                                       ni=n_in, o0=ob * 512, no=no)
                    psp = [sw.ps("l2pp", [128, 512], F32)
                           for _ in range(4)]
                    for mm in range(max(1, no // 128)):
                        m = (ob * 512) // 128 + mm
                        for tg in range(nT // 512):
                            pb = psp[tg % 4]
                            sl = slice(t0 + tg * 512, t0 + tg * 512 + 512)
                            ssl = slice(src_t0 + tg * 512,
                                        src_t0 + tg * 512 + 512)
                            P.mm_group(pb[:, :],
                                       [(wT[i][:, mm * 128:(mm + 1) * 128],
                                         src[i][:, ssl])
                                        for i in range(n_in // 128)])
                            P.cp("v", dst[m][:, sl], pb[:, :])
        with P.scope() as s2:
            cKV = [s2.sb("l2ckv", [128, T], BF16) for _ in range(4)]
            with P.scope() as skr:
                kR = skr.sb("l2krt", [32, T], BF16)
                for xh in range(2):
                    x0 = xh * 1024
                    with P.scope() as sx:
                        xn = [sx.sb("xn", [128, 1024], BF16)
                              for _ in range(DC)]
                        emit_ln(P, sx, X, xn, x0, 1024)
                        proj_fm(cQ, "W_DQ", xn, 1024, 512, t0=x0, nT=1024)
                        proj_fm(cKV, "W_DKV", xn, 1024, 512, t0=x0, nT=1024)
                        with P.scope() as sw:
                            wT = load_wT_slabs(P, sw, w16["W_KR"], "W_KR",
                                               no=32)
                            psp = sw.ps("l2pkr", [32, 512], F32)
                            for tg in range(2):
                                sl = slice(x0 + tg * 512, x0 + tg * 512 + 512)
                                slr = slice(tg * 512, tg * 512 + 512)
                                P.mm_group(psp[:, :],
                                           [(wT[i][:, :], xn[i][:, slr])
                                            for i in range(DC)])
                                P.cp("v", kR[:, sl], psp[:, :])
                with P.scope() as sr:
                    Ct = P.const_scoped(sr, "l2C", Cnp, BF16)
                    St = P.const_scoped(sr, "l2S", Snp, BF16)
                    _l2_rope_inplace(P, sr, [kR], Ct, St, R4T, T, nrows=32)
                for b in range(3):
                    P.cp("v", kRr2[32 * b:32 * b + 32, :], kR[:, :])
            proj_fm(kC, "W_UK", cKV, 512, 1024)
            for ob in range(2):
                with P.scope() as sw:
                    wT = load_wT_slabs(P, sw, w16["W_UV"], "W_UV",
                                       ni=512, o0=ob * 512, no=512)
                    psp = [sw.ps("l2pv", [128, 512], F32) for _ in range(4)]
                    for tt in range(T // 128):
                        pb = psp[tt % 4]
                        P.mm_group(pb[:, :],
                                   [(cKV[i][:, tt * 128:(tt + 1) * 128],
                                     wT[i][:, :]) for i in range(4)])
                        dst = vtm[tt][:, :].rearrange(
                            "p (h c) -> p h c", c=65)[:, ob * 8:(ob + 1) * 8, 0:64]
                        src = pb[:, :].rearrange("p (h c) -> p h c", c=64)
                        P.op("v", lambda e, dst=dst, src=src:
                             e.tensor_copy(dst, src),
                             reads=[pb[:, :]], writes=[vtm[tt][:, :]])
            for tt in range(T // 128):
                ones_sl = vtm[tt][:, :].rearrange(
                    "p (h c) -> p h c", c=65)[:, :, 64:65]
                P.op("v", lambda e, a=ones_sl: e.memset(a, 1.0),
                     reads=[], writes=[vtm[tt][:, :]])
        for qh_i in range(2):
            q0 = qh_i * 1024
            with P.scope() as sq:
                qC = [sq.sb("l2qc", [128, 1024], BF16) for _ in range(DC)]
                qR = [sq.sb("l2qr", [96, 1024], BF16) for _ in range(6)]
                P.memset("v", qR[5][32:64, :], 0.0)
                P.memset("v", qR[5][64:96, :], 0.0)
                atn = [sq.sb("l2at", [128, 1024], BF16) for _ in range(DC)]
                with P.scope() as sp:
                    for (dst, wname, no_all) in ((qC, "W_UQ", 1024),
                                                 (qR, "W_QR", 512)):
                        for ob in range(no_all // 512):
                            with P.scope() as sw:
                                wT = load_wT_slabs(P, sw, w16[wname], wname,
                                                   ni=512, o0=ob * 512, no=512)
                                psp = [sw.ps("l2pq", [128, 512], F32)
                                       for _ in range(4)]
                                for mm in range(4):
                                    m = ob * 4 + mm
                                    for tg in range(2):
                                        pb = psp[tg % 4]
                                        sl = slice(tg * 512, tg * 512 + 512)
                                        slx = slice(q0 + tg * 512,
                                                    q0 + tg * 512 + 512)
                                        P.mm_group(pb[:, :],
                                                   [(wT[i][:, mm * 128:(mm + 1) * 128],
                                                     cQ[i][:, slx])
                                                    for i in range(4)])
                                        if dst is qC:
                                            P.cp("v", dst[m][:, sl], pb[:, :])
                                        else:
                                            for j in range(4):
                                                hh = 4 * m + j
                                                P.cp("v",
                                                     dst[hh // 3][32 * (hh % 3):
                                                                  32 * (hh % 3) + 32,
                                                                  sl],
                                                     pb[32 * j:32 * j + 32, :])
                    with P.scope() as sr:
                        Ct = P.const_scoped(sr, "l2C", Cnp, BF16)
                        St = P.const_scoped(sr, "l2S", Snp, BF16)
                        _l2_rope_inplace(P, sr, qR, Ct[:, q0:q0 + 1024],
                                         St[:, q0:q0 + 1024], R4T, 1024,
                                         nrows=96)
                with P.scope() as sa:
                    E = [sa.sb("l2E", [128, 512], BF16)
                         for _ in range(8)]
                    psE = [sa.ps("l2pe", [128, 512], F32) for _ in range(4)]
                    psAT = sa.ps("l2pat", [65, 512], F32)
                    psBC = sa.ps("l2pbc", [64, 512], F32)
                    bcsb = sa.sb("l2bcsb", [64, 512], BF16)
                    r1 = sa.sb("l2r1", [1, 512], BF16)
                    onesr = P.ones_row()
                    for h in range(NH):
                        hc, hp = h // 2, 64 * (h % 2)
                        qb_base = 32 * (h % 3)
                        qr_tile = qR[h // 3]
                        for qb in range(2):
                            qsl = slice(qb * 512, qb * 512 + 512)
                            kt_max = (q0 + qb * 512 + 512) // 128
                            ngrp = (kt_max + 7) // 8
                            for kg in range(ngrp):
                                kts = list(range(kg * 8,
                                                 min(kt_max, kg * 8 + 8)))
                                for kt in kts:
                                    ksl = slice(kt * 128, kt * 128 + 128)
                                    pe = psE[kt % 4]
                                    P.mm_group(pe[:, :],
                                               [(kC[hc][hp:hp + 64, ksl],
                                                 qC[hc][hp:hp + 64, qsl]),
                                                (kRr2[qb_base:qb_base + 32, ksl],
                                                 qr_tile[qb_base:qb_base + 32,
                                                         qsl])])
                                    P.act(E[kt % 8][:, :], pe[:, :], AF.Exp,
                                          scale=INVS)
                                    P.ts("v", E[kt % 8][:, :], E[kt % 8][:, :],
                                         CLIP_HI, CLIP_LO, ALU.min, ALU.max)
                                    jd = kt - (q0 + qb * 512) // 128
                                    if 0 <= jd <= 3:
                                        P.tt("v", E[kt % 8][:, :],
                                             E[kt % 8][:, :],
                                             BIG[:, 384 - 128 * jd:896 - 128 * jd],
                                             ALU.mult)
                                P.mm_group(psAT[:, :],
                                           [(vtm[kt][:, 65 * h:65 * h + 65],
                                             E[kt % 8][:, :]) for kt in kts],
                                           start=(kg == 0),
                                           stop=(kg == ngrp - 1))
                            P.act(r1[:, :], psAT[64:65, :], AF.Ln)
                            P.act(r1[:, :], r1[:, :], AF.Exp, scale=-1.0)
                            P.mm_group(psBC[:, :],
                                       [(onesr[0:1, 0:64], r1[:, :])])
                            P.cp("v", bcsb[:, :], psBC[:, :])
                            P.tt("v", atn[hc][hp:hp + 64, qsl],
                                 psAT[0:64, :], bcsb[:, :], ALU.mult)
                for ob in range(2):
                    with P.scope() as sw:
                        wT = load_wT_slabs(P, sw, w16["W_O"], "W_O",
                                           o0=ob * 512, no=512)
                        psp = [sw.ps("l2po", [128, 512], F32) for _ in range(4)]
                        for mm in range(4):
                            m = ob * 4 + mm
                            for tg in range(2):
                                pb = psp[tg % 4]
                                sl = slice(tg * 512, tg * 512 + 512)
                                slx = slice(q0 + tg * 512, q0 + tg * 512 + 512)
                                P.mm_group(pb[:, :],
                                           [(wT[i][:, mm * 128:(mm + 1) * 128],
                                             atn[i][:, sl]) for i in range(DC)])
                                P.tt("v", X[m][:, slx], X[m][:, slx],
                                     pb[:, :], ALU.add)



CAP = 640
SENT = float(1 << 20)


def _l0f_repkT(k):
    m = np.zeros((128, 128), np.float32)
    for t in range(64):
        m[t, 2 * t + k] = 1.0
        m[64 + t, 2 * t + k] = 1.0
    return m


def _l0f_LT():
    m = np.zeros((128, 128), np.float32)
    for s in range(128):
        m[:s + 1, s] = 1.0
    return m


def emit_l0_ffn(P, X, W):
    E, F, FC = 8, 3584, 28
    cid = P.uname("l0f")
    xn_tm = P.nc.dram_tensor(cid + "_xntm", [T, 1024], BF16, kind="Internal")
    src_tab = P.nc.dram_tensor(cid + "_src", [E * CAP, 1], I32, kind="Internal")
    eout = P.nc.dram_tensor(cid + "_eout", [E * CAP, 1024], BF16,
                            kind="Internal")
    bc_tab = P.eng["g"].to_reg(E * CAP - 1)
    bc_tok = P.eng["g"].to_reg(T - 1)
    with P.scope() as sc:
        w1_16 = [cast_weight_bf16(P, sc, W["L0_ffn_w1"][e_, :, :],
                                  f"l0w1e{e_}") for e_ in range(E)]
        w2_16 = [cast_weight_bf16(P, sc, W["L0_ffn_w2"][e_, :, :],
                                  f"l0w2e{e_}") for e_ in range(E)]
        w3_16 = [cast_weight_bf16(P, sc, W["L0_ffn_w3"][e_, :, :],
                                  f"l0w3e{e_}") for e_ in range(E)]
        iota8 = P.const_scoped(sc, "iota8",
                               np.tile(np.arange(8.0), (128, 1)), F32)
        tok64 = P.const_scoped(sc, "tok64",
                               np.repeat(np.arange(64.0), 2)[:, None], F32)
        repT = [P.const_scoped(sc, f"repTf{k}", _l0f_repkT(k), F32)
                for k in range(2)]
        deinT = [P.const_scoped(sc, f"deinTf{k}",
                                _l0f_repkT(k)[:64].T.copy(), F32)
                 for k in range(2)]
        LT = P.const_scoped(sc, "cumLTf", _l0f_LT(), F32)
        onesr_f = P.ones_row_f32()
        onesc_f = P.const_sbuf("ones_col_f32", np.ones((128, 1)), F32)
        gidx = [sc.sb("l0fgi", [128, 16], F32) for _ in range(2)]
        m1 = sc.sb("l0fm1", [128, 16], F32)
        m2 = sc.sb("l0fm2", [128, 16], F32)
        with P.scope() as si:
            sent = si.sb("l0fsent", [E * CAP // 128, 128], I32)
            P.memset("v", sent[:, :], 1 << 20)
            P.dma("y", src_tab.ap().rearrange("(p c) x -> p (c x)",
                                              p=E * CAP // 128),
                  sent[:, :])
        with P.scope() as sr:
            xn = [sr.sb("xn", [128, T], BF16) for _ in range(DC)]
            emit_ln(P, sr, X, xn)
            with P.scope() as st:
                ptt = [st.ps("l0ftp", [128, 128], BF16) for _ in range(2)]
                tmt = [st.sb("l0ftm", [128, 1024], BF16) for _ in range(2)]
                for tt in range(T // 128):
                    tm = tmt[tt % 2]
                    for c in range(DC):
                        pb = ptt[(tt * DC + c) % 2]
                        P.transpose(pb[:, :],
                                    xn[c][:, tt * 128:(tt + 1) * 128])
                        P.cp("v", tm[:, c * 128:(c + 1) * 128], pb[:, :])
                    P.dma("y", xn_tm.ap()[tt * 128:(tt + 1) * 128, :],
                          tm[:, :])
            with P.scope() as sg:
                gwT = load_wT_slabs(P, sg, cast_weight_bf16(
                    P, sg, W["L0_ffn_gate_w"], "l0gw"), "l0gwT", no=8)
                psr = sg.ps("l0fpr", [128, 8], F32)
                probs = sg.sb("l0fpb", [128, 8], F32)
                red = sg.sb("l0fred", [128, 1], F32)
                i1 = sg.sb("l0fi1", [128, 16], F32)
                i2 = sg.sb("l0fi2", [128, 16], F32)
                msk = sg.sb("l0fmsk", [128, 8], F32)
                tmp = sg.sb("l0ftmp", [128, 8], F32)
                oh = [[sg.sb("l0foh", [128, 8], F32)
                       for _ in range(2)] for _ in range(T // 128)]

                def top_extract(mv, iv, ohv, tt):
                    P.red("v", mv[:, tt:tt + 1], probs[:, :], ALU.max)
                    P.ts("v", msk[:, :], probs[:, :], mv[:, tt:tt + 1], None,
                         ALU.is_equal)
                    P.cp("v", ohv[:, :], msk[:, :])
                    P.tt("v", tmp[:, :], iota8[:, :], msk[:, :], ALU.mult)
                    P.ts("v", msk[:, :], msk[:, :], -999.0, 999.0,
                         ALU.mult, ALU.add)
                    P.tt("v", tmp[:, :], tmp[:, :], msk[:, :], ALU.add)
                    P.red("v", iv[:, tt:tt + 1], tmp[:, :], ALU.min)

                for tt in range(T // 128):
                    P.mm_group(psr[:, :],
                               [(xn[i][:, tt * 128:(tt + 1) * 128],
                                 gwT[i][:, :]) for i in range(DC)])
                    P.red("v", red[:, :], psr[:, :], ALU.max)
                    P.ts("v", probs[:, :], psr[:, :], red[:, 0:1], None,
                         ALU.subtract)
                    P.act(probs[:, :], probs[:, :], AF.Exp)
                    P.red("v", red[:, :], probs[:, :], ALU.add)
                    P.op("v", lambda e: e.reciprocal(red[:, :], red[:, :]),
                         reads=[red[:, :]], writes=[red[:, :]])
                    P.ts("v", probs[:, :], probs[:, :], red[:, 0:1], None,
                         ALU.mult)
                    top_extract(m1, i1, oh[tt][0], tt)
                    P.ts("v", tmp[:, :], probs[:, :], m1[:, tt:tt + 1], None,
                         ALU.is_equal)
                    P.ts("v", tmp[:, :], tmp[:, :], -1.0, 1.0,
                         ALU.mult, ALU.add)
                    P.tt("v", probs[:, :], probs[:, :], tmp[:, :], ALU.mult)
                    top_extract(m2, i2, oh[tt][1], tt)
                tot = sg.sb("l0ftot", [1, 8], F32)
                totp = sg.ps("l0ftotp", [1, 8], F32)
                psc_r = [sg.ps("l0fpc", [128, 8], F32) for _ in range(2)]
                at_r = [sg.sb("l0fat", [128, 8], F32) for _ in range(2)]
                cum_r = [sg.sb("l0fcum", [128, 8], F32) for _ in range(2)]
                posS_r = [sg.sb("l0fpos", [128, 1], F32) for _ in range(2)]
                eS_r = [sg.sb("l0fes", [128, 1], F32) for _ in range(2)]
                offs_r = [sg.sb("l0foff", [128, 1], F32) for _ in range(2)]
                offsi_r = [sg.sb("l0foffi", [128, 1], I32) for _ in range(2)]
                tokS_r = [sg.sb("l0ftok", [128, 1], F32) for _ in range(2)]
                tokSi_r = [sg.sb("l0ftoki", [128, 1], I32) for _ in range(2)]
                drp_r = [sg.sb("l0fdrp", [128, 1], F32) for _ in range(2)]
                pde_r = [sg.ps("l0fpd", [64, 1], F32) for _ in range(2)]
                P.memset("v", tot[:, :], 0.0)
                for ch in range(2 * T // 128):
                    tt, half = ch // 2, 64 * (ch % 2)
                    psc, at_sb, cum = psc_r[ch % 2], at_r[ch % 2], cum_r[ch % 2]
                    posS, eS, offs = posS_r[ch % 2], eS_r[ch % 2], offs_r[ch % 2]
                    offs_i, tokS = offsi_r[ch % 2], tokS_r[ch % 2]
                    tokS_i, drp = tokSi_r[ch % 2], drp_r[ch % 2]
                    pde = pde_r[ch % 2]
                    P.mm_group(psc[:, :],
                               [(repT[0][half:half + 64, :],
                                 oh[tt][0][half:half + 64, :]),
                                (repT[1][half:half + 64, :],
                                 oh[tt][1][half:half + 64, :])])
                    P.cp("v", at_sb[:, :], psc[:, :])
                    P.mm_group(psc[:, :],
                               [(LT[:, :], at_sb[:, :]),
                                (onesr_f[:, :], tot[:, :])])
                    P.cp("v", cum[:, :], psc[:, :])
                    P.mm_group(totp[:, :],
                               [(onesc_f[:, :], at_sb[:, :]),
                                (onesr_f[0:1, 0:1], tot[:, :])])
                    P.cp("v", tot[:, :], totp[:, :])
                    P.tt("v", cum[:, :], cum[:, :], at_sb[:, :], ALU.mult)
                    P.red("v", posS[:, :], cum[:, :], ALU.add)
                    P.mm_group(psc[:, 0:1],
                               [(repT[0][half:half + 64, :],
                                 i1[half:half + 64, tt:tt + 1]),
                                (repT[1][half:half + 64, :],
                                 i2[half:half + 64, tt:tt + 1])])
                    P.cp("v", eS[:, :], psc[:, 0:1])
                    P.ts("v", drp[:, :], posS[:, :], float(CAP), SENT,
                         ALU.is_gt, ALU.mult)
                    P.stt("v", offs[:, :], eS[:, :], float(CAP), posS[:, :],
                          ALU.mult, ALU.add)
                    P.ts("v", offs[:, :], offs[:, :], -1.0, None, ALU.add)
                    P.tt("v", offs[:, :], offs[:, :], drp[:, :], ALU.add)
                    P.cp("v", offs_i[:, :], offs[:, :])
                    P.ts("v", tokS[:, :], tok64[:, :], 1.0, float(ch * 64),
                         ALU.mult, ALU.add)
                    P.cp("v", tokS_i[:, :], tokS[:, :])
                    P.op("g", lambda e: e.indirect_dma_start(
                        src_tab.ap(),
                        IndirectOffsetOnAxis(offs_i[:, :], 0),
                        tokS_i[:, :], None,
                        bounds_check=bc_tab, oob_is_err=False),
                        reads=[offs_i[:, :], tokS_i[:, :]],
                        writes=[src_tab.ap()], dma=True)
                    for k in range(2):
                        P.mm_group(pde[:, :], [(deinT[k][:, :], offs[:, :])])
                        P.cp("v", gidx[k][half:half + 64, tt:tt + 1],
                             pde[:, :])
        for e in range(E):
            with P.scope() as se:
                xe = [se.sb("l0fxe", [128, CAP], BF16) for _ in range(DC)]
                with P.scope() as sgt:
                    idxt = sgt.sb("l0fidx", [128, 1], I32)
                    xet = [sgt.sb("l0fxet", [128, 1024], BF16)
                           for _ in range(2)]
                    ptt = [sgt.ps("l0fxp", [128, 128], BF16)
                           for _ in range(2)]
                    for blk in range(CAP // 128):
                        xt = xet[blk % 2]
                        P.dma("y", idxt[:, :],
                              src_tab.ap()[e * CAP + blk * 128:
                                           e * CAP + blk * 128 + 128, :])
                        P.op("g", lambda e_, xt=xt: e_.indirect_dma_start(
                            xt[:, :], None, xn_tm.ap(),
                            IndirectOffsetOnAxis(idxt[:, :], 0),
                            bounds_check=bc_tok, oob_is_err=False),
                            reads=[idxt[:, :], xn_tm.ap()],
                            writes=[xt[:, :]], dma=True)
                        for c in range(DC):
                            pb = ptt[c % 2]
                            P.transpose(pb[:, :],
                                        xt[:, c * 128:(c + 1) * 128])
                            P.cp("v", xe[c][:, blk * 128:(blk + 1) * 128],
                                 pb[:, :])
                g = [se.sb("l0fg", [128, CAP], BF16) for _ in range(FC)]
                with P.scope() as sw:
                    w13 = [[sw.sb("w13s", [128, 896], BF16)
                            for _ in range(2 * DC)] for _ in range(2)]
                    psp = [sw.ps("l0fp1", [128, 512], F32) for _ in range(6)]
                    aa = [sw.sb("l0faa", [128, 512], BF16) for _ in range(3)]

                    def load13(ob):
                        ws = w13[ob % 2]
                        for i in range(DC):
                            P.dma("y", ws[i][:, :],
                                  w1_16[e].ap()[i * 128:(i + 1) * 128,
                                                ob * 896:(ob + 1) * 896],
                                  stream=f"l0a{ob % 2}y")
                            P.dma("s", ws[DC + i][:, :],
                                  w3_16[e].ap()[i * 128:(i + 1) * 128,
                                                ob * 896:(ob + 1) * 896],
                                  stream=f"l0a{ob % 2}s")
                    load13(0)
                    for ob in range(4):
                        if ob + 1 < 4:
                            load13(ob + 1)
                        ws = w13[ob % 2]
                        for m7 in range(7):
                            m = ob * 7 + m7
                            for gi, (g0, gw) in enumerate(((0, 512),
                                                          (512, 128))):
                                r = (m7 * 2 + gi) % 3
                                pa, pb = psp[2 * r], psp[2 * r + 1]
                                a = aa[r]
                                P.mm_group(pa[:, 0:gw],
                                           [(ws[i][:, m7 * 128:(m7 + 1) * 128],
                                             xe[i][:, g0:g0 + gw])
                                            for i in range(DC)])
                                P.act(a[:, 0:gw], pa[:, 0:gw], AF.Silu)
                                P.mm_group(pb[:, 0:gw],
                                           [(ws[DC + i][:, m7 * 128:(m7 + 1) * 128],
                                             xe[i][:, g0:g0 + gw])
                                            for i in range(DC)])
                                P.tt("v", g[m][:, g0:g0 + gw], pb[:, 0:gw],
                                     a[:, 0:gw], ALU.mult)
                oe = [se.sb("l0foe", [128, CAP], BF16) for _ in range(DC)]
                with P.scope() as sw:
                    w2s = [[sw.sb("w2s", [128, 512], BF16)
                            for _ in range(FC)] for _ in range(2)]
                    psp = [sw.ps("l0fp2", [128, 512], F32) for _ in range(4)]

                    def load2(og):
                        ws = w2s[og % 2]
                        for i in range(FC):
                            q = "y" if i % 2 == 0 else "s"
                            P.dma(q, ws[i][:, :],
                                  w2_16[e].ap()[i * 128:(i + 1) * 128,
                                                og * 512:(og + 1) * 512],
                                  stream=f"l0b{og % 2}{q}")
                    load2(0)
                    for og in range(2):
                        if og + 1 < 2:
                            load2(og + 1)
                        ws = w2s[og % 2]
                        for m4 in range(4):
                            m = og * 4 + m4
                            for gi, (g0, gw) in enumerate(((0, 512),
                                                          (512, 128))):
                                pa = psp[(m4 * 2 + gi) % 4]
                                P.mm_group(pa[:, 0:gw],
                                           [(ws[i][:, m4 * 128:(m4 + 1) * 128],
                                             g[i][:, g0:g0 + gw])
                                            for i in range(FC)])
                                P.cp("v", oe[m][:, g0:g0 + gw], pa[:, 0:gw])
                with P.scope() as sot:
                    ptt = [sot.ps("l0fop", [128, 128], BF16)
                           for _ in range(2)]
                    tmo = [sot.sb("l0fot", [128, 1024], BF16)
                           for _ in range(2)]
                    for blk in range(CAP // 128):
                        tm = tmo[blk % 2]
                        for c in range(DC):
                            pb = ptt[c % 2]
                            P.transpose(pb[:, :],
                                        oe[c][:, blk * 128:(blk + 1) * 128])
                            P.cp("v", tm[:, c * 128:(c + 1) * 128], pb[:, :])
                        P.dma("y", eout.ap()[e * CAP + blk * 128:
                                             e * CAP + blk * 128 + 128, :],
                              tm[:, :])
        with P.scope() as scb:
            idxt = [scb.sb("l0fci", [128, 1], I32) for _ in range(2)]
            gt = [scb.sb("l0fgt", [128, 1024], BF16) for _ in range(2)]
            acc = scb.sb("l0facc", [128, 1024], F32)
            ptt = [scb.ps("l0fcp", [128, 128], F32) for _ in range(2)]
            for tt in range(T // 128):
                for k in range(2):
                    P.cp("v", idxt[k][:, :], gidx[k][:, tt:tt + 1])
                    P.op("g", lambda e_, k=k: e_.indirect_dma_start(
                        gt[k][:, :], None, eout.ap(),
                        IndirectOffsetOnAxis(idxt[k][:, :], 0),
                        bounds_check=bc_tab, oob_is_err=False),
                        reads=[idxt[k][:, :], eout.ap()],
                        writes=[gt[k][:, :]], dma=True)
                P.ts("v", acc[:, :], gt[0][:, :], m1[:, tt:tt + 1], None,
                     ALU.mult)
                P.op("v", lambda e_, tt=tt: e_.scalar_tensor_tensor(
                    acc[:, :], gt[1][:, :], m2[:, tt:tt + 1], acc[:, :],
                    ALU.mult, ALU.add),
                    reads=[gt[1][:, :], m2[:, tt:tt + 1], acc[:, :]],
                    writes=[acc[:, :]])
                for c in range(DC):
                    pb = ptt[c % 2]
                    P.transpose(pb[:, :], acc[:, c * 128:(c + 1) * 128])
                    P.op("v", lambda e_, pb=pb, c=c, tt=tt:
                         e_.scalar_tensor_tensor(
                             X[c][:, tt * 128:(tt + 1) * 128], pb[:, :], 0.5,
                             X[c][:, tt * 128:(tt + 1) * 128],
                             ALU.mult, ALU.add),
                         reads=[pb[:, :], X[c][:, tt * 128:(tt + 1) * 128]],
                         writes=[X[c][:, tt * 128:(tt + 1) * 128]])


TRANSPOSED = {
    "L1_ffn_W_r", "L1_ffn_W_k", "L1_ffn_W_v",
    "L1_attn_wq", "L1_attn_wkv", "L1_attn_wo",
    "L2_attn_W_DQ", "L2_attn_W_UQ", "L2_attn_W_QR", "L2_attn_W_DKV",
    "L2_attn_W_UK", "L2_attn_W_UV", "L2_attn_W_KR", "L2_attn_W_O",
    "L3_attn_w_attn", "L3_attn_w_proj",
    "L3_ffn_w1_w", "L3_ffn_w2_w", "L3_ffn_h1_w", "L3_ffn_h2_w",
    "L3_ffn_w3_w",
    "L0_ffn_gate_w",
}


def stage_input(wn, arr):
    a = np.asarray(arr, dtype=np.float32)
    if wn in TRANSPOSED:
        a = a.T
    return np.ascontiguousarray(a)


def staged_shape(wn):
    shp = WEIGHT_SHAPES[wn]
    if wn in TRANSPOSED:
        return list(reversed(shp))
    return list(shp)


WEIGHT_NAMES = {
    "l1_ffn": ["L1_ffn_W_r", "L1_ffn_W_k", "L1_ffn_W_v",
               "L1_ffn_mu_r", "L1_ffn_mu_k"],
    "l3_ffn": ["L3_ffn_w1_w", "L3_ffn_w1_s", "L3_ffn_w2_w", "L3_ffn_w2_s",
               "L3_ffn_h1_w", "L3_ffn_h1_s", "L3_ffn_h2_w", "L3_ffn_h2_s",
               "L3_ffn_w3_w", "L3_ffn_w3_s"],
    "l0_ffn": ["L0_ffn_gate_w", "L0_ffn_w1", "L0_ffn_w2", "L0_ffn_w3"],
    "l2_attn": ["L2_attn_W_DQ", "L2_attn_W_UQ", "L2_attn_W_QR",
                "L2_attn_W_DKV", "L2_attn_W_UK", "L2_attn_W_UV",
                "L2_attn_W_KR", "L2_attn_W_O"],
    "l3_attn": ["L3_attn_w_attn", "L3_attn_w_proj"],
    "l1_attn": ["L1_attn_wq", "L1_attn_wkv", "L1_attn_wo",
                "L1_attn_gamma", "L1_attn_beta"],
    "l0_attn": ["L0_attn_Wq", "L0_attn_bq", "L0_attn_Wk", "L0_attn_bk",
                "L0_attn_Wv", "L0_attn_bv"],
    "l2_ffn": ["L2_ffn_W_enc", "L2_ffn_b_enc", "L2_ffn_W_s2c", "L2_ffn_b_s2c",
               "L2_ffn_Wr", "L2_ffn_br", "L2_ffn_W1", "L2_ffn_b1",
               "L2_ffn_W2", "L2_ffn_b2", "L2_ffn_W_c2s", "L2_ffn_b_c2s",
               "L2_ffn_W_dec", "L2_ffn_b_dec", "L2_ffn_ln_g", "L2_ffn_ln_b"],
}

WEIGHT_SHAPES = {
    "L1_ffn_W_r": [1024, 1024], "L1_ffn_W_k": [1024, 1024],
    "L1_ffn_W_v": [1024, 1024],
    "L1_ffn_mu_r": [1024], "L1_ffn_mu_k": [1024],
    "L3_ffn_w1_w": [4096, 1024], "L3_ffn_w1_s": [4096],
    "L3_ffn_w2_w": [4096, 1024], "L3_ffn_w2_s": [4096],
    "L3_ffn_h1_w": [4096, 4096], "L3_ffn_h1_s": [4096],
    "L3_ffn_h2_w": [4096, 4096], "L3_ffn_h2_s": [4096],
    "L3_ffn_w3_w": [1024, 4096], "L3_ffn_w3_s": [1024],
    "L0_ffn_gate_w": [8, 1024],
    "L0_ffn_w1": [8, 1024, 3584], "L0_ffn_w2": [8, 3584, 1024],
    "L0_ffn_w3": [8, 1024, 3584],
    "L2_attn_W_DQ": [512, 1024], "L2_attn_W_UQ": [1024, 512],
    "L2_attn_W_QR": [512, 512], "L2_attn_W_DKV": [512, 1024],
    "L2_attn_W_UK": [1024, 512], "L2_attn_W_UV": [1024, 512],
    "L2_attn_W_KR": [32, 1024], "L2_attn_W_O": [1024, 1024],
    "L3_attn_w_attn": [3072, 1024], "L3_attn_w_proj": [1024, 1024],
    "L1_attn_wq": [1024, 1024], "L1_attn_wkv": [2048, 1024],
    "L1_attn_wo": [1024, 1024],
    "L1_attn_gamma": [1024], "L1_attn_beta": [1024],
    "L0_attn_Wq": [1024, 1024], "L0_attn_bq": [1024],
    "L0_attn_Wk": [1024, 1024], "L0_attn_bk": [1024],
    "L0_attn_Wv": [1024, 1024], "L0_attn_bv": [1024],
    "L2_ffn_W_enc": [1024, 2048], "L2_ffn_b_enc": [2048],
    "L2_ffn_W_s2c": [2048, 64], "L2_ffn_b_s2c": [64],
    "L2_ffn_Wr": [64, 8], "L2_ffn_br": [8],
    "L2_ffn_W1": [8, 64, 1024], "L2_ffn_b1": [8, 1024],
    "L2_ffn_W2": [8, 1024, 64], "L2_ffn_b2": [8, 64],
    "L2_ffn_W_c2s": [64, 2048], "L2_ffn_b_c2s": [2048],
    "L2_ffn_W_dec": [2048, 1024], "L2_ffn_b_dec": [1024],
    "L2_ffn_ln_g": [1024], "L2_ffn_ln_b": [1024],
}

EMITTERS = {
    "l1_ffn": emit_l1_ffn,
    "l3_ffn": emit_l3_ffn,
    "l2_ffn": emit_l2_ffn,
    "l0_attn": emit_l0_attn,
    "l1_attn": emit_l1_attn,
    "l3_attn": emit_l3_attn,
    "l2_attn": emit_l2_attn,
    "l0_ffn": emit_l0_ffn,
}


CAST_PREP = {
    "l0_ffn": ([(f"l0w1e{e}", ("L0_ffn_w1", e)) for e in range(8)]
               + [(f"l0w3e{e}", ("L0_ffn_w3", e)) for e in range(8)]
               + [(f"l0w2e{e}", ("L0_ffn_w2", e)) for e in range(8)]
               + [("l0gw", ("L0_ffn_gate_w", None))]),
    "l1_ffn": [("l1wr", ("L1_ffn_W_r", None)), ("l1wk", ("L1_ffn_W_k", None)),
               ("l1wv", ("L1_ffn_W_v", None))],
    "l1_attn": [("l1wq", ("L1_attn_wq", None)),
                ("l1wkv", ("L1_attn_wkv", None)),
                ("l1wo", ("L1_attn_wo", None))],
    "l2_attn": [("l2aW_" + n, ("L2_attn_W_" + n, None))
                for n in ["DQ", "UQ", "QR", "DKV", "UK", "UV", "KR", "O"]],
    "l3_attn": [("l3wa", ("L3_attn_w_attn", None)),
                ("l3wp", ("L3_attn_w_proj", None))],
    "l3_ffn": [("l3w1", ("L3_ffn_w1_w", None)), ("l3w2", ("L3_ffn_w2_w", None)),
               ("l3h1", ("L3_ffn_h1_w", None)), ("l3h2", ("L3_ffn_h2_w", None)),
               ("l3w3", ("L3_ffn_w3_w", None))],
    "l0_attn": [], "l2_ffn": [],
}


def emit_weight_prep(P, W, steps):
    seen = set()
    for st in steps:
        if st in seen:
            continue
        seen.add(st)
        for name, (key, e) in CAST_PREP[st]:
            ap = W[key] if e is None else W[key][e, :, :]
            cast_weight_bf16(P, None, ap, name)


def build_program(steps):
    P = Prog()
    nc = P.nc
    xin = nc.dram_tensor("xT_in", [D, T], F32, kind="ExternalInput").ap()
    xout = nc.dram_tensor("xT_out", [D, T], F32, kind="ExternalOutput").ap()
    W = {}
    for st in steps:
        for wn in WEIGHT_NAMES[st]:
            if wn not in W:
                W[wn] = nc.dram_tensor(wn, staged_shape(wn), F32,
                                       kind="ExternalInput").ap()
    X = [P.sb("X", [128, T], F32) for _ in range(DC)]
    for c in range(DC):
        P.dma("y", X[c][:, :], xin[c * 128:(c + 1) * 128, :])
    emit_weight_prep(P, W, steps)
    for st in steps:
        EMITTERS[st](P, X, W)
    for c in range(DC):
        P.dma("y", xout[c * 128:(c + 1) * 128, :], X[c][:, :])
    P.eng["y"].wait_ge(P.sem["yd"], P.cnt["yd"])
    return P


ALL_STEPS = ["l0_ffn", "l0_attn", "l0_ffn",
             "l1_ffn", "l1_attn", "l1_ffn",
             "l2_ffn", "l2_attn", "l2_ffn",
             "l3_ffn", "l3_attn", "l3_ffn"]


def kernel(**inputs):
    x = np.ascontiguousarray(inputs["L0_attn_minibatch"])
    xT = [np.ascontiguousarray(x[b].T) for b in range(NCORES)]
    for group in [ALL_STEPS]:
        P = build_program(group)
        in_maps = []
        for b in range(NCORES):
            m = {"xT_in": xT[b]}
            for st in group:
                for wn in WEIGHT_NAMES[st]:
                    m[wn] = stage_input(wn, inputs[wn])
            in_maps.append(m)
        res = run_bass_kernel_spmd(P.nc, in_maps, core_ids=list(range(NCORES)))
        xT = [res.results[b]["xT_out"] for b in range(NCORES)]
    out = np.stack([xT[b].T for b in range(NCORES)], axis=0)
    return np.ascontiguousarray(out.astype(np.float32))
